# version 01
# baseline (speedup 1.0000x reference)
import contextlib
import os
import numpy as np

P1LVL = int(os.environ.get("KB_P1_LVL", "9"))
P1NST = int(os.environ.get("KB_P1_NST", str(8)))
P1DIRS = os.environ.get("KB_P1_DIRS", "fr")

import concourse.bass as bass
import concourse.mybir as mybir
from concourse.bass_utils import run_bass_kernel_spmd

F32 = mybir.dt.float32
BF16 = mybir.dt.bfloat16
AF = mybir.ActivationFunctionType
ALU = mybir.AluOpType

B, L, D = 4, 4096, 1024
TOK = 2048
Q = 128
NCH = 16
ST1 = 256
NST1 = TOK // ST1
WIN = 512
D_INNER, NHEADS, HEADDIM, D_STATE = 2048, 32, 64, 64
CONV_DIM = 2176
XW = 2816
OWN0 = 640
KV0 = 128
HID, HIDP = 2730, 2816
EPS = 1e-5
RW = 2560
SEG = 2049
PAIRS = [[0, 1], [2, 3], [4, 5], [6, 7]]


def bcast_rows_ap(row_ap, nparts):
    return row_ap.broadcast_to([nparts, row_ap.shape[1]])


def rep_outer(ap, reps):
    a = ap.rearrange("p (a b) -> p a b", a=1)
    return a.broadcast_to([a.shape[0], reps, a.shape[2]])


def rep_inner(ap, reps):
    a = ap.rearrange("p (a b) -> p a b", b=1)
    return a.broadcast_to([a.shape[0], a.shape[1], reps])



class _Track:

    def __init__(self, eng, kern=None, qkey=None):
        self._eng = eng
        self.last = None
        self._kern = kern
        self._qkey = qkey
        self._fenced = False

    def __getattr__(self, name):
        fn = getattr(self._eng, name)
        if callable(fn):
            def wrap(*a, **kw):
                if (name == "dma_start" and self._kern is not None
                        and not self._fenced and not _Track.no_fence):
                    self._fenced = True
                    self._eng.wait_ge(self._kern.sem[self._qkey],
                                      self._kern.cnt[self._qkey])
                r = fn(*a, **kw)
                if hasattr(r, "then_inc"):
                    self.last = r
                if name == "dma_start" and self._kern is not None:
                    self._kern.dma_audit[self._qkey] = \
                        self._kern.dma_audit.get(self._qkey, 0) + 16
                return r
            return wrap
        return fn

    no_fence = False


class Kern:
    def __init__(self, nc, es, dbg=()):
        self.nc = nc
        self.es = es
        self.dbg = set(dbg)
        self.dbg_out = {}
        self.block = es.enter_context(nc.Block())
        self.sem = {}
        self.cnt = {}
        self.dma_audit = {}
        for k in ("T", "V", "S", "G", "LD", "SP"):
            self.sem[k] = es.enter_context(nc.semaphore("sem" + k))
            self.cnt[k] = 0

    def inc(self, instr, k):
        n = 16 if k in ("LD", "SP") else 1
        self.cnt[k] += n
        instr.then_inc(self.sem[k], n)
        return (k, self.cnt[k])

    def now(self, k):
        return (k, self.cnt[k])

    def wait(self, eng, *evs):
        for k, v in evs:
            if k in ("LD", "SP"):
                v = self.cnt[k]
            if v > 0:
                eng.wait_ge(self.sem[k], v)

    def T(self, fn): self.block.tensor(lambda e: fn(_Track(e, self, "T")))
    def V(self, fn): self.block.vector(lambda e: fn(_Track(e, self, "V")))
    def S(self, fn): self.block.scalar(lambda e: fn(_Track(e, self, "S")))
    def G(self, fn): self.block.gpsimd(lambda e: fn(_Track(e, self, "SP")))
    def Y(self, fn): self.block.sync(lambda e: fn(_Track(e, self, "LD")))

    def incl(self, eng, k):
        assert eng.last is not None
        return self.inc(eng.last, k)

    def barrier(self):
        for qk in ("LD", "SP"):
            a, c = self.dma_audit.get(qk, 0), self.cnt[qk]
            assert a == c, f"DMA audit mismatch {qk}: emitted incs {a} != tracked cnt {c}"
        evs = [self.now(k) for k in ("T", "V", "S", "G", "LD", "SP")]
        for sec in (self.T, self.V, self.S, self.G, self.Y):
            sec(lambda e, evs=evs: self.wait(e, *evs))

    _uid = 0

    alloc_log = []

    def sbt(self, scope, name, shape, dt):
        Kern._uid += 1
        b0 = self.nc.sbuf_base
        t = scope.enter_context(
            self.nc.sbuf_tensor(f"{name}_u{Kern._uid}", shape, dt))
        Kern.alloc_log.append((name, b0, self.nc.sbuf_base, tuple(shape), str(dt)))
        return t

    def pst(self, scope, name, shape, dt=F32):
        Kern._uid += 1
        return scope.enter_context(
            self.nc.psum_tensor(f"{name}_u{Kern._uid}", shape, dt))

    def mkdbg(self, name, shape):
        if name in self.dbg and name not in self.dbg_out:
            self.dbg_out[name] = self.nc.dram_tensor(
                "dbg_" + name, shape, F32, kind="ExternalOutput").ap()
        return self.dbg_out.get(name)

    def dump(self, name, buf, nrt, width, cols0=0, src_width=None):
        d = self.mkdbg(name, [nrt * 128, width])
        if d is None:
            return
        sw = src_width or width
        def g(gg):
            self.wait(gg, self.now("V"), self.now("S"), self.now("T"),
                      self.now("G") if False else (0, 0))
            for rt in range(nrt):
                self.inc(gg.dma_start(
                    out=d[rt * 128:(rt + 1) * 128, :],
                    in_=buf[:, rt * sw + cols0: rt * sw + cols0 + width]), "SP")
        self.G(g)

    def declare_io(self):
        DT = self.nc.dram_tensor
        ap = lambda *a, **kw: DT(*a, **kw).ap()
        self.x_fm = ap("x_fm", [D, XW], F32, kind="ExternalInput")
        self.wx = {d: ap(f"wx_{d}", [D, 2208], F32, kind="ExternalInput") for d in "fr"}
        self.wz = {d: ap(f"wz_{d}", [D, D_INNER], F32, kind="ExternalInput") for d in "fr"}
        self.wout = {d: ap(f"wout_{d}", [D_INNER, D], F32, kind="ExternalInput") for d in "fr"}
        self.convw = {d: ap(f"convw_{d}", [128, 68], F32, kind="ExternalInput") for d in "fr"}
        self.convb = {d: ap(f"convb_{d}", [128, 17], F32, kind="ExternalInput") for d in "fr"}
        self.mv = {d: ap(f"mv_{d}", [32, 3], F32, kind="ExternalInput") for d in "fr"}
        self.dexp = {d: ap(f"dexp_{d}", [128, 16], F32, kind="ExternalInput") for d in "fr"}
        self.rmsw = {d: ap(f"rmsw_{d}", [128, 16], F32, kind="ExternalInput") for d in "fr"}
        self.lnwb = ap("lnwb", [128, 16], F32, kind="ExternalInput")
        self.wqkv = ap("wqkv", [2, D, 1536], F32, kind="ExternalInput")
        self.wo2 = ap("wo2", [2, 512, D], F32, kind="ExternalInput")
        self.w1p = ap("w1p", [2, D, 2 * HIDP], F32, kind="ExternalInput")
        self.b1p = ap("b1p", [2, 128, 44], F32, kind="ExternalInput")
        self.w2p = ap("w2p", [2, HIDP, D], F32, kind="ExternalInput")
        self.b2v = ap("b2v", [2, 128, 8], F32, kind="ExternalInput")
        self.lng = ap("lng", [2, 128, 8], F32, kind="ExternalInput")
        self.lnb2 = ap("lnb2", [2, 128, 8], F32, kind="ExternalInput")
        self.qsc = ap("qsc", [2, 128, 1], F32, kind="ExternalInput")
        self.ksc = ap("ksc", [2, 128, 1], F32, kind="ExternalInput")
        self.rot = ap("rot", [4, 128, RW], F32, kind="ExternalInput")
        self.flags = ap("flags", [128, 8], F32, kind="ExternalInput")
        self.y_fm = ap("y_fm", [D, TOK], F32, kind="ExternalOutput")

        self.sp_xnbf = ap("sp_xnbf", [D, XW], BF16)
        self.sp_xn32 = ap("sp_xn32", [D, TOK], F32)
        self.sp_xD = {d: ap(f"sp_xD_{d}", [D_INNER, TOK], BF16) for d in "fr"}
        self.sp_xT = {d: ap(f"sp_xT_{d}", [TOK, D_INNER], BF16) for d in "fr"}
        self.sp_B = {d: ap(f"sp_B_{d}", [64, TOK], BF16) for d in "fr"}
        self.sp_C = {d: ap(f"sp_C_{d}", [64, TOK], BF16) for d in "fr"}
        self.sp_sadd = {d: ap(f"sp_sadd_{d}", [64, NCH * D_INNER], F32) for d in "fr"}
        self.sp_cl = {d: ap(f"sp_cl_{d}", [32, TOK], F32) for d in "fr"}
        self.sp_gam = {d: ap(f"sp_gam_{d}", [32, TOK], F32) for d in "fr"}
        self.sp_ctl = {d: ap(f"sp_ctl_{d}", [TOK, 64], F32) for d in "fr"}
        self.sp_dr = {d: ap(f"sp_dr_{d}", [NCH, D_INNER], F32) for d in "fr"}
        self.sp_zs = {d: ap(f"sp_zs_{d}", [D_INNER, TOK], F32) for d in "fr"}
        self.acc_m = ap("acc_m", [D, TOK], F32)
        self.sp_s1 = ap("sp_s1", [D, SEG], F32)
        self.sp_s2 = ap("sp_s2", [D, SEG], F32)
        self.sp_rows = ap("sp_rows", [16, 4096], F32)
        self.cc1_in = ap("cc1_in", [128, 2056], F32)
        self.cc1_out = ap("cc1_out", [256, 2056], F32)
        self.cc2a_in = ap("cc2a_in", [128, 8], F32)
        self.cc2a_out = ap("cc2a_out", [256, 8], F32)
        self.cc2b_in = ap("cc2b_in", [128, 4104], F32)
        self.cc2b_out = ap("cc2b_out", [256, 4104], F32)

    def persistent(self):
        es, p = self.es, self
        self.consts = self.sbt(es, "consts", [128, 10], F32)
        self.ones_bf = self.sbt(es, "ones_bf", [128, 1], BF16)
        self.ident_bf = self.sbt(es, "ident_bf", [128, 128], BF16)
        self.ident_f = self.sbt(es, "ident_f", [128, 128], F32)
        self.blk2 = self.sbt(es, "blk2", [128, 2], BF16)
        self.flags_sb = self.sbt(es, "flags_sb", [128, 8], F32)
        self.lnwb_sb = self.sbt(es, "lnwb_sb", [128, 16], F32)
        self.s_state = self.sbt(es, "s_state", [64, 2 * D_INNER], F32)
        tmpf = self.sbt(es, "tmpf", [128, 128], F32)
        self.ph_h2 = contextlib.ExitStack()
        self.h2 = self.sbt(self.ph_h2, "h2", [128, 8 * XW], BF16)

        def v0(vv):
            for i, v in enumerate([EPS, -0.5, 1.0, -8.0, 1.0 / 2048.0, 0.0,
                                   1e12, -1.0, 1e-30, 0.0]):
                vv.memset(self.consts[:, i:i + 1], v)
            vv.memset(self.ones_bf[:], 1.0)
            vv.memset(self.blk2[:], 0.0)
            vv.memset(self.blk2[0:64, 0:1], 1.0)
            vv.memset(self.blk2[64:128, 1:2], 1.0)
            p.inc(vv.memset(tmpf[:], 1.0), "V")
        p.V(v0)

        def g(gg):
            p.wait(gg, p.now("V"))
            p.inc(gg.affine_select(
                out=tmpf[:], in_=tmpf[:], pattern=[[-1, 128]],
                compare_op=ALU.is_equal, fill=0.0, base=0,
                channel_multiplier=1), "G")
            gg.dma_start(out=self.flags_sb[:], in_=self.flags[:, :]).then_inc(
                p.sem["SP"], 16)
            gg.dma_start(out=self.lnwb_sb[:], in_=self.lnwb[:, :]).then_inc(
                p.sem["SP"], 16)
        p.G(g)
        p.cnt["SP"] += 32

        def v(vv):
            p.wait(vv, p.now("G"), p.now("SP"))
            vv.tensor_copy(self.ident_bf[:], tmpf[:])
            p.inc(vv.tensor_copy(self.ident_f[:], tmpf[:]), "V")
        p.V(v)
        self.mf = self.flags_sb[:, 0:1]
        self.mb = self.flags_sb[:, 1:2]
        self.C_EPS, self.C_NH, self.C_ONE, self.C_M8 = 0, 1, 2, 3
        self.C_IDIM, self.C_ZERO, self.C_BIG, self.C_M1 = 4, 5, 6, 7
        self.C_TINY = 8

    def cc(self, col):
        return self.consts[:, col:col + 1]

    def p0(self):
        p, nc = self, self.nc
        ph = contextlib.ExitStack()
        with ph:
            xrot = [self.sbt(ph, f"xrot{i}", [128, XW], F32) for i in range(2)]
            xbf = self.sbt(ph, "xbf", [128, 8 * XW], BF16)
            rows0 = self.sbt(ph, "rows0", [1, XW], F32)
            rows1 = self.sbt(ph, "rows1", [1, XW], F32)
            mub = self.sbt(ph, "mub", [128, XW], F32)
            rb = self.sbt(ph, "rb", [128, XW], F32)
            x32r = [self.sbt(ph, f"x32r{i}", [128, TOK], F32) for i in range(2)]
            sqs = [self.sbt(ph, f"sqs{i}", [128, XW], BF16) for i in range(2)]
            ps_st = self.pst(ph, "ps_st", [1, XW])
            MCH = [(i * 512, min(512, XW - i * 512)) for i in range(6)]

            done = [None] * 8
            for rt in range(8):
                ldev = [None]
                sev = [None]
                def y(yq, rt=rt, ldev=ldev):
                    if rt >= 2:
                        p.wait(yq, *done[rt - 2])
                    ldev[0] = p.inc(yq.dma_start(
                        out=xrot[rt % 2][:],
                        in_=self.x_fm[rt * 128:(rt + 1) * 128, :]), "LD")
                p.Y(y)
                def s_(ss, rt=rt, ldev=ldev, sev=sev):
                    p.wait(ss, ldev[0])
                    if rt >= 2:
                        p.wait(ss, p.now("T"))
                    sev[0] = p.inc(ss.activation(out=sqs[rt % 2][:],
                                                 in_=xrot[rt % 2][:],
                                                 func=AF.Square), "S")
                p.S(s_)
                def t(tt, rt=rt, sev=sev):
                    p.wait(tt, sev[0])
                    for c0, w in MCH:
                        tt.matmul(ps_st[0:1, c0:c0 + w], self.ones_bf[:],
                                  sqs[rt % 2][:, c0:c0 + w],
                                  start=(rt == 0), stop=(rt == 7))
                    p.incl(tt, 'T')
                p.T(t)
                def v(vv, rt=rt, ldev=ldev):
                    p.wait(vv, ldev[0])
                    p.inc(vv.tensor_copy(xbf[:, rt * XW:(rt + 1) * XW],
                                         xrot[rt % 2][:]), "V")
                p.V(v)
                done[rt] = [p.now("S"), p.now("V")]

            rowsl = [None, None]
            def evac_row(dst_row):
                rowsl[dst_row] = rows0 if dst_row == 0 else rows1
                def s_(ss):
                    p.wait(ss, p.now("T"))
                    p.inc(ss.copy((rows0 if dst_row == 0 else rows1)[0:1, :],
                                  ps_st[0:1, :]), "S")
                p.S(s_)

            def sum_pass(dst_row):
                def t(tt):
                    p.wait(tt, p.now("V"), p.now("S"))
                    for c0, w in MCH:
                        for rt in range(8):
                            tt.matmul(ps_st[0:1, c0:c0 + w], self.ones_bf[:],
                                      xbf[:, rt * XW + c0:rt * XW + c0 + w],
                                      start=(rt == 0), stop=(rt == 7))
                    p.incl(tt, 'T')
                p.T(t)
                evac_row(dst_row)

            def sq_pass(dst_row):
                for rt in range(8):
                    sev = [None]
                    def s_(ss, rt=rt, sev=sev):
                        if rt >= 2:
                            p.wait(ss, p.now("T"))
                        sev[0] = p.inc(ss.activation(
                            out=sqs[rt % 2][:], in_=xbf[:, rt * XW:(rt + 1) * XW],
                            func=AF.Square), "S")
                    p.S(s_)
                    def t(tt, rt=rt, sev=sev):
                        p.wait(tt, sev[0])
                        for c0, w in MCH:
                            tt.matmul(ps_st[0:1, c0:c0 + w], self.ones_bf[:],
                                      sqs[rt % 2][:, c0:c0 + w],
                                      start=(rt == 0), stop=(rt == 7))
                        p.incl(tt, 'T')
                    p.T(t)
                evac_row(dst_row)

            def rows_math():
                def v(vv):
                    p.wait(vv, p.now("S"))
                    vv.tensor_scalar_mul(rows0[0:1, :], rows0[0:1, :], 1.0 / 1024.0)
                    vv.tensor_scalar_mul(rows1[0:1, :], rows1[0:1, :], 1.0 / 1024.0)
                    vv.tensor_tensor(out=mub[0:1, :], in0=rows0[0:1, :],
                                     in1=rows0[0:1, :], op=ALU.mult)
                    p.inc(vv.tensor_tensor(out=rows1[0:1, :], in0=rows1[0:1, :],
                                           in1=mub[0:1, :], op=ALU.subtract), "V")
                p.V(v)
                def s_(ss):
                    p.wait(ss, p.now("V"))
                    ss.activation(out=rows1[0:1, :], in_=rows1[0:1, :], func=AF.Ln,
                                  bias=self.cc(self.C_EPS)[0:1, :])
                    p.inc(ss.activation(out=rows1[0:1, :], in_=rows1[0:1, :],
                                        func=AF.Exp,
                                        scale=self.cc(self.C_NH)[0:1, :]), "S")
                p.S(s_)

            def bcast_rows(off):
                def g(gg):
                    p.wait(gg, p.now("S"), p.now("V"))
                    gg.dma_start(out=self.sp_rows[off:off + 1, 0:XW],
                                 in_=rows0[0:1, :]).then_inc(p.sem["SP"], 16)
                    gg.dma_start(out=self.sp_rows[off + 1:off + 2, 0:XW],
                                 in_=rows1[0:1, :]).then_inc(p.sem["SP"], 16)
                    p.cnt["SP"] += 32
                    gg.wait_ge(p.sem["SP"], p.cnt["SP"])
                    gg.dma_start(out=mub[:], in_=self.sp_rows[
                        off:off + 1, 0:XW].broadcast_to([128, XW])
                    ).then_inc(p.sem["SP"], 16)
                    gg.dma_start(out=rb[:], in_=self.sp_rows[
                        off + 1:off + 2, 0:XW].broadcast_to([128, XW])
                    ).then_inc(p.sem["SP"], 16)
                    p.cnt["SP"] += 32
                p.G(g)

            evac_row(1)
            sum_pass(0)
            rows_math()
            bcast_rows(0)
            ev_bc = p.now("SP")

            for rt in range(8):
                ldev = [None]
                vev = [None]
                def y(yq, rt=rt, ldev=ldev):
                    if rt < 2:
                        p.wait(yq, p.now("V"), p.now("S"))
                    else:
                        p.wait(yq, done[rt - 2][0])
                    ldev[0] = p.inc(yq.dma_start(
                        out=xrot[rt % 2][:],
                        in_=self.x_fm[rt * 128:(rt + 1) * 128, :]), "LD")
                p.Y(y)
                def v(vv, rt=rt, ldev=ldev, vev=vev):
                    p.wait(vv, ldev[0], ev_bc)
                    if rt >= 2 and len(done[rt - 2]) > 1:
                        p.wait(vv, done[rt - 2][1])
                    sl = slice(rt * XW, (rt + 1) * XW)
                    vv.tensor_tensor(out=xrot[rt % 2][:], in0=xrot[rt % 2][:],
                                     in1=mub[:], op=ALU.subtract)
                    vv.tensor_tensor(out=xbf[:, sl], in0=xrot[rt % 2][:],
                                     in1=rb[:], op=ALU.mult)
                    vev[0] = p.inc(vv.tensor_tensor(
                        out=x32r[rt % 2][:],
                        in0=xrot[rt % 2][:, OWN0:OWN0 + TOK],
                        in1=rb[:, OWN0:OWN0 + TOK], op=ALU.mult), "V")
                    done[rt] = [p.now("V")]
                p.V(v)
                def g(gg, rt=rt, vev=vev):
                    p.wait(gg, vev[0])
                    p.inc(gg.dma_start(
                        out=self.sp_xn32[rt * 128:(rt + 1) * 128, :],
                        in_=x32r[rt % 2][:]), "SP")
                p.G(g)
                done[rt].append(p.now("SP"))

            sum_pass(0)
            sq_pass(1)
            rows_math()
            bcast_rows(2)
            ev_bc2 = p.now("SP")
            def v2(vv):
                p.wait(vv, ev_bc2)
                for rt in range(8):
                    sl = slice(rt * XW, (rt + 1) * XW)
                    vv.tensor_tensor(out=xrot[rt % 2][:], in0=xbf[:, sl],
                                     in1=mub[:], op=ALU.subtract)
                    vv.tensor_tensor(out=xrot[rt % 2][:], in0=xrot[rt % 2][:],
                                     in1=rb[:], op=ALU.mult)
                    p.inc(vv.tensor_scalar(
                        out=self.h2[:, sl], in0=xrot[rt % 2][:],
                        scalar1=self.lnwb_sb[:, rt:rt + 1],
                        scalar2=self.lnwb_sb[:, 8 + rt:9 + rt],
                        op0=ALU.mult, op1=ALU.add), "V")
            p.V(v2)

            def g2(gg):
                p.wait(gg, p.now("V"))
                for rt in range(8):
                    p.inc(gg.dma_start(
                        out=self.sp_xnbf[rt * 128:(rt + 1) * 128, :],
                        in_=xbf[:, rt * XW:(rt + 1) * XW]), "SP")
            p.G(g2)

            self.dump("h2", self.h2, 8, XW, src_width=XW)
            self.dump("xn", xbf, 8, XW, src_width=XW)
            p.barrier()

    def p1(self, d, rev):
        p, nc = self, self.nc
        sdir = slice(0, D_INNER) if d == "f" else slice(D_INNER, 2 * D_INNER)
        ph = contextlib.ExitStack()
        with ph:
            wxb = self.sbt(ph, "wxb", [128, 8 * 2208], BF16)
            wstg0 = self.sbt(ph, "wstg0", [128, 2208], F32)
            wstg = [wstg0, wstg0]
            convw_sb = self.sbt(ph, "convw_sb", [128, 68], F32)
            convb_sb = self.sbt(ph, "convb_sb", [128, 17], F32)
            mv_sb = self.sbt(ph, "mv_sb", [32, 4], F32)
            dexp_sb = self.sbt(ph, "dexp_sb", [128, 16], F32)
            WID = ST1 + 6
            XPRE = self.sbt(ph, "XPRE", [128, 17 * WID], F32)
            DTR = self.sbt(ph, "DTR", [32, WID], F32)
            XBC = self.sbt(ph, "XBC", [128, 17 * ST1], BF16)
            xDst = self.sbt(ph, "xDst", [128, 16 * ST1], BF16)
            ctmpa = self.sbt(ph, "ctmpa", [128, ST1], F32)
            ctmpb = self.sbt(ph, "ctmpb", [128, ST1], F32)
            caccf = self.sbt(ph, "caccf", [128, 2 * ST1], F32)
            dtv = self.sbt(ph, "dtv", [32, ST1], F32)
            lam = self.sbt(ph, "lam", [32, ST1], F32)
            lndt = self.sbt(ph, "lndt", [32, ST1], F32)
            zero32 = self.sbt(ph, "zero32", [32, Q], F32)
            clg = self.sbt(ph, "clg", [32, 2 * Q], F32)
            clx = self.sbt(ph, "clx", [32, 2 * Q], F32)
            gmr = self.sbt(ph, "gmr", [32, 2 * Q], F32)
            wdec = self.sbt(ph, "wdec", [32, Q], F32)
            dtw = self.sbt(ph, "dtw", [32, Q], F32)
            pk4 = self.sbt(ph, "pk4", [128, Q], F32)
            tctl = self.sbt(ph, "tctl", [128, 128], F32)
            trow12 = self.sbt(ph, "trow12", [1, 32], F32)
            drow = self.sbt(ph, "drow", [1, D_INNER], F32)
            xt_sb = self.sbt(ph, "xt_sb", [128, 2 * D_INNER], BF16)
            xwt_sb = self.sbt(ph, "xwt_sb", [128, D_INNER], BF16)
            bt_sb = self.sbt(ph, "bt_sb", [128, 64], BF16)
            sadd_sb = self.sbt(ph, "sadd_sb", [64, D_INNER], F32)
            dtile = self.sbt(ph, "dtile", [64, D_INNER], F32)
            ps_ip = self.pst(ph, "ps_ip", [128, 1024])
            ps_tpb = self.pst(ph, "ps_tpb", [128, 2048], BF16)
            ps_tpb2 = self.pst(ph, "ps_tpb2", [128, 64], BF16)
            ps_tctl = self.pst(ph, "ps_tctl", [128, 128])
            ps_sa = self.pst(ph, "ps_sa", [64, 1024])

            cast_ev = [None] * 8
            for kt in range(8):
                ldev = [None]
                def y(yq, kt=kt, ldev=ldev):
                    if kt >= 1:
                        p.wait(yq, cast_ev[kt - 1])
                    ldev[0] = p.inc(yq.dma_start(
                        out=wstg[kt % 2][:],
                        in_=self.wx[d][kt * 128:(kt + 1) * 128, :]), "LD")
                p.Y(y)
                def v(vv, kt=kt, ldev=ldev):
                    p.wait(vv, ldev[0])
                    cast_ev[kt] = p.inc(vv.tensor_copy(
                        wxb[:, kt * 2208:(kt + 1) * 2208], wstg[kt % 2][:]), "V")
                p.V(v)
            ev_wx = cast_ev[7]

            def y1(yq):
                for dst, src in ((convw_sb, self.convw[d]),
                                 (convb_sb, self.convb[d]),
                                 (dexp_sb, self.dexp[d])):
                    yq.dma_start(out=dst[:], in_=src[:, :]).then_inc(
                        p.sem["LD"], 16)
                yq.dma_start(out=mv_sb[:, 0:3], in_=self.mv[d][:, :]).then_inc(
                    p.sem["LD"], 16)
                p.cnt["LD"] += 64
            p.Y(y1)
            ev_small = p.now("LD")

            def s0(ss):
                p.wait(ss, ev_small)
                p.inc(ss.activation(out=mv_sb[:, 3:4], in_=mv_sb[:, 1:2],
                                    func=AF.Exp), "S")
            p.S(s0)
            def v0(vv):
                p.wait(vv, p.now("S"))
                vv.memset(zero32[:], 0.0)
                vv.tensor_scalar_mul(mv_sb[:, 3:4], mv_sb[:, 3:4], -1.0)
                p.inc(vv.memset(self.s_state[:, sdir], 0.0), "V")
            p.V(v0)

            sts = (list(range(P1NST)) if not rev else list(range(P1NST - 1, -1, -1))) if P1LVL >= 1 else []
            chain_ev = [("V", 0)]
            xtsp_ev = {0: ("SP", 0), 1: ("SP", 0)}
            for sti, st in enumerate(sts):
                c0 = OWN0 + st * ST1 - 3
                ipev = {}
                for nt in range(int(os.environ.get("KB_P1_NT", "18"))):
                    nw = 128 if nt < 17 else 32
                    def t1(tt, nt=nt, nw=nw):
                        if nt < 2:
                            p.wait(tt, ev_wx, p.now("V"), p.now("S"))
                        if nt >= 2:
                            p.wait(tt, ipev[nt - 2][1])
                        bk = (nt % 2) * 512
                        if not os.environ.get("KB_P1_NOMM"):
                            for kt in range(8):
                                tt.matmul(
                                    ps_ip[0:nw, bk:bk + WID],
                                    wxb[:, kt * 2208 + nt * 128:kt * 2208 + nt * 128 + nw],
                                    self.h2[:, kt * XW + c0:kt * XW + c0 + WID],
                                    start=(kt == 0), stop=(kt == 7))
                        ipev[nt] = [p.incl(tt, 'T'), None]
                    p.T(t1)
                    def s1(ss, nt=nt, nw=nw):
                        p.wait(ss, ipev[nt][0])
                        bk = (nt % 2) * 512
                        if nt < 17:
                            e = p.inc(ss.copy(XPRE[:, nt * WID:(nt + 1) * WID],
                                              ps_ip[:, bk:bk + WID]), "S")
                        else:
                            e = p.inc(ss.copy(DTR[:, :], ps_ip[0:32, bk:bk + WID]), "S")
                        ipev[nt][1] = e
                    p.S(s1)
                ev_xpre = p.now("S")

                if P1LVL < 2:
                    continue
                first_st = (st == 0)
                last_st = (st == NST1 - 1)
                def vmask(vv):
                    p.wait(vv, ev_xpre)
                    if first_st and 18 == int(os.environ.get("KB_P1_NT", "18")):
                        for ntt in range(17):
                            vv.tensor_scalar_mul(
                                XPRE[:, ntt * WID:ntt * WID + 3],
                                XPRE[:, ntt * WID:ntt * WID + 3], self.mf)
                    if last_st:
                        for ntt in range(17):
                            vv.tensor_scalar_mul(
                                XPRE[:, ntt * WID + WID - 3:(ntt + 1) * WID],
                                XPRE[:, ntt * WID + WID - 3:(ntt + 1) * WID],
                                self.mb)
                    p.incl(vv, 'V')
                p.V(vmask)
                ev_m = p.now("V")

                convev = {}
                offs = [0, 1, 2, 3] if not rev else [6, 5, 4, 3]
                for ct in range(17):
                    def vc(vv, ct=ct):
                        if ct == 0:
                            p.wait(vv, ev_m)
                        if ct >= 2:
                            p.wait(vv, convev[ct - 2])
                        base = ct * WID
                        vv.tensor_scalar(
                            out=ctmpa[:], in0=XPRE[:, base + offs[0]:base + offs[0] + ST1],
                            scalar1=convw_sb[:, ct * 4:ct * 4 + 1], scalar2=None,
                            op0=ALU.mult)
                        vv.scalar_tensor_tensor(
                            out=ctmpb[:], in0=XPRE[:, base + offs[1]:base + offs[1] + ST1],
                            scalar=convw_sb[:, ct * 4 + 1:ct * 4 + 2],
                            in1=ctmpa[:], op0=ALU.mult, op1=ALU.add)
                        vv.scalar_tensor_tensor(
                            out=ctmpa[:], in0=XPRE[:, base + offs[2]:base + offs[2] + ST1],
                            scalar=convw_sb[:, ct * 4 + 2:ct * 4 + 3],
                            in1=ctmpb[:], op0=ALU.mult, op1=ALU.add)
                        p.inc(vv.scalar_tensor_tensor(
                            out=caccf[:, (ct % 2) * ST1:(ct % 2 + 1) * ST1],
                            in0=XPRE[:, base + offs[3]:base + offs[3] + ST1],
                            scalar=convw_sb[:, ct * 4 + 3:ct * 4 + 4],
                            in1=ctmpa[:], op0=ALU.mult, op1=ALU.add), "V")
                    p.V(vc)
                    def sc(ss, ct=ct):
                        p.wait(ss, p.now("V"))
                        e = p.inc(ss.activation(
                            out=XBC[:, ct * ST1:(ct + 1) * ST1],
                            in_=caccf[:, (ct % 2) * ST1:(ct % 2 + 1) * ST1],
                            func=AF.Silu, bias=convb_sb[:, ct:ct + 1]), "S")
                        convev[ct] = e
                    p.S(sc)
                ev_xbc = p.now("S")

                def sdt(ss):
                    p.wait(ss, ipev[17][1], p.now("V"))
                    ss.activation(out=dtv[:], in_=DTR[:, 3:3 + ST1], func=AF.Exp,
                                  bias=mv_sb[:, 0:1])
                    ss.activation(out=dtv[:], in_=dtv[:], func=AF.Ln,
                                  bias=self.cc(self.C_ONE)[0:32, :])
                    p.inc(ss.activation(out=lndt[:], in_=dtv[:], func=AF.Ln,
                                        bias=self.cc(self.C_TINY)[0:32, :]), "S")
                p.S(sdt)
                def vdt(vv):
                    p.wait(vv, p.now("S"))
                    p.inc(vv.tensor_scalar(out=lam[:], in0=dtv[:],
                                           scalar1=mv_sb[:, 3:4], scalar2=None,
                                           op0=ALU.mult), "V")
                p.V(vdt)
                ev_dt = p.now("V")

                def vxd(vv):
                    p.wait(vv, ev_xbc)
                    for ct in range(16):
                        vv.tensor_scalar(
                            out=xDst[:, ct * ST1:(ct + 1) * ST1],
                            in0=XBC[:, ct * ST1:(ct + 1) * ST1],
                            scalar1=dexp_sb[:, ct:ct + 1], scalar2=None,
                            op0=ALU.mult)
                    p.incl(vv, 'V')
                p.V(vxd)
                ev_xd = p.now("V")

                def gxd(gg, st=st):
                    p.wait(gg, ev_xd, ev_xbc)
                    for ct in range(16):
                        p.inc(gg.dma_start(
                            out=self.sp_xD[d][ct * 128:(ct + 1) * 128,
                                              st * ST1:(st + 1) * ST1],
                            in_=xDst[:, ct * ST1:(ct + 1) * ST1]), "SP")
                    p.inc(gg.dma_start(
                        out=self.sp_B[d][:, st * ST1:(st + 1) * ST1],
                        in_=XBC[0:64, 16 * ST1:17 * ST1]), "SP")
                    p.inc(gg.dma_start(
                        out=self.sp_C[d][:, st * ST1:(st + 1) * ST1],
                        in_=XBC[64:128, 16 * ST1:17 * ST1]), "SP")
                p.G(gxd)

                qs_l = ([0, 1] if not rev else [1, 0]) if P1LVL >= 3 else []
                for qi in qs_l:
                    cg = st * 2 + qi
                    qsl = slice(qi * Q, (qi + 1) * Q)
                    csl = slice(qi * Q, qi * Q + Q)
                    def vch(vv, qi=qi):
                        p.wait(vv, ev_dt)
                        vv.tensor_tensor_scan(
                            out=clg[:, qsl], data0=lam[:, qsl], data1=zero32[:],
                            initial=0.0, op0=ALU.add, op1=ALU.add)
                        if rev:
                            vv.tensor_tensor(out=clx[:, qsl], in0=clg[:, qsl],
                                             in1=lam[:, qsl], op=ALU.subtract)
                        else:
                            vv.tensor_copy(clx[:, qsl], clg[:, qsl])
                        if not rev:
                            vv.tensor_scalar(
                                out=wdec[:], in0=clg[:, qsl],
                                scalar1=clg[:, qi * Q + Q - 1:qi * Q + Q],
                                scalar2=None, op0=ALU.subtract)
                            vv.tensor_copy(gmr[:, qsl], clx[:, qsl])
                        else:
                            vv.tensor_copy(wdec[:], clx[:, qsl])
                            vv.tensor_scalar(
                                out=gmr[:, qsl], in0=clx[:, qsl],
                                scalar1=clg[:, qi * Q + Q - 1:qi * Q + Q],
                                scalar2=None, op0=ALU.subtract)
                        p.inc(vv.tensor_tensor(
                            out=pk4[0:32, :], in0=clx[:, qsl], in1=lndt[:, qsl],
                            op=(ALU.subtract if not rev else ALU.add)), "V")
                    p.V(vch)
                    def sch(ss, qi=qi):
                        p.wait(ss, p.now("V"))
                        if not rev:
                            ss.activation(out=wdec[:], in_=wdec[:], func=AF.Exp,
                                          scale=self.cc(self.C_M1)[0:32, :])
                            p.inc(ss.activation(out=gmr[:, qsl], in_=gmr[:, qsl],
                                                func=AF.Exp), "S")
                        else:
                            ss.activation(out=wdec[:], in_=wdec[:], func=AF.Exp)
                            p.inc(ss.activation(
                                out=gmr[:, qsl], in_=gmr[:, qsl], func=AF.Exp,
                                scale=self.cc(self.C_M1)[0:32, :]), "S")
                    p.S(sch)
                    def vch2(vv, qi=qi):
                        p.wait(vv, p.now("S"))
                        vv.tensor_tensor(out=dtw[:], in0=dtv[:, qsl],
                                         in1=wdec[:], op=ALU.mult)
                        vv.tensor_copy(pk4[32:64, :], lndt[:, qsl])
                        vv.tensor_copy(pk4[64:96, :], dtw[:])
                        p.inc(vv.tensor_copy(pk4[96:128, :], lam[:, qsl]), "V")
                    p.V(vch2)
                    ev_pk = p.now("V")
                    def ttc(tt, qi=qi):
                        p.wait(tt, ev_pk, p.now("S"))
                        p.inc(tt.transpose(ps_tctl[:], pk4[:], self.ident_f[:]), "T")
                    p.T(ttc)
                    def stc(ss, qi=qi):
                        p.wait(ss, p.now("T"))
                        p.inc(ss.copy(tctl[:], ps_tctl[:]), "S")
                    p.S(stc)
                    ev_tctl = p.now("S")

                    def gdr(gg, qi=qi):
                        p.wait(gg, p.now("V"), p.now("S"))
                        p.inc(gg.dma_start(
                            out=trow12[:],
                            in_=clg[:, qi * Q + Q - 1:qi * Q + Q],
                            allow_slow_non_contiguous=True), "SP")
                    p.G(gdr)
                    ev_tr = p.now("SP")
                    def sdr(ss, qi=qi):
                        p.wait(ss, ev_tr)
                        p.inc(ss.activation(
                            out=drow[:].rearrange("p (a b) -> p a b", b=64),
                            in_=rep_inner(trow12[:], 64),
                            func=AF.Exp), "S")
                    p.S(sdr)
                    ev_drow = p.now("S")

                    def gsp(gg, cg=cg, qi=qi):
                        p.wait(gg, ev_drow, p.now("V"))
                        gg.dma_start(out=self.sp_cl[d][:, cg * Q:(cg + 1) * Q],
                                     in_=clx[:, qsl]).then_inc(p.sem["SP"], 16)
                        gg.dma_start(out=self.sp_gam[d][:, cg * Q:(cg + 1) * Q],
                                     in_=gmr[:, qsl]).then_inc(p.sem["SP"], 16)
                        gg.dma_start(out=self.sp_ctl[d][cg * Q:(cg + 1) * Q, :],
                                     in_=tctl[:, 0:64]).then_inc(p.sem["SP"], 16)
                        gg.dma_start(out=self.sp_dr[d][cg:cg + 1, :],
                                     in_=drow[:]).then_inc(p.sem["SP"], 16)
                        p.cnt["SP"] += 64
                    p.G(gsp)

                    tev = [None]
                    def ttp(tt, qi=qi, tev=tev):
                        p.wait(tt, ev_xbc, p.now("V"))
                        for hp in range(16):
                            tt.transpose(
                                ps_tpb[:, hp * 128:(hp + 1) * 128],
                                XBC[:, hp * ST1 + qi * Q: hp * ST1 + qi * Q + Q],
                                self.ident_bf[:])
                        tev[0] = p.inc(tt.transpose(
                            ps_tpb2[:, 0:64],
                            XBC[0:64, 16 * ST1 + qi * Q:16 * ST1 + qi * Q + Q],
                            self.ident_bf[0:64, 0:64]), "T")
                    p.T(ttp)
                    def vev_(vv, qi=qi, tev=tev):
                        p.wait(vv, tev[0], xtsp_ev[qi % 2])
                        xts = xt_sb[:, (qi % 2) * D_INNER:(qi % 2 + 1) * D_INNER]
                        vv.tensor_copy(xts, ps_tpb[:])
                        vv.tensor_copy(bt_sb[:, 0:64], ps_tpb2[:, 0:64])
                        p.inc(vv.tensor_tensor(
                            out=xwt_sb[:].rearrange("p (a b) -> p a b", b=64),
                            in0=xts.rearrange("p (a b) -> p a b", b=64),
                            in1=rep_inner(tctl[:, 64:96], 64),
                            op=ALU.mult), "V")
                    p.V(vev_)
                    ev_xw = p.now("V")

                    def gxt(gg, cg=cg, qi=qi):
                        p.wait(gg, ev_xw)
                        xtsp_ev[qi % 2] = p.inc(gg.dma_start(
                            out=self.sp_xT[d][cg * Q:(cg + 1) * Q, :],
                            in_=xt_sb[:, (qi % 2) * D_INNER:(qi % 2 + 1) * D_INNER]), "SP")
                    p.G(gxt)

                    for half in range(2):
                        def tsa(tt, qi=qi, half=half):
                            p.wait(tt, ev_xw)
                            if half == 1:
                                p.wait(tt, p.now("S"))
                            for mi in range(2):
                                m = half * 2 + mi
                                tt.matmul(ps_sa[:, mi * 512:(mi + 1) * 512],
                                          bt_sb[:, 0:64],
                                          xwt_sb[:, m * 512:(m + 1) * 512],
                                          start=True, stop=True)
                            p.incl(tt, 'T')
                        p.T(tsa)
                        def ssa(ss, qi=qi, half=half):
                            p.wait(ss, p.now("T"), chain_ev[0])
                            p.inc(ss.copy(
                                sadd_sb[:, half * 1024:(half + 1) * 1024],
                                ps_sa[:]), "S")
                        p.S(ssa)
                    ev_sadd = p.now("S")
                    def gsa(gg, cg=cg):
                        p.wait(gg, ev_sadd)
                        p.inc(gg.dma_start(
                            out=self.sp_sadd[d][:, cg * D_INNER:(cg + 1) * D_INNER],
                            in_=sadd_sb[:]), "SP")
                        gg.wait_ge(p.sem["SP"], p.cnt["SP"])
                        p.inc(gg.dma_start(
                            out=dtile[:], in_=self.sp_dr[d][cg:cg + 1, :]
                            .broadcast_to([64, D_INNER])), "SP")
                    p.G(gsa)
                    ev_dtile = p.now("SP")

                    def vchain(vv, qi=qi):
                        p.wait(vv, ev_dtile)
                        vv.tensor_tensor(out=self.s_state[:, sdir],
                                         in0=self.s_state[:, sdir],
                                         in1=dtile[:], op=ALU.mult)
                        chain_ev[0] = p.inc(vv.tensor_tensor(
                            out=self.s_state[:, sdir], in0=self.s_state[:, sdir],
                            in1=sadd_sb[:], op=ALU.add), "V")
                    p.V(vchain)

            for kt in range(8):
                ldev = [None]
                def y(yq, kt=kt, ldev=ldev):
                    p.wait(yq, cast_ev[kt - 1] if kt >= 1 else ev_wx)
                    ldev[0] = p.inc(yq.dma_start(
                        out=wstg[kt % 2][:, 0:D_INNER],
                        in_=self.wz[d][kt * 128:(kt + 1) * 128, :]), "LD")
                p.Y(y)
                def v(vv, kt=kt, ldev=ldev):
                    p.wait(vv, ldev[0], p.now("T"))
                    cast_ev[kt] = p.inc(vv.tensor_copy(
                        wxb[:, kt * D_INNER:(kt + 1) * D_INNER],
                        wstg[kt % 2][:, 0:D_INNER]), "V")
                p.V(v)
            ev_wz = cast_ev[7]

            zrot = self.sbt(ph, "zrot", [128, 1024], F32)
            zsev = {}
            zgev = {}
            for zst in range(4 if P1LVL >= 4 else 0):
                zc0 = OWN0 + zst * 512
                for nt in range(16):
                    gi = zst * 16 + nt
                    zev = [None]
                    def t(tt, zst=zst, nt=nt, zev=zev, gi=gi):
                        if gi == 0:
                            p.wait(tt, ev_wz)
                        if gi >= 2:
                            p.wait(tt, zsev[gi - 2])
                        bk = (nt % 2) * 512
                        for kt in range(8):
                            tt.matmul(ps_ip[:, bk:bk + 512],
                                      wxb[:, kt * D_INNER + nt * 128:
                                          kt * D_INNER + nt * 128 + 128],
                                      self.h2[:, kt * XW + zc0:kt * XW + zc0 + 512],
                                      start=(kt == 0), stop=(kt == 7))
                        zev[0] = p.incl(tt, 'T')
                    p.T(t)
                    def s_(ss, zst=zst, nt=nt, zev=zev, gi=gi):
                        p.wait(ss, zev[0])
                        if gi >= 2:
                            p.wait(ss, zgev[gi - 2])
                        zsev[gi] = p.inc(ss.activation(
                            out=zrot[:, (nt % 2) * 512:(nt % 2 + 1) * 512],
                            in_=ps_ip[:, (nt % 2) * 512:(nt % 2) * 512 + 512],
                            func=AF.Silu), "S")
                    p.S(s_)
                    def g(gg, zst=zst, nt=nt, gi=gi):
                        p.wait(gg, zsev[gi])
                        zgev[gi] = p.inc(gg.dma_start(
                            out=self.sp_zs[d][nt * 128:(nt + 1) * 128,
                                              zst * 512:(zst + 1) * 512],
                            in_=zrot[:, (nt % 2) * 512:(nt % 2 + 1) * 512]), "SP")
                    p.G(g)

            rowsl = slice(0, 64) if d == "f" else slice(64, 128)
            def gsend(gg):
                p.wait(gg, p.now("V"))
                p.inc(gg.dma_start(out=self.cc1_in[rowsl, 0:D_INNER],
                                   in_=self.s_state[:, sdir]), "SP")
            p.G(gsend)

            if f"xbc_{d}" in self.dbg:
                self.dump(f"xbc_{d}", XBC, 17, ST1, src_width=ST1)
            p.barrier()

    def dump_dram(self, name, src, shape):
        d = self.mkdbg(name, list(shape))
        if d is None:
            return
        p = self
        ph = contextlib.ExitStack()
        with ph:
            bb = self.sbt(ph, "dbg_b_" + name, [128, shape[1]], F32)
            nrt = (shape[0] + 127) // 128
            for rt in range(nrt):
                r0 = rt * 128
                r1 = min(shape[0], r0 + 128)
                ev = [None]
                def g1(gg, r0=r0, r1=r1, ev=ev):
                    p.wait(gg, p.now("SP"), p.now("V"), p.now("S"), p.now("T"))
                    ev[0] = p.inc(gg.dma_start(out=bb[0:r1 - r0, :],
                                               in_=src[r0:r1, :]), "SP")
                p.G(g1)
                def g2(gg, r0=r0, r1=r1, ev=ev):
                    p.wait(gg, ev[0])
                    p.inc(gg.dma_start(out=d[r0:r1, :], in_=bb[0:r1 - r0, :]), "SP")
                p.G(g2)
            self.barrier()

    def x1_issue(self, stream):
        p = self
        def g(gg):
            p.wait(gg, p.now("V"))
            p.inc(gg.dma_start(
                out=self.cc1_in[:, 2048:2056],
                in_=stream[:].rearrange("p (a b) -> p a b", b=SEG)[:, :, SEG - 1:SEG],
                allow_slow_non_contiguous=True), "SP")
            gg.wait_ge(p.sem["SP"], p.cnt["SP"])
            p.inc(gg.collective_compute(
                "AllGather", ALU.bypass, replica_groups=PAIRS,
                ins=[self.cc1_in[:, :]], outs=[self.cc1_out[:, :]]), "G")
        p.G(g)
        self.ev_x1 = p.now("G")

    def x1_consume(self, ph):
        p = self
        tf = [self.sbt(ph, f"x1tf{i}", [64, D_INNER], F32) for i in range(2)]
        tr = [self.sbt(ph, f"x1tr{i}", [64, D_INNER], F32) for i in range(2)]
        tm = [self.sbt(ph, f"x1tm{i}", [128, 8], F32) for i in range(2)]
        ev = [None]
        def g(gg):
            p.wait(gg, self.ev_x1)
            for sl, base in ((0, 0), (1, 128)):
                gg.dma_start(out=tf[sl][:],
                             in_=self.cc1_out[base:base + 64, 0:D_INNER]
                             ).then_inc(p.sem["SP"], 16)
                gg.dma_start(out=tr[sl][:],
                             in_=self.cc1_out[base + 64:base + 128, 0:D_INNER]
                             ).then_inc(p.sem["SP"], 16)
                gg.dma_start(out=tm[sl][:],
                             in_=self.cc1_out[base:base + 128, 2048:2056]
                             ).then_inc(p.sem["SP"], 16)
            p.cnt["SP"] += 96
            ev[0] = p.now("SP")
        p.G(g)
        x1c_sec = int(os.environ.get("KB_X1C_SEC", "99"))
        if x1c_sec <= 1:
            return
        def v(vv):
            p.wait(vv, ev[0])
            F = self.flags_sb
            vv.tensor_scalar(out=self.s_state[:, 0:D_INNER], in0=tf[0][:],
                             scalar1=F[0:64, 2:3], scalar2=None, op0=ALU.mult)
            vv.scalar_tensor_tensor(
                out=self.s_state[:, 0:D_INNER], in0=tf[1][:],
                scalar=F[0:64, 3:4], in1=self.s_state[:, 0:D_INNER],
                op0=ALU.mult, op1=ALU.add)
            vv.tensor_scalar(out=self.s_state[:, D_INNER:], in0=tr[0][:],
                             scalar1=F[0:64, 4:5], scalar2=None, op0=ALU.mult)
            vv.scalar_tensor_tensor(
                out=self.s_state[:, D_INNER:], in0=tr[1][:],
                scalar=F[0:64, 5:6], in1=self.s_state[:, D_INNER:],
                op0=ALU.mult, op1=ALU.add)
            vv.tensor_scalar(out=tm[0][:], in0=tm[0][:],
                             scalar1=F[:, 2:3], scalar2=None, op0=ALU.mult)
            p.inc(vv.scalar_tensor_tensor(
                out=tm[0][:], in0=tm[1][:], scalar=F[:, 3:4],
                in1=tm[0][:], op0=ALU.mult, op1=ALU.add), "V")
        p.V(v)
        if x1c_sec <= 2:
            return
        t0 = tm[0]
        def g2(gg):
            p.wait(gg, p.now("V"))
            for rt in range(8):
                p.inc(gg.dma_start(
                    out=self.sp_s1[rt * 128:(rt + 1) * 128, 0:1],
                    in_=t0[:, rt:rt + 1],
                    allow_slow_non_contiguous=True), "SP")
        p.G(g2)

    def p3(self, d, rev):
        p, nc = self, self.nc
        sdir = slice(0, D_INNER) if d == "f" else slice(D_INNER, 2 * D_INNER)
        STP = 256
        ph = contextlib.ExitStack()
        with ph:
            zs = self.sbt(ph, "zs", [128, 16 * STP], F32)
            xD = self.sbt(ph, "xD", [128, 16 * STP], BF16)
            xt0 = self.sbt(ph, "xt0", [128, D_INNER], BF16)
            xt = [xt0, xt0]
            Bc = self.sbt(ph, "Bc", [64, STP], BF16)
            Cc = self.sbt(ph, "Cc", [64, STP], BF16)
            ctl = [self.sbt(ph, f"ctl{i}", [128, 64], F32) for i in range(2)]
            CLB = self.sbt(ph, "CLB", [128, NHEADS * Q], F32)
            gmb = self.sbt(ph, "gmb", [64, NHEADS * Q], F32)
            tprime = CLB
            Ee = self.sbt(ph, "Ee", [128, NHEADS * Q], BF16)
            Sreal = self.sbt(ph, "Sreal", [128, NHEADS * Q], BF16)
            Svirt = self.sbt(ph, "Svirt", [64, NHEADS * Q], BF16)
            Gm = self.sbt(ph, "Gm", [128, Q], F32)
            y_sb = self.sbt(ph, "y_sb", [128, 16 * STP], F32)
            sadd20 = self.sbt(ph, "sadd20", [64, D_INNER], F32)
            s_bf = self.sbt(ph, "s_bf", [64, D_INNER], BF16)
            sadd2 = [sadd20, sadd20]
            dtile = self.sbt(ph, "dtile", [64, D_INNER], F32)
            gbf = self.sbt(ph, "gbf", [128, 16 * STP], BF16)
            woutb = self.sbt(ph, "woutb", [128, 16 * 1024], BF16)
            guardhi = self.sbt(ph, "guardhi", [128, 1024], BF16)
            wstg = [self.sbt(ph, f"wstg3{i}", [128, 1024], F32) for i in range(2)]
            rnrow = self.sbt(ph, "rnrow", [1, STP], F32)
            rnb = self.sbt(ph, "rnb", [128, STP], F32)
            oacc = self.sbt(ph, "oacc", [128, STP], F32)
            ps_y = self.pst(ph, "ps_y", [128, 2048])
            ps_g = self.pst(ph, "ps_g", [128, 512])
            ps_r = self.pst(ph, "ps_r", [1, 512])
            ps_o = self.pst(ph, "ps_o", [128, 1024])

            rmsw_sb = self.sbt(ph, "rmsw_sb", [128, 16], F32)
            cast_ev = [None] * 16
            def y0(yq):
                p.inc(yq.dma_start(out=rmsw_sb[:], in_=self.rmsw[d][:, :]), "LD")
            p.Y(y0)
            ev_rm = p.now("LD")
            for kt in range(16):
                ldev = [None]
                def y(yq, kt=kt, ldev=ldev):
                    if kt >= 2:
                        p.wait(yq, cast_ev[kt - 2])
                    ldev[0] = p.inc(yq.dma_start(
                        out=wstg[kt % 2][:],
                        in_=self.wout[d][kt * 128:(kt + 1) * 128, :]), "LD")
                p.Y(y)
                def v(vv, kt=kt, ldev=ldev):
                    p.wait(vv, ldev[0], ev_rm)
                    cast_ev[kt] = p.inc(vv.tensor_scalar(
                        out=woutb[:, kt * 1024:(kt + 1) * 1024],
                        in0=wstg[kt % 2][:], scalar1=rmsw_sb[:, kt:kt + 1],
                        scalar2=None, op0=ALU.mult), "V")
                p.V(v)
            ev_wo = cast_ev[15]
            dwb0 = self.mkdbg(f"woutb0_{d}", [128, 16 * 1024])
            if dwb0 is not None:
                def gdwb0(gg, dwb0=dwb0):
                    p.wait(gg, ev_wo)
                    p.inc(gg.dma_start(out=dwb0[:, :], in_=woutb[:]), "SP")
                p.G(gdwb0)
                p.barrier()

            p3s = int(os.environ.get("KB_P3_SEC", "99"))
            sts = list(range(8)) if not rev else list(range(7, -1, -1))
            chain_ev = [p.now("V")]
            ldch = {}
            prev = {"yeev": (0, 0), "yev": (0, 0), "eev": (0, 0),
                    "svev": (0, 0), "goev": (0, 0)}
            for sti, st in enumerate(sts):
                scol = slice(st * STP, (st + 1) * STP)
                ldst = [None]
                def yst(yq, st=st, ldst=ldst):
                    if sti > 0:
                        p.wait(yq, p.now("V"), p.now("T"), p.now("S"))
                    for ct in range(16):
                        yq.dma_start(
                            out=zs[:, ct * STP:(ct + 1) * STP],
                            in_=self.sp_zs[d][ct * 128:(ct + 1) * 128, scol]
                        ).then_inc(p.sem["LD"], 16)
                        yq.dma_start(
                            out=xD[:, ct * STP:(ct + 1) * STP],
                            in_=self.sp_xD[d][ct * 128:(ct + 1) * 128, scol]
                        ).then_inc(p.sem["LD"], 16)
                    yq.dma_start(out=Bc[:], in_=self.sp_B[d][:, scol]).then_inc(
                        p.sem["LD"], 16)
                    yq.dma_start(out=Cc[:], in_=self.sp_C[d][:, scol]).then_inc(
                        p.sem["LD"], 16)
                    p.cnt["LD"] += 34 * 16
                    ldst[0] = p.now("LD")
                p.Y(yst)
                if p3s <= 1:
                    continue

                qs_l = [0, 1] if not rev else [1, 0]
                for qi in qs_l:
                    cg = st * 2 + qi
                    ldq = [None]
                    def yq_(yq, cg=cg, qi=qi, ldq=ldq):
                        p.wait(yq, chain_ev[0])
                        yq.dma_start(out=xt[qi % 2][:],
                                     in_=self.sp_xT[d][cg * Q:(cg + 1) * Q, :]
                                     ).then_inc(p.sem["LD"], 16)
                        yq.dma_start(out=ctl[qi % 2][:],
                                     in_=self.sp_ctl[d][cg * Q:(cg + 1) * Q, :]
                                     ).then_inc(p.sem["LD"], 16)
                        yq.dma_start(out=sadd2[qi % 2][:],
                                     in_=self.sp_sadd[d][:, cg * D_INNER:(cg + 1) * D_INNER]
                                     ).then_inc(p.sem["LD"], 16)
                        yq.dma_start(out=CLB[:], in_=rep_outer(
                            self.sp_cl[d][:, cg * Q:(cg + 1) * Q].transpose([1, 0])
                            if False else self.sp_cl[d][0:1, 0:1], 1)
                        ).then_inc(p.sem["LD"], 16) if False else None
                        p.cnt["LD"] += 48
                        ldq[0] = p.now("LD")
                    p.Y(yq_)
                    ldb = [None]
                    def gq(gg, cg=cg, ldb=ldb):
                        p.wait(gg, p.now("V"), prev["eev"])
                        src = self.sp_cl[d].rearrange(
                            "h (c i) -> h c i", i=Q)[:, cg, :]
                        src = src.rearrange(
                            "h (a i) -> a h i", a=1).broadcast_to([128, 32, Q])
                        gg.dma_start(out=CLB[:].rearrange("p (h i) -> p h i", i=Q),
                                     in_=src).then_inc(p.sem["SP"], 16)
                        src2 = self.sp_gam[d].rearrange(
                            "h (c i) -> h c i", i=Q)[:, cg, :]
                        src2 = src2.rearrange("h (a i) -> a h i", a=1).broadcast_to([64, 32, Q])
                        gg.dma_start(out=gmb[:].rearrange("p (h i) -> p h i", i=Q),
                                     in_=src2).then_inc(p.sem["SP"], 16)
                        gg.dma_start(
                            out=dtile[:], in_=self.sp_dr[d][cg:cg + 1, :]
                            .broadcast_to([64, D_INNER])
                        ).then_inc(p.sem["SP"], 16)
                        p.cnt["SP"] += 48
                        ldb[0] = p.now("SP")
                    p.G(gq)
                    if p3s <= 2:
                        continue

                    gev = [None]
                    def tg(tt, qi=qi, gev=gev, ldst=ldst):
                        p.wait(tt, ldst[0], prev.get("gsev", (0, 0)))
                        gev[0] = p.inc(tt.matmul(
                            ps_g[:, 0:Q], Bc[:, qi * Q:(qi + 1) * Q],
                            Cc[:, qi * Q:(qi + 1) * Q], start=True, stop=True), "T")
                    p.T(tg)
                    gsev = [None]
                    def sg2(ss, gev=gev, gsev=gsev):
                        p.wait(ss, gev[0], prev["svev"])
                        gsev[0] = p.inc(ss.copy(Gm[:], ps_g[:, 0:Q]), "S")
                    p.S(sg2)
                    gmev = [None]
                    def gg2(gg, gsev=gsev, gmev=gmev):
                        p.wait(gg, gsev[0])
                        if not rev:
                            gmev[0] = p.inc(gg.affine_select(
                                out=Gm[:], in_=Gm[:], pattern=[[1, Q]],
                                compare_op=ALU.is_ge, fill=0.0, base=0,
                                channel_multiplier=-1), "G")
                        else:
                            gmev[0] = p.inc(gg.affine_select(
                                out=Gm[:], in_=Gm[:], pattern=[[-1, Q]],
                                compare_op=ALU.is_ge, fill=0.0, base=0,
                                channel_multiplier=1), "G")
                    p.G(gg2)

                    vtev = [None]
                    def vt(vv, qi=qi, ldq=ldq, ldb=ldb, vtev=vtev):
                        p.wait(vv, ldq[0], ldb[0], prev["eev"])
                        cfT = rep_inner(ctl[qi % 2][:, 0:32], Q)
                        clb3 = CLB[:].rearrange("p (h i) -> p h i", i=Q)
                        tp3 = tprime[:].rearrange("p (h i) -> p h i", i=Q)
                        if not rev:
                            vv.tensor_tensor(out=tp3, in0=clb3, in1=cfT,
                                             op=ALU.subtract)
                        else:
                            vv.tensor_tensor(out=tp3, in0=cfT, in1=clb3,
                                             op=ALU.subtract)
                        vtev[0] = p.inc(vv.tensor_tensor(
                            out=tp3, in0=tp3,
                            in1=rep_inner(ctl[qi % 2][:, 32:64], Q),
                            op=ALU.min), "V")
                    p.V(vt)
                    eev = [None]
                    def se(ss, vtev=vtev, eev=eev):
                        p.wait(ss, vtev[0])
                        eev[0] = p.inc(ss.activation(out=Ee[:], in_=tprime[:],
                                                     func=AF.Exp), "S")
                    p.S(se)
                    svev = [None]
                    def vs(vv, gmev=gmev, eev=eev, ldb=ldb, svev=svev):
                        p.wait(vv, gmev[0], eev[0])
                        vv.tensor_tensor(
                            out=Sreal[:].rearrange("p (h i) -> p h i", i=Q),
                            in0=rep_outer(Gm[:], NHEADS),
                            in1=Ee[:].rearrange("p (h i) -> p h i", i=Q),
                            op=ALU.mult)
                        svev[0] = p.inc(vv.tensor_tensor(
                            out=Svirt[:].rearrange("p (h i) -> p h i", i=Q),
                            in0=rep_outer(Cc[0:64, (0 if True else 0):Q] if False
                                          else Cc[:, 0:Q], NHEADS) if False else
                            rep_outer(Cc[:, 0:Q], NHEADS),
                            in1=gmb[:].rearrange("p (h i) -> p h i", i=Q),
                            op=ALU.mult), "V")
                    def vs_fixed(vv, qi=qi, gmev=gmev, eev=eev, ldb=ldb, svev=svev):
                        p.wait(vv, gmev[0], eev[0], ldb[0], prev["yev"])
                        vv.tensor_tensor(
                            out=Sreal[:].rearrange("p (h i) -> p h i", i=Q),
                            in0=rep_outer(Gm[:], NHEADS),
                            in1=Ee[:].rearrange("p (h i) -> p h i", i=Q),
                            op=ALU.mult)
                        vv.tensor_tensor(
                            out=Svirt[:].rearrange("p (h i) -> p h i", i=Q),
                            in0=rep_outer(Cc[:, qi * Q:(qi + 1) * Q], NHEADS),
                            in1=gmb[:].rearrange("p (h i) -> p h i", i=Q),
                            op=ALU.mult)
                        p.wait(vv, chain_ev[0])
                        svev[0] = p.inc(vv.tensor_copy(
                            s_bf[:], self.s_state[:, sdir]), "V")
                    p.V(vs_fixed)
                    if p3s <= 3:
                        continue

                    if cg == 0 and d == "f":
                        dsr = self.mkdbg("sreal0", [128, NHEADS * Q])
                        dclb = self.mkdbg("clb0", [128, NHEADS * Q])
                        dgm = self.mkdbg("gm0", [128, Q])
                        dxt = self.mkdbg("xt0d", [128, D_INNER])
                        if dsr is not None:
                            def gdsr(gg):
                                p.wait(gg, p.now("V"), p.now("S"), p.now("G"))
                                p.inc(gg.dma_start(out=dsr[:, :], in_=Sreal[:]), "SP")
                                p.inc(gg.dma_start(out=dclb[:, :], in_=CLB[:]), "SP")
                                p.inc(gg.dma_start(out=dgm[:, :], in_=Gm[:]), "SP")
                                p.inc(gg.dma_start(out=dxt[:, :], in_=xt[qi % 2][:]), "SP")
                            p.G(gdsr)
                            p.barrier()
                    yev = [None, None]
                    yeev = [None, None]
                    for grp in range(2):
                        def tyg(tt, qi=qi, grp=grp):
                            if grp == 0:
                                p.wait(tt, svev[0], ldq[0], chain_ev[0],
                                       prev["yeev"])
                            else:
                                p.wait(tt, yeev[0])
                            novirt = bool(os.environ.get("KB_P3_NOVIRT"))
                            for hp in range(grp * 8, grp * 8 + 8):
                                st_ap = xt[qi % 2][:, hp * 128:(hp + 1) * 128]
                                hl = hp % 8
                                first = (hl % 4 == 0)
                                last = (hl % 4 == 3)
                                for hh in range(2):
                                    h = hp * 2 + hh
                                    pslice = ps_y[
                                        hh * 64:(hh + 1) * 64,
                                        hh * 1024 + hl * 128:
                                        hh * 1024 + hl * 128 + 128]
                                    tt.matmul(
                                        pslice,
                                        st_ap[:, hh * 64:(hh + 1) * 64],
                                        Sreal[:, h * Q:(h + 1) * Q],
                                        start=first,
                                        stop=(last and novirt),
                                        tile_position=(0, hh * 64))
                                    if not novirt:
                                        tt.matmul(
                                            pslice,
                                            s_bf[:, h * 64:(h + 1) * 64],
                                            Svirt[:, h * Q:(h + 1) * Q],
                                            start=False, stop=last,
                                            tile_position=(0, hh * 64))
                            yev[grp] = p.incl(tt, 'T')
                        p.T(tyg)

                        def vyg(vv, qi=qi, grp=grp):
                            p.wait(vv, yev[grp])
                            for hh in range(2):
                                hps = slice(grp * 8, grp * 8 + 8)
                                xAP = xD[hh * 64:(hh + 1) * 64, :].rearrange(
                                    "p (c w) -> p c w", w=STP)[
                                    :, hps, qi * Q:(qi + 1) * Q]
                                yAP = y_sb[hh * 64:(hh + 1) * 64, :].rearrange(
                                    "p (c w) -> p c w", w=STP)[
                                    :, hps, qi * Q:(qi + 1) * Q]
                                pAP = ps_y[hh * 64:(hh + 1) * 64,
                                           hh * 1024:hh * 1024 + 1024].rearrange(
                                    "p (c w) -> p c w", w=128)
                                if os.environ.get("KB_P3_NOD"):
                                    vv.scalar_tensor_tensor(
                                        out=yAP, in0=pAP, scalar=1.0, in1=pAP,
                                        op0=ALU.mult, op1=ALU.bypass)
                                else:
                                    vv.scalar_tensor_tensor(
                                        out=yAP, in0=xAP, scalar=1.0, in1=pAP,
                                        op0=ALU.mult, op1=ALU.add)
                            yeev[grp] = p.incl(vv, 'V')
                        p.V(vyg)

                    def vchain(vv, qi=qi, ldb=ldb, yev=yev):
                        p.wait(vv, ldb[0], yev[1])
                        vv.tensor_tensor(out=self.s_state[:, sdir],
                                         in0=self.s_state[:, sdir],
                                         in1=dtile[:], op=ALU.mult)
                        chain_ev[0] = p.inc(vv.tensor_tensor(
                            out=self.s_state[:, sdir], in0=self.s_state[:, sdir],
                            in1=sadd2[qi % 2][:], op=ALU.add), "V")
                    p.V(vchain)
                    ldch[qi % 2] = chain_ev[0]
                    prev["yeev"] = yeev[1]
                    prev["yev"] = yev[1]
                    prev["eev"] = eev[0]
                    prev["svev"] = svev[0]
                    prev["gsev"] = gsev[0]

                if p3s <= 4:
                    continue
                dy = self.mkdbg(f"y_{d}", [D_INNER, TOK])
                if dy is not None:
                    def gdy(gg, st=st, dy=dy):
                        p.wait(gg, p.now("V"))
                        for ct in range(16):
                            p.inc(gg.dma_start(
                                out=dy[ct * 128:(ct + 1) * 128,
                                       st * STP:(st + 1) * STP],
                                in_=y_sb[:, ct * STP:(ct + 1) * STP]), "SP")
                    p.G(gdy)
                    p.barrier()
                def vgate(vv, st=st):
                    p.wait(vv, p.now("V"), p.now("T"))
                    vv.tensor_tensor(out=y_sb[:], in0=y_sb[:], in1=zs[:],
                                     op=ALU.mult)
                    p.inc(vv.tensor_copy(gbf[:], y_sb[:]), "V")
                p.V(vgate)
                dg = self.mkdbg(f"gate_{d}", [D_INNER, TOK])
                if dg is not None:
                    def gdg(gg, st=st, dg=dg):
                        p.wait(gg, p.now("V"))
                        for ct in range(16):
                            p.inc(gg.dma_start(
                                out=dg[ct * 128:(ct + 1) * 128,
                                       st * STP:(st + 1) * STP],
                                in_=y_sb[:, ct * STP:(ct + 1) * STP]), "SP")
                    p.G(gdg)
                    p.barrier()
                def sgate(ss, st=st):
                    p.wait(ss, p.now("V"))
                    p.inc(ss.activation(out=xD[:], in_=y_sb[:], func=AF.Square), "S")
                p.S(sgate)
                def trms(tt, st=st):
                    p.wait(tt, p.now("S"))
                    for ct in range(16):
                        tt.matmul(ps_r[:, 0:STP], self.ones_bf[:],
                                  xD[:, ct * STP:(ct + 1) * STP],
                                  start=(ct == 0), stop=(ct == 15))
                    p.incl(tt, 'T')
                p.T(trms)
                def srms(ss, st=st):
                    p.wait(ss, p.now("T"))
                    ss.activation(out=rnrow[:], in_=ps_r[0:1, 0:STP], func=AF.Ln,
                                  scale=self.cc(self.C_IDIM)[0:1, :],
                                  bias=self.cc(self.C_EPS)[0:1, :])
                    p.inc(ss.activation(out=rnrow[:], in_=rnrow[:], func=AF.Exp,
                                        scale=self.cc(self.C_NH)[0:1, :]), "S")
                p.S(srms)
                def grn(gg, st=st):
                    p.wait(gg, p.now("S"))
                    gg.dma_start(out=self.sp_rows[8:9, 0:STP], in_=rnrow[:]
                                 ).then_inc(p.sem["SP"], 16)
                    p.cnt["SP"] += 16
                    gg.wait_ge(p.sem["SP"], p.cnt["SP"])
                    p.inc(gg.dma_start(out=rnb[:], in_=self.sp_rows[
                        8:9, 0:STP].broadcast_to([128, STP])), "SP")
                p.G(grn)
                ev_rnb = p.now("SP")
                dr_ = self.mkdbg(f"rnb_{d}", [128, 8 * STP])
                if dr_ is not None:
                    def gdr(gg, st=st, dr_=dr_):
                        p.wait(gg, ev_rnb)
                        p.inc(gg.dma_start(
                            out=dr_[:, st * STP:(st + 1) * STP],
                            in_=rnb[:]), "SP")
                    p.G(gdr)
                    p.barrier()

                if p3s <= 5:
                    continue
                for ont in range(8):
                    oev = [None]
                    def to(tt, ont=ont, st=st, oev=oev):
                        if ont == 0:
                            p.wait(tt, ev_wo, p.now("V"))
                        if ont >= 2:
                            p.wait(tt, p.now("SP"))
                        bk = (ont % 2) * 512
                        for kt in range(16):
                            tt.matmul(ps_o[:, bk:bk + STP],
                                      woutb[:, kt * 1024 + ont * 128:
                                            kt * 1024 + ont * 128 + 128],
                                      gbf[:, kt * STP:(kt + 1) * STP],
                                      start=(kt == 0), stop=(kt == 15))
                        oev[0] = p.incl(tt, 'T')
                    p.T(to)
                    sev = [None]
                    if os.environ.get("KB_P3_STOUCH"):
                        def sto(ss, ont=ont, oev=oev, sev=sev):
                            p.wait(ss, oev[0])
                            bk = (ont % 2) * 512
                            sev[0] = p.inc(ss.copy(rnrow[0:1, 0:STP],
                                                   ps_o[0:1, bk:bk + STP]), "S")
                        p.S(sto)
                    def vo(vv, ont=ont, oev=oev, sev=sev):
                        p.wait(vv, oev[0], ev_rnb, prev["goev"])
                        if sev[0] is not None:
                            p.wait(vv, sev[0])
                        bk = (ont % 2) * 512
                        p.inc(vv.tensor_tensor(out=oacc[:], in0=ps_o[:, bk:bk + STP],
                                               in1=rnb[:], op=ALU.mult), "V")
                    p.V(vo)
                    def go(gg, ont=ont, st=st):
                        p.wait(gg, p.now("V"))
                        if d == "f":
                            p.inc(gg.dma_start(
                                out=self.acc_m[ont * 128:(ont + 1) * 128, scol],
                                in_=oacc[:]), "SP")
                        else:
                            p.inc(gg.dma_start(
                                out=self.acc_m[ont * 128:(ont + 1) * 128, scol],
                                in_=oacc[:], accum_op=ALU.add), "SP")
                        prev["goev"] = p.now("SP")
                    p.G(go)
                if os.environ.get("KB_P3_OPBAR"):
                    p.barrier()
            dwb = self.mkdbg(f"woutb_{d}", [128, 16 * 1024])
            if dwb is not None:
                def gdwb(gg, dwb=dwb):
                    p.wait(gg, p.now("V"), p.now("T"))
                    p.inc(gg.dma_start(out=dwb[:, :], in_=woutb[:]), "SP")
                p.G(gdwb)
            p.barrier()

    def ffn_qkv(self, lidx, ph, QH, KH, VT, src_dram, src_dt, n_mc, kw0, tc0):
        p, nc = self, self.nc
        qsl = self.sbt(ph, "qsl", [128, 1], F32)
        ksl = self.sbt(ph, "ksl", [128, 1], F32)
        wqb = self.sbt(ph, "wqb", [128, 8 * 1536], BF16)
        wstq = [self.sbt(ph, f"wstq{i}", [128, 1536], F32) for i in range(2)]
        sbf = self.sbt(ph, "sbf", [128, 8 * 513], BF16)
        sstg = [self.sbt(ph, f"sstg{i}", [128, 513], F32) for i in range(2)]
        qkst = self.sbt(ph, "qkst", [128, 8 * 512], F32)
        vstg = self.sbt(ph, "vstg", [128, 4 * 512], BF16)
        sqb = self.sbt(ph, "sqb", [128, 512], BF16)
        nrw = self.sbt(ph, "nrw", [2, 2048], F32)
        invb = self.sbt(ph, "invb", [128, 512], F32)
        tq = self.sbt(ph, "tq", [128, 512], F32)
        tperm = self.sbt(ph, "tperm", [128, 512], F32)
        r1 = self.sbt(ph, "r1", [128, 512], F32)
        rtb = self.sbt(ph, "rtb", [128, 4 * 512], F32)
        ps_qk = self.pst(ph, "ps_qk", [128, 1024])
        ps_n = self.pst(ph, "ps_n", [2, 2048])
        ps_vt = self.pst(ph, "ps_vt", [128, 2048], BF16)

        def ysc(yq):
            yq.dma_start(out=qsl[:], in_=self.qsc[lidx, :, :]).then_inc(
                p.sem["LD"], 16)
            yq.dma_start(out=ksl[:], in_=self.ksc[lidx, :, :]).then_inc(
                p.sem["LD"], 16)
            p.cnt["LD"] += 32
        p.Y(ysc)
        ev_sc = p.now("LD")

        cast_ev = [None] * 8
        for kt in range(8):
            ldev = [None]
            def y(yq, kt=kt, ldev=ldev):
                if kt >= 2:
                    p.wait(yq, cast_ev[kt - 2])
                ldev[0] = p.inc(yq.dma_start(
                    out=wstq[kt % 2][:],
                    in_=self.wqkv[lidx, kt * 128:(kt + 1) * 128, :]), "LD")
            p.Y(y)
            def v(vv, kt=kt, ldev=ldev):
                p.wait(vv, ldev[0])
                cast_ev[kt] = p.inc(vv.tensor_copy(
                    wqb[:, kt * 1536:(kt + 1) * 1536], wstq[kt % 2][:]), "V")
            p.V(v)
        ev_wq = cast_ev[7]

        qkv_sec = int(os.environ.get("KB_QKV_SEC", "99"))
        prevm = {"stg": ("V", 0), "use": [("T", 0)] * 8}
        for m in range(n_mc):
            if qkv_sec == 0:
                break
            kw = kw0 + m
            stev = [None] * 8
            for rt in range(8):
                ldev = [None]
                def y(yq, m=m, rt=rt, ldev=ldev):
                    p.wait(yq, prevm["use"][rt])
                    if src_dram.dtype == F32 and rt >= 2 and stev[rt - 2]:
                        p.wait(yq, stev[rt - 2])
                    c0 = src_dt(m)
                    ldev[0] = p.inc(yq.dma_start(
                        out=sstg[rt % 2][:] if src_dram.dtype == F32 else
                        sbf[:, rt * 513:(rt + 1) * 513],
                        in_=src_dram[rt * 128:(rt + 1) * 128, c0:c0 + 513]), "LD")
                p.Y(y)
                if src_dram.dtype == F32:
                    def v(vv, rt=rt, ldev=ldev):
                        p.wait(vv, ldev[0])
                        stev[rt] = p.inc(vv.tensor_copy(
                            sbf[:, rt * 513:(rt + 1) * 513], sstg[rt % 2][:]), "V")
                    p.V(v)
                else:
                    stev[rt] = ldev[0]
            rtev = [None]
            def yr(yq, m=m, rtev=rtev):
                p.wait(yq, prevm["stg"])
                t0 = tc0(m)
                for i in range(4):
                    yq.dma_start(out=rtb[:, i * 512:(i + 1) * 512],
                                 in_=self.rot[i, :, t0:t0 + 512]).then_inc(
                        p.sem["LD"], 16)
                p.cnt["LD"] += 64
                rtev[0] = p.now("LD")
            p.Y(yr)

            if qkv_sec <= 1:
                prevm["stg"] = p.now("V")
                continue
            ntev = {}
            for nt in range(12):
                def t(tt, m=m, nt=nt):
                    if nt < 2:
                        p.wait(tt, ev_wq, *stev)
                        p.wait(tt, p.now("V"), p.now("S"))
                    else:
                        p.wait(tt, ntev[nt - 2][1])
                    bk = (nt % 2) * 512
                    for kt in range(8):
                        off = 1 if kt < 4 else 0
                        tt.matmul(ps_qk[:, bk:bk + 512],
                                  wqb[:, kt * 1536 + nt * 128:
                                      kt * 1536 + nt * 128 + 128],
                                  sbf[:, rt_off(kt, off)],
                                  start=(kt == 0), stop=(kt == 7))
                    ntev[nt] = [p.incl(tt, 'T'), None]
                def rt_off(kt, off):
                    return slice(kt * 513 + off, kt * 513 + off + 512)
                p.T(t)
                def s_(ss, m=m, nt=nt):
                    p.wait(ss, ntev[nt][0])
                    bk = (nt % 2) * 512
                    if nt < 8:
                        e = p.inc(ss.copy(qkst[:, nt * 512:(nt + 1) * 512],
                                          ps_qk[:, bk:bk + 512]), "S")
                    else:
                        e = p.inc(ss.activation(
                            out=vstg[:, (nt - 8) * 512:(nt - 7) * 512],
                            in_=ps_qk[:, bk:bk + 512], func=AF.Copy), "S")
                    ntev[nt][1] = e
                p.S(s_)
            ev_qkv = p.now("S")

            if qkv_sec <= 2:
                prevm["stg"] = p.now("V")
                prevm["use"] = [ntev[11][0]] * 8
                continue
            vt_evs = []
            vt_n = int(os.environ.get("KB_VT_N", "16"))
            for vt4 in range(4):
                for blk in range(4):
                    if vt4 * 4 + blk >= vt_n:
                        continue
                    tev = [None]
                    pb = (len(vt_evs) % 2) * 1024
                    def t(tt, m=m, vt4=vt4, blk=blk, tev=tev, pb=pb):
                        p.wait(tt, ntev[8 + vt4][1])
                        if len(vt_evs) >= 2:
                            p.wait(tt, vt_evs[-2])
                        tev[0] = p.inc(tt.transpose(
                            ps_vt[:, pb:pb + 128],
                            vstg[:, vt4 * 512 + blk * 128:
                                 vt4 * 512 + blk * 128 + 128],
                            self.ident_bf[:]), "T")
                    p.T(t)
                    def v(vv, m=m, vt4=vt4, blk=blk, tev=tev, pb=pb):
                        p.wait(vv, tev[0])
                        ktg = (kw0 + m) * 4 + blk
                        vt_evs.append(p.inc(vv.tensor_copy(
                            VT[:, ktg * 512 + vt4 * 128:ktg * 512 + vt4 * 128 + 128],
                            ps_vt[:, pb:pb + 128]), "V"))
                    p.V(v)

            if qkv_sec <= 3:
                prevm["stg"] = p.now("V")
                prevm["use"] = [ntev[11][0]] * 8
                continue
            use_evs = [None] * 8
            for iq, isq in ((0, False), (1, True)):
                if iq == 1 and kw == 0:
                    continue
                for t4 in range(4):
                    base = (4 if iq == 0 else 0) + t4
                    ti = (0 + t4) if iq == 1 else (4 + t4)
                    sq_ev = [None]
                    def s_(ss, m=m, ti=ti, sq_ev=sq_ev):
                        p.wait(ss, ev_qkv, p.now("V"))
                        sq_ev[0] = p.inc(ss.activation(
                            out=sqb[:], in_=qkst[:, ti * 512:(ti + 1) * 512],
                            func=AF.Square), "S")
                    p.S(s_)
                    nev = [None]
                    def t(tt, m=m, ti=ti, t4=t4, sq_ev=sq_ev, nev=nev):
                        p.wait(tt, sq_ev[0])
                        nev[0] = p.inc(tt.matmul(
                            ps_n[:, t4 * 512:(t4 + 1) * 512], self.blk2[:],
                            sqb[:], start=True, stop=True), "T")
                    p.T(t)
                    sinv = [None]
                    def s2(ss, m=m, t4=t4, nev=nev, sinv=sinv):
                        p.wait(ss, nev[0])
                        sl = slice(t4 * 512, (t4 + 1) * 512)
                        ss.activation(out=nrw[:, sl], in_=ps_n[:, sl], func=AF.Ln,
                                      bias=self.cc(self.C_TINY)[0:2, :])
                        sinv[0] = p.inc(ss.activation(
                            out=nrw[:, sl], in_=nrw[:, sl], func=AF.Exp,
                            scale=self.cc(self.C_NH)[0:2, :]), "S")
                    p.S(s2)
                    vcl = [None]
                    def v0(vv, t4=t4, sinv=sinv, vcl=vcl):
                        p.wait(vv, sinv[0])
                        sl = slice(t4 * 512, (t4 + 1) * 512)
                        vcl[0] = p.inc(vv.tensor_scalar_min(
                            nrw[:, sl], nrw[:, sl], 1e12), "V")
                    p.V(v0)
                    bev = [None]
                    def g(gg, m=m, t4=t4, vcl=vcl, bev=bev):
                        p.wait(gg, vcl[0])
                        gg.dma_start(out=self.sp_rows[10:12, 0:512],
                                     in_=nrw[:, t4 * 512:(t4 + 1) * 512]
                                     ).then_inc(p.sem["SP"], 16)
                        p.cnt["SP"] += 16
                        gg.wait_ge(p.sem["SP"], p.cnt["SP"])
                        src = self.sp_rows[10:12, 0:512].rearrange(
                            "a (b w) -> a b w", b=1).broadcast_to([2, 64, 512])
                        bev[0] = p.inc(gg.dma_start(out=invb[:], in_=src), "SP")
                    p.G(g)
                    def v1(vv, m=m, iq=iq, ti=ti, t4=t4, bev=bev):
                        p.wait(vv, bev[0], rtev[0])
                        qk = qkst[:, ti * 512:(ti + 1) * 512]
                        scv = qsl if iq == 1 else ksl
                        vv.scalar_tensor_tensor(out=tq[:], in0=qk, scalar=1.0,
                                                in1=invb[:], op0=ALU.mult,
                                                op1=ALU.mult)
                        vv.tensor_scalar(out=tq[:], in0=tq[:], scalar1=scv[:, 0:1],
                                         scalar2=None, op0=ALU.mult)
                        A = rtb[:, 0:512] if iq == 1 else rtb[:, 1024:1536]
                        Bs = rtb[:, 512:1024] if iq == 1 else rtb[:, 1536:2048]
                        vv.tensor_tensor(out=r1[:], in0=tq[:], in1=A, op=ALU.mult)
                        for half in range(2):
                            b0 = half * 64
                            vv.tensor_copy(tperm[b0:b0 + 32, :], tq[b0 + 32:b0 + 64, :])
                            vv.tensor_copy(tperm[b0 + 32:b0 + 64, :], tq[b0:b0 + 32, :])
                        vv.tensor_tensor(out=tperm[:], in0=tperm[:], in1=Bs,
                                         op=ALU.mult)
                        dst = QH if iq == 1 else KH
                        dw = 2048 if iq == 1 else 2560
                        dcol = (t4 * (dw) + (m - (1 - kw0)) * 512) if iq == 1 \
                            else (t4 * dw + (kw0 + m) * 512)
                        use_evs[ti] = p.inc(vv.tensor_tensor(
                            out=dst[:, dcol:dcol + 512], in0=r1[:], in1=tperm[:],
                            op=ALU.add), "V")
                    p.V(v1)
            prevm["stg"] = p.now("V")
            prevm["use"] = [ntev[11][0]] * 8
        self.ev_qkv_done = [p.now("V"), p.now("S"), p.now("T")]

    def ffn_attn(self, lidx, ph, wins, QH, KH, VT, stream, wo_bf, AO,
                 ebuf, recipb, first_kv_mask):
        p, nc = self, self.nc
        ph = self.es_attn = contextlib.ExitStack()
        self._attn_stack = ph
        ps_sc = self.pst(ph, "ps_sc", [128, 2 * 512])
        ps_s0 = self.pst(ph, "ps_s0", [1, 512])
        ps_s1 = self.pst(ph, "ps_s1", [1, 512])
        ps_av = self.pst(ph, "ps_av", [128, 1024])
        srow0 = self.sbt(ph, "srow0", [1, 512], F32)
        srow1 = self.sbt(ph, "srow1", [1, 512], F32)

        for w in wins:
            for hp in range(4):
                mev = {}
                for blk in range(2):
                    kw = w + blk
                    for kt in range(4):
                        bi = blk * 4 + kt
                        scev = [None]
                        def t(tt, w=w, hp=hp, blk=blk, kt=kt, bi=bi, scev=scev):
                            if bi < 1:
                                p.wait(tt, *self.ev_qkv_done)
                                p.wait(tt, p.now("G"), p.now("V"))
                            if bi >= 1:
                                p.wait(tt, mev.get(bi - 1, (0, 0)))
                            kcol = hp * 2560 + (w + blk) * 512 + kt * 128
                            for hh in range(2):
                                tt.matmul(
                                    ps_sc[:, hh * 512:(hh + 1) * 512],
                                    KH[hh * 64:(hh + 1) * 64, kcol:kcol + 128],
                                    QH[hh * 64:(hh + 1) * 64,
                                       hp * 2048 + w * 512:hp * 2048 + w * 512 + 512],
                                    start=True, stop=True,
                                    tile_position=(hh * 64, 0))
                            scev[0] = p.incl(tt, 'T')
                        p.T(t)
                        eev = [None]
                        def s_(ss, hp=hp, blk=blk, kt=kt, bi=bi, scev=scev, eev=eev):
                            p.wait(ss, scev[0])
                            for hh in range(2):
                                ss.activation(
                                    out=ebuf[:, (hh * 8 + bi) * 512:
                                             (hh * 8 + bi) * 512 + 512],
                                    in_=ps_sc[:, hh * 512:(hh + 1) * 512],
                                    func=AF.Exp, bias=self.cc(self.C_M8))
                            eev[0] = p.incl(ss, 'S')
                        p.S(s_)
                        def g(gg, w=w, blk=blk, kt=kt, bi=bi, eev=eev):
                            p.wait(gg, eev[0])
                            for hh in range(2):
                                esl = ebuf[:, (hh * 8 + bi) * 512:
                                           (hh * 8 + bi) * 512 + 512]
                                if blk == 1:
                                    gg.affine_select(
                                        out=esl, in_=esl, pattern=[[1, 512]],
                                        compare_op=ALU.is_ge, fill=0.0,
                                        base=-128 * kt, channel_multiplier=-1)
                                else:
                                    gg.affine_select(
                                        out=esl, in_=esl, pattern=[[-1, 512]],
                                        compare_op=ALU.is_ge, fill=0.0,
                                        base=128 * kt, channel_multiplier=1)
                            mev[bi] = p.incl(gg, 'G')
                        p.G(g)
                if w == 0:
                    def vkm(vv, hp=hp):
                        p.wait(vv, mev[3])
                        for hh in range(2):
                            for kt in range(4):
                                bi = kt
                                esl = ebuf[:, (hh * 8 + bi) * 512:
                                           (hh * 8 + bi) * 512 + 512]
                                vv.tensor_scalar_mul(esl, esl, first_kv_mask)
                        p.incl(vv, 'V')
                    p.V(vkm)
                    ev_km = p.now("V")
                else:
                    ev_km = (0, 0)

                smev = [None]
                def t2(tt, hp=hp, smev=smev):
                    p.wait(tt, mev[7], ev_km, p.now("S"))
                    for hh, psn in ((0, ps_s0), (1, ps_s1)):
                        for bi in range(8):
                            tt.matmul(psn[:], self.ones_bf[:],
                                      ebuf[:, (hh * 8 + bi) * 512:
                                           (hh * 8 + bi) * 512 + 512],
                                      start=(bi == 0), stop=(bi == 7))
                    smev[0] = p.incl(tt, 'T')
                p.T(t2)
                srev = [None]
                def s2(ss, smev=smev, srev=srev):
                    p.wait(ss, smev[0])
                    ss.copy(srow0[0:1, :], ps_s0[:])
                    srev[0] = p.inc(ss.copy(srow1[0:1, :], ps_s1[:]), "S")
                p.S(s2)
                vrev = [None]
                def v2(vv, srev=srev, vrev=vrev):
                    p.wait(vv, srev[0])
                    vv.reciprocal(srow0[:], srow0[:])
                    vrev[0] = p.inc(vv.reciprocal(srow1[:], srow1[:]), "V")
                p.V(v2)
                bev = [None]
                def g2(gg, vrev=vrev, bev=bev):
                    p.wait(gg, vrev[0], p.now("V"))
                    gg.dma_start(out=self.sp_rows[12:13, 0:512], in_=srow0[:]
                                 ).then_inc(p.sem["SP"], 16)
                    gg.dma_start(out=self.sp_rows[13:14, 0:512], in_=srow1[:]
                                 ).then_inc(p.sem["SP"], 16)
                    p.cnt["SP"] += 32
                    gg.wait_ge(p.sem["SP"], p.cnt["SP"])
                    src = self.sp_rows[12:14, 0:512].rearrange(
                        "a (b w) -> a b w", b=1).broadcast_to([2, 64, 512])
                    bev[0] = p.inc(gg.dma_start(out=recipb[:], in_=src), "SP")
                p.G(g2)

                avev = [None]
                def t3(tt, w=w, hp=hp, avev=avev):
                    p.wait(tt, mev[7], ev_km, p.now("V"))
                    for bi in range(8):
                        blk, kt = bi // 4, bi % 4
                        ktg = (w + blk) * 4 + kt
                        for hh in range(2):
                            tt.matmul(
                                ps_av[hh * 64:(hh + 1) * 64,
                                      hh * 512:hh * 512 + 512],
                                VT[:, ktg * 512 + hp * 128 + hh * 64:
                                   ktg * 512 + hp * 128 + hh * 64 + 64],
                                ebuf[:, (hh * 8 + bi) * 512:
                                     (hh * 8 + bi) * 512 + 512],
                                start=(bi == 0), stop=(bi == 7),
                                tile_position=(0, hh * 64))
                    avev[0] = p.incl(tt, 'T')
                p.T(t3)
                def v3(vv, w=w, hp=hp, avev=avev, bev=bev):
                    p.wait(vv, avev[0], bev[0])
                    vv.scalar_tensor_tensor(
                        out=AO[0:64, hp * 2048 + w * 512:hp * 2048 + w * 512 + 512],
                        in0=ps_av[0:64, 0:512], scalar=1.0, in1=recipb[0:64, :],
                        op0=ALU.mult, op1=ALU.mult)
                    p.inc(vv.scalar_tensor_tensor(
                        out=AO[64:128, hp * 2048 + w * 512:hp * 2048 + w * 512 + 512],
                        in0=ps_av[64:128, 512:1024], scalar=1.0,
                        in1=recipb[64:128, :],
                        op0=ALU.mult, op1=ALU.mult), "V")
                p.V(v3)
            for ont in range(8):
                oev = [None]
                def t4(tt, w=w, ont=ont, oev=oev):
                    p.wait(tt, p.now("V"))
                    if ont >= 2:
                        p.wait(tt, p.now("V"))
                    bk = (ont % 2) * 512
                    for k4 in range(4):
                        tt.matmul(ps_sc[:, bk:bk + 512],
                                  wo_bf[:, k4 * 1024 + ont * 128:
                                        k4 * 1024 + ont * 128 + 128],
                                  AO[:, k4 * 2048 + w * 512:k4 * 2048 + w * 512 + 512],
                                  start=(k4 == 0), stop=(k4 == 3))
                    oev[0] = p.incl(tt, 'T')
                p.T(t4)
                def v4(vv, w=w, ont=ont, oev=oev):
                    p.wait(vv, oev[0])
                    sl = slice(ont * SEG + 1 + w * 512, ont * SEG + 1 + w * 512 + 512)
                    p.inc(vv.scalar_tensor_tensor(
                        out=stream[:, sl], in0=ps_sc[:, (ont % 2) * 512:
                                                     (ont % 2) * 512 + 512],
                        scalar=1.0, in1=stream[:, sl],
                        op0=ALU.mult, op1=ALU.add), "V")
                p.V(v4)
        self.barrier()
        ph.close()

    def mlp_ln(self, lidx, ph, stream_dram, LNIN, mcs):
        p = self
        stf = self.sbt(ph, "stf", [128, 8 * 513], F32)
        sbfm = self.sbt(ph, "sbfm", [128, 512], BF16)
        sqm = self.sbt(ph, "sqm", [128, 512], BF16)
        rows0 = self.sbt(ph, "rows_m0", [1, 512], F32)
        rows1 = self.sbt(ph, "rows_m1", [1, 512], F32)
        rows2 = self.sbt(ph, "rows_m2", [1, 512], F32)
        mub = self.sbt(ph, "mub_m", [128, 512], F32)
        rb = self.sbt(ph, "rb_m", [128, 512], F32)
        tq = self.sbt(ph, "tq_m", [128, 512], F32)
        ps_st = self.pst(ph, "ps_stm", [1, 512])
        ps_st2 = self.pst(ph, "ps_stm2", [1, 512])

        for mc in mcs:
            ldev = [None]
            def y(yq, mc=mc, ldev=ldev):
                p.wait(yq, p.now("T"), p.now("V"))
                for rt in range(8):
                    yq.dma_start(out=stf[:, rt * 513:(rt + 1) * 513],
                                 in_=stream_dram[rt * 128:(rt + 1) * 128,
                                                 mc * 512:mc * 512 + 513]
                                 ).then_inc(p.sem["LD"], 16)
                p.cnt["LD"] += 128
                ldev[0] = p.now("LD")
            p.Y(y)
            stev = [None]
            for pi, (dstrow, which) in enumerate(((0, "v"), (1, "s"))):
                for rt in range(8):
                    off = 1 if rt < 4 else 0
                    src = stf[:, rt * 513 + off: rt * 513 + off + 512]
                    cev = [None]
                    if which == "v":
                        def v(vv, src=src, ldev=ldev, cev=cev):
                            p.wait(vv, ldev[0], p.now("T"))
                            cev[0] = p.inc(vv.tensor_copy(sbfm[:], src), "V")
                        p.V(v)
                        buf = sbfm
                    else:
                        def s_(ss, src=src, ldev=ldev, cev=cev):
                            p.wait(ss, ldev[0], p.now("T"))
                            cev[0] = p.inc(ss.activation(out=sqm[:], in_=src,
                                                         func=AF.Square), "S")
                        p.S(s_)
                        buf = sqm
                    def t(tt, rt=rt, dstrow=dstrow, buf=buf, cev=cev):
                        p.wait(tt, cev[0])
                        pst = ps_st if dstrow == 0 else ps_st2
                        p.inc(tt.matmul(pst[0:1, :],
                                        self.ones_bf[:], buf[:],
                                        start=(rt == 0), stop=(rt == 7)), "T")
                    p.T(t)
            def sev(ss):
                p.wait(ss, p.now("T"))
                ss.copy(rows0[0:1, :], ps_st[0:1, :])
                p.inc(ss.copy(rows1[0:1, :], ps_st2[0:1, :]), "S")
            p.S(sev)
            def vr(vv):
                p.wait(vv, p.now("S"))
                vv.tensor_scalar_mul(rows0[0:1, :], rows0[0:1, :], 1.0 / 1024.0)
                vv.tensor_scalar_mul(rows1[0:1, :], rows1[0:1, :], 1.0 / 1024.0)
                vv.tensor_tensor(out=rows2[0:1, :], in0=rows0[0:1, :],
                                 in1=rows0[0:1, :], op=ALU.mult)
                p.inc(vv.tensor_tensor(out=rows1[0:1, :], in0=rows1[0:1, :],
                                       in1=rows2[0:1, :], op=ALU.subtract), "V")
            p.V(vr)
            def sr(ss):
                p.wait(ss, p.now("V"))
                ss.activation(out=rows1[0:1, :], in_=rows1[0:1, :], func=AF.Ln,
                              bias=self.cc(self.C_EPS)[0:1, :])
                p.inc(ss.activation(out=rows1[0:1, :], in_=rows1[0:1, :],
                                    func=AF.Exp, scale=self.cc(self.C_NH)[0:1, :]), "S")
            p.S(sr)
            bev = [None]
            def g(gg, bev=bev):
                p.wait(gg, p.now("S"), p.now("V"))
                gg.dma_start(out=self.sp_rows[14:15, 0:512], in_=rows0[0:1, :]
                             ).then_inc(p.sem["SP"], 16)
                gg.dma_start(out=self.sp_rows[15:16, 0:512], in_=rows1[0:1, :]
                             ).then_inc(p.sem["SP"], 16)
                p.cnt["SP"] += 32
                gg.wait_ge(p.sem["SP"], p.cnt["SP"])
                gg.dma_start(out=mub[:], in_=self.sp_rows[
                    14:15, 0:512].broadcast_to([128, 512])
                ).then_inc(p.sem["SP"], 16)
                gg.dma_start(out=rb[:], in_=self.sp_rows[
                    15:16, 0:512].broadcast_to([128, 512])
                ).then_inc(p.sem["SP"], 16)
                p.cnt["SP"] += 32
                bev[0] = p.now("SP")
            p.G(g)
            def va(vv, mc=mc, ldev=ldev, bev=bev):
                p.wait(vv, bev[0])
                for rt in range(8):
                    off = 1 if rt < 4 else 0
                    src = stf[:, rt * 513 + off: rt * 513 + off + 512]
                    vv.tensor_tensor(out=tq[:], in0=src, in1=mub[:],
                                     op=ALU.subtract)
                    vv.tensor_tensor(out=tq[:], in0=tq[:], in1=rb[:], op=ALU.mult)
                    vv.tensor_scalar(
                        out=LNIN[:, rt * 2048 + mc * 512:rt * 2048 + mc * 512 + 512],
                        in0=tq[:], scalar1=self.lng_sb[:, rt:rt + 1],
                        scalar2=self.lnb2_sb[:, rt:rt + 1], op0=ALU.mult,
                        op1=ALU.add)
                p.incl(vv, 'V')
            p.V(va)

    def mlp_a(self, lidx, ph, LNIN, gated, mcs, bufs):
        p = self
        w1s, w1b, ggf, tmpa, ps_a, ps_g, b1sb = bufs
        prevk = {"ev": ("V", 0)}
        for k in range(22):
            ldev = [None]
            def y(yq, k=k, ldev=ldev):
                p.wait(yq, prevk["ev"])
                for rt in range(8):
                    yq.dma_start(
                        out=w1s[:, rt * 256:rt * 256 + 128],
                        in_=self.w1p[lidx, rt * 128:(rt + 1) * 128,
                                     k * 128:(k + 1) * 128]).then_inc(p.sem["LD"], 16)
                    yq.dma_start(
                        out=w1s[:, rt * 256 + 128:rt * 256 + 256],
                        in_=self.w1p[lidx, rt * 128:(rt + 1) * 128,
                                     HIDP + k * 128:HIDP + (k + 1) * 128]
                    ).then_inc(p.sem["LD"], 16)
                p.cnt["LD"] += 256
                ldev[0] = p.now("LD")
            p.Y(y)
            cev = [None]
            def v(vv, ldev=ldev, cev=cev):
                p.wait(vv, ldev[0])
                cev[0] = p.inc(vv.tensor_copy(w1b[:], w1s[:]), "V")
            p.V(v)
            mmev = [None]
            def t(tt, k=k, mcs=mcs, cev=cev, mmev=mmev):
                p.wait(tt, cev[0], p.now("V"))
                for mi, mc in enumerate(mcs):
                    for ag in range(2):
                        ps = ps_a if ag == 0 else ps_g
                        for kt in range(8):
                            tt.matmul(
                                ps[:, mi * 512:(mi + 1) * 512],
                                w1b[:, kt * 256 + ag * 128:kt * 256 + ag * 128 + 128],
                                LNIN[:, kt * 2048 + mc * 512:kt * 2048 + mc * 512 + 512],
                                start=(kt == 0), stop=(kt == 7))
                mmev[0] = p.incl(tt, 'T')
            p.T(t)
            gev = [None]
            def s_(ss, k=k, mcs=mcs, mmev=mmev, gev=gev):
                p.wait(ss, mmev[0])
                for mi, mc in enumerate(mcs):
                    ss.activation(out=ggf[:, mi * 512:(mi + 1) * 512],
                                  in_=ps_g[:, mi * 512:(mi + 1) * 512],
                                  func=AF.Gelu, bias=b1sb[:, 22 + k:23 + k])
                gev[0] = p.incl(ss, 'S')
            p.S(s_)
            def v2(vv, k=k, mcs=mcs, mmev=mmev, gev=gev):
                p.wait(vv, gev[0])
                for mi, mc in enumerate(mcs):
                    vv.tensor_scalar(out=tmpa[:], in0=ps_a[:, mi * 512:(mi + 1) * 512],
                                     scalar1=b1sb[:, k:k + 1], scalar2=None,
                                     op0=ALU.add)
                    vv.tensor_tensor(
                        out=gated[:, k * 2048 + mc * 512:k * 2048 + mc * 512 + 512],
                        in0=tmpa[:], in1=ggf[:, mi * 512:(mi + 1) * 512],
                        op=ALU.mult)
                prevk["ev"] = p.incl(vv, 'V')
            p.V(v2)

    def mlp_b(self, lidx, ph, gated, mode):
        p = self
        w2s = [self.sbt(ph, f"w2s{i}", [128, 1024], F32) for i in range(2)]
        w2b = self.sbt(ph, "w2b", [128, 22 * 1024], BF16)
        t1 = self.sbt(ph, "t1_m", [128, 512], F32)
        t2 = self.sbt(ph, "t2_m", [128, 512], F32)
        rl1 = [self.sbt(ph, f"rl1_{i}", [128, 512], F32) for i in range(2)]
        rl2 = [self.sbt(ph, f"rl2_{i}", [128, 512], F32) for i in range(2)]
        rl3 = [self.sbt(ph, f"rl3_{i}", [128, 512], F32) for i in range(2)]
        rl4 = [self.sbt(ph, f"rl4_{i}", [128, 512], F32) for i in range(2)]
        ps_w2 = self.pst(ph, "ps_w2", [128, 1024])

        cast_ev = [None] * 22
        for kt in range(22):
            ldev = [None]
            def y(yq, kt=kt, ldev=ldev):
                if kt >= 2:
                    p.wait(yq, cast_ev[kt - 2])
                ldev[0] = p.inc(yq.dma_start(
                    out=w2s[kt % 2][:],
                    in_=self.w2p[lidx, kt * 128:(kt + 1) * 128, :]), "LD")
            p.Y(y)
            def v(vv, kt=kt, ldev=ldev):
                p.wait(vv, ldev[0])
                cast_ev[kt] = p.inc(vv.tensor_copy(
                    w2b[:, kt * 1024:(kt + 1) * 1024], w2s[kt % 2][:]), "V")
            p.V(v)
        ev_w2 = cast_ev[21]

        prevev = [("SP", 0), ("SP", 0)]
        for ont in range(8):
            for mc in range(4):
                i = (ont * 4 + mc) % 2
                ldev = [None]
                def y(yq, ont=ont, mc=mc, i=i, ldev=ldev):
                    p.wait(yq, prevev[i])
                    rsl = slice(ont * 128, (ont + 1) * 128)
                    csl = slice(1 + mc * 512, 1 + mc * 512 + 512)
                    yq.dma_start(out=rl1[i][:], in_=self.sp_s1[rsl, csl]
                                 ).then_inc(p.sem["LD"], 16)
                    p.cnt["LD"] += 16
                    if mode == "l2":
                        yq.dma_start(out=rl2[i][:],
                                     in_=self.x_fm[rsl, OWN0 + mc * 512:
                                                   OWN0 + mc * 512 + 512]
                                     ).then_inc(p.sem["LD"], 16)
                        yq.dma_start(out=rl3[i][:],
                                     in_=self.sp_xn32[rsl, mc * 512:mc * 512 + 512]
                                     ).then_inc(p.sem["LD"], 16)
                        yq.dma_start(out=rl4[i][:],
                                     in_=self.acc_m[rsl, mc * 512:mc * 512 + 512]
                                     ).then_inc(p.sem["LD"], 16)
                        p.cnt["LD"] += 48
                    ldev[0] = p.now("LD")
                p.Y(y)
                mmev = [None]
                def t(tt, ont=ont, mc=mc, i=i, mmev=mmev):
                    if ont == 0 and mc < 2:
                        p.wait(tt, ev_w2, p.now("V"))
                    p.wait(tt, prevev[i])
                    bk = i * 512
                    for kt in range(22):
                        tt.matmul(ps_w2[:, bk:bk + 512],
                                  w2b[:, kt * 1024 + ont * 128:
                                      kt * 1024 + ont * 128 + 128],
                                  gated[:, kt * 2048 + mc * 512:
                                        kt * 2048 + mc * 512 + 512],
                                  start=(kt == 0), stop=(kt == 21))
                    mmev[0] = p.incl(tt, 'T')
                p.T(t)
                vev = [None]
                def v(vv, ont=ont, mc=mc, i=i, ldev=ldev, mmev=mmev, vev=vev):
                    p.wait(vv, ldev[0], mmev[0])
                    bk = i * 512
                    vv.tensor_scalar(out=t1[:], in0=ps_w2[:, bk:bk + 512],
                                     scalar1=self.b2_sb[:, ont:ont + 1],
                                     scalar2=None, op0=ALU.add)
                    vv.tensor_tensor(out=t1[:], in0=t1[:], in1=rl1[i][:],
                                     op=ALU.add)
                    if mode == "l2":
                        vv.tensor_tensor(out=t2[:], in0=rl2[i][:], in1=rl3[i][:],
                                         op=ALU.add)
                        vv.tensor_tensor(out=t2[:], in0=t2[:], in1=rl4[i][:],
                                         op=ALU.add)
                        vv.tensor_tensor(out=t1[:], in0=t1[:], in1=t2[:],
                                         op=ALU.add)
                    vev[0] = p.inc(vv.tensor_copy(
                        (rl1[i][:] if True else t1[:]), t1[:]), "V")
                p.V(v)
                def g(gg, ont=ont, mc=mc, i=i, vev=vev):
                    p.wait(gg, vev[0])
                    if mode == "l1":
                        e = p.inc(gg.dma_start(
                            out=self.sp_s2[ont * 128:(ont + 1) * 128,
                                           1 + mc * 512:1 + mc * 512 + 512],
                            in_=rl1[i][:]), "SP")
                    else:
                        e = p.inc(gg.dma_start(
                            out=self.y_fm[ont * 128:(ont + 1) * 128,
                                          mc * 512:mc * 512 + 512],
                            in_=rl1[i][:]), "SP")
                    prevev[i] = e
                p.G(g)
        self.ev_mlpb_out = p.now("SP")

    def p2ab(self):
        p = self
        self.barrier()
        self.ph_h2.close()
        ph = contextlib.ExitStack()
        with ph:
            QH = self.sbt(ph, "QH", [128, 4 * 2048], BF16)
            KH = self.sbt(ph, "KH", [128, 4 * 2560], BF16)
            VT = self.sbt(ph, "VT", [128, 20 * 512], BF16)
            ph_q = contextlib.ExitStack()
            cut = os.environ.get("KB_P2AB_CUT", "")
            with ph_q:
                self.ffn_qkv(0, ph_q, QH, KH, VT, self.sp_xnbf,
                             lambda m: KV0 - 1 + m * 512, 5, 0,
                             lambda m: m * 512)
                p.barrier()
            if cut == "qkv":
                return

            stream = self.sbt(ph, "stream1", [128, 8 * SEG], F32)
            AO = self.sbt(ph, "AO", [128, 4 * 2048], BF16)
            ebuf = self.sbt(ph, "ebuf", [128, 16 * 512], BF16)
            recipb = self.sbt(ph, "recipb", [128, 512], F32)
            wo_bf = self.sbt(ph, "wo_bf", [128, 4 * 1024], BF16)
            wos = self.sbt(ph, "wos", [128, 1024], F32)

            for kt in range(4):
                ldev = [None]
                def y(yq, kt=kt, ldev=ldev):
                    p.wait(yq, p.now("V"))
                    ldev[0] = p.inc(yq.dma_start(
                        out=wos[:], in_=self.wo2[0, kt * 128:(kt + 1) * 128, :]), "LD")
                p.Y(y)
                def v(vv, kt=kt, ldev=ldev):
                    p.wait(vv, ldev[0])
                    p.inc(vv.tensor_copy(wo_bf[:, kt * 1024:(kt + 1) * 1024],
                                         wos[:]), "V")
                p.V(v)
            def yst(yq):
                for rt in range(8):
                    yq.dma_start(out=stream[:, rt * SEG + 1:rt * SEG + 1 + TOK],
                                 in_=self.sp_xn32[rt * 128:(rt + 1) * 128, :]
                                 ).then_inc(p.sem["LD"], 16)
                p.cnt["LD"] += 128
            p.Y(yst)
            def vst(vv):
                p.wait(vv, p.now("LD"))
                p.incl(vv, 'V')
            p.V(vst)

            self.ffn_attn(0, ph, [0, 1, 2, 3], QH, KH, VT, stream, wo_bf, AO,
                          ebuf, recipb, self.mf)
            if cut == "attn":
                p.barrier()
                return
            self.x1_issue(stream)
            def gsp(gg):
                p.wait(gg, p.now("V"))
                for rt in range(8):
                    p.inc(gg.dma_start(
                        out=self.sp_s1[rt * 128:(rt + 1) * 128, 1:1 + TOK],
                        in_=stream[:, rt * SEG + 1:rt * SEG + 1 + TOK]), "SP")
            p.G(gsp)
            self.dump("stream1", stream, 8, TOK, cols0=1, src_width=SEG)
            p.barrier()

    def p2c(self):
        p = self
        ph = contextlib.ExitStack()
        with ph:
            self.b1_sb = self.sbt(ph, "b1_sb", [128, 44], F32)
            self.b2_sb = self.sbt(ph, "b2_sb", [128, 8], F32)
            self.lng_sb = self.sbt(ph, "lng_sb", [128, 8], F32)
            self.lnb2_sb = self.sbt(ph, "lnb2_sb", [128, 8], F32)
            def y(yq):
                yq.dma_start(out=self.b1_sb[:], in_=self.b1p[0, :, :]).then_inc(
                    p.sem["LD"], 16)
                yq.dma_start(out=self.b2_sb[:], in_=self.b2v[0, :, :]).then_inc(
                    p.sem["LD"], 16)
                yq.dma_start(out=self.lng_sb[:], in_=self.lng[0, :, :]).then_inc(
                    p.sem["LD"], 16)
                yq.dma_start(out=self.lnb2_sb[:], in_=self.lnb2[0, :, :]).then_inc(
                    p.sem["LD"], 16)
                p.cnt["LD"] += 64
            p.Y(y)
            gated = self.sbt(ph, "gated", [128, 22 * 2048], BF16)
            ph_a = contextlib.ExitStack()
            with ph_a:
                LNIN = self.sbt(ph_a, "LNIN", [128, 8 * 2048], BF16)
                ph_ln = contextlib.ExitStack()
                with ph_ln:
                    self.mlp_ln(0, ph_ln, self.sp_s1, LNIN, [0, 1, 2, 3])
                    self.barrier()
                w1s = self.sbt(ph_a, "w1s", [128, 8 * 256], F32)
                w1b = self.sbt(ph_a, "w1b", [128, 8 * 256], BF16)
                ggf = self.sbt(ph_a, "ggf", [128, 4 * 512], F32)
                tmpa = self.sbt(ph_a, "tmpa", [128, 512], F32)
                ps_a = self.pst(ph_a, "ps_a", [128, 2048])
                ps_g = self.pst(ph_a, "ps_g", [128, 2048])
                self.mlp_a(0, ph_a, LNIN, gated, [0, 1, 2, 3],
                           (w1s, w1b, ggf, tmpa, ps_a, ps_g, self.b1_sb))
                p.barrier()
            self.mlp_b(0, ph, gated, "l1")
            p.barrier()

    def x2a(self):
        p = self
        ph = contextlib.ExitStack()
        with ph:
            hb = self.sbt(ph, "hb2a", [128, 8], F32)
            h0 = self.sbt(ph, "h02a", [128, 8], F32)
            h1 = self.sbt(ph, "h12a", [128, 8], F32)
            def g(gg):
                p.wait(gg, self.ev_mlpb_out)
                gg.dma_start(
                    out=hb[:],
                    in_=self.sp_s2[:, 2048:2049].rearrange(
                        "(a q) w -> q (a w)", q=128),
                    allow_slow_non_contiguous=True).then_inc(p.sem["SP"], 16)
                p.cnt["SP"] += 16
                gg.wait_ge(p.sem["SP"], p.cnt["SP"])
                gg.dma_start(out=self.cc2a_in[:, :], in_=hb[:]).then_inc(
                    p.sem["SP"], 16)
                p.cnt["SP"] += 16
                gg.wait_ge(p.sem["SP"], p.cnt["SP"])
                p.inc(gg.collective_compute(
                    "AllGather", ALU.bypass, replica_groups=PAIRS,
                    ins=[self.cc2a_in[:, :]], outs=[self.cc2a_out[:, :]]), "G")
                gg.wait_ge(p.sem["G"], p.cnt["G"])
                gg.dma_start(out=h0[:], in_=self.cc2a_out[0:128, :]).then_inc(
                    p.sem["SP"], 16)
                gg.dma_start(out=h1[:], in_=self.cc2a_out[128:256, :]).then_inc(
                    p.sem["SP"], 16)
                p.cnt["SP"] += 32
            p.G(g)
            ev = p.now("SP")
            def v(vv):
                p.wait(vv, ev)
                F = self.flags_sb
                vv.tensor_scalar(out=h0[:], in0=h0[:], scalar1=F[:, 2:3],
                                 scalar2=None, op0=ALU.mult)
                p.inc(vv.scalar_tensor_tensor(
                    out=h0[:], in0=h1[:], scalar=F[:, 3:4], in1=h0[:],
                    op0=ALU.mult, op1=ALU.add), "V")
            p.V(v)
            def g2(gg):
                p.wait(gg, p.now("V"))
                for rt in range(8):
                    p.inc(gg.dma_start(
                        out=self.sp_s2[rt * 128:(rt + 1) * 128, 0:1],
                        in_=h0[:, rt:rt + 1],
                        allow_slow_non_contiguous=True), "SP")
            p.G(g2)
            p.barrier()

    def p4(self):
        p = self
        ph = contextlib.ExitStack()
        with ph:
            QH = self.sbt(ph, "QH2", [128, 4 * 2048], BF16)
            KH = self.sbt(ph, "KH2", [128, 4 * 2560], BF16)
            VT = self.sbt(ph, "VT2", [128, 20 * 512], BF16)
            ph_q = contextlib.ExitStack()
            with ph_q:
                self.ffn_qkv(1, ph_q, QH, KH, VT, self.sp_s2,
                             lambda m: m * 512, 4, 1,
                             lambda m: 512 + m * 512)
                p.barrier()

            stream = self.sbt(ph, "stream2", [128, 8 * SEG], F32)
            AO = self.sbt(ph, "AO2", [128, 4 * 2048], BF16)
            ebuf = self.sbt(ph, "ebuf2", [128, 16 * 512], BF16)
            recipb = self.sbt(ph, "recipb2", [128, 512], F32)
            wo_bf = self.sbt(ph, "wo_bf2", [128, 4 * 1024], BF16)
            wos = self.sbt(ph, "wos2", [128, 1024], F32)
            for kt in range(4):
                ldev = [None]
                def y(yq, kt=kt, ldev=ldev):
                    p.wait(yq, p.now("V"))
                    ldev[0] = p.inc(yq.dma_start(
                        out=wos[:], in_=self.wo2[1, kt * 128:(kt + 1) * 128, :]), "LD")
                p.Y(y)
                def v(vv, kt=kt, ldev=ldev):
                    p.wait(vv, ldev[0])
                    p.inc(vv.tensor_copy(wo_bf[:, kt * 1024:(kt + 1) * 1024],
                                         wos[:]), "V")
                p.V(v)
            def yst(yq):
                for rt in range(8):
                    yq.dma_start(out=stream[:, rt * SEG:rt * SEG + SEG],
                                 in_=self.sp_s2[rt * 128:(rt + 1) * 128, :]
                                 ).then_inc(p.sem["LD"], 16)
                p.cnt["LD"] += 128
            p.Y(yst)
            def vst(vv):
                p.wait(vv, p.now("LD"))
                p.incl(vv, 'V')
            p.V(vst)

            def vz(vv):
                for t4 in range(4):
                    vv.memset(KH[:, t4 * 2560:t4 * 2560 + 512], 0.0)
                for ktg in range(4):
                    vv.memset(VT[:, ktg * 512:(ktg + 1) * 512], 0.0)
                p.incl(vv, 'V')
            p.V(vz)

            self.ffn_attn(1, ph, [1, 2, 3], QH, KH, VT, stream, wo_bf, AO,
                          ebuf, recipb, self.mf)

            def gx(gg):
                p.wait(gg, p.now("V"), *self.ev_qkv_done)
                for t4 in range(4):
                    gg.dma_start(
                        out=self.cc2b_in[:, t4 * 512:(t4 + 1) * 512],
                        in_=KH[:, t4 * 2560 + 2048:t4 * 2560 + 2560]).then_inc(
                        p.sem["SP"], 16)
                for kt in range(4):
                    gg.dma_start(
                        out=self.cc2b_in[:, 2048 + kt * 512:2048 + (kt + 1) * 512],
                        in_=VT[:, (16 + kt) * 512:(17 + kt) * 512]).then_inc(
                        p.sem["SP"], 16)
                gg.dma_start(
                    out=self.cc2b_in[:, 4096:4104],
                    in_=stream[:].rearrange("p (a b) -> p a b", b=SEG)
                    [:, :, SEG - 1:SEG],
                    allow_slow_non_contiguous=True).then_inc(p.sem["SP"], 16)
                p.cnt["SP"] += 144
                gg.wait_ge(p.sem["SP"], p.cnt["SP"])
                p.inc(gg.collective_compute(
                    "AllGather", ALU.bypass, replica_groups=PAIRS,
                    ins=[self.cc2b_in[:, :]], outs=[self.cc2b_out[:, :]]), "G")
            p.G(gx)
            ev_x2b = p.now("G")

            t0 = self.sbt(ph, "x2t0", [128, 4096], BF16)
            t1 = self.sbt(ph, "x2t1", [128, 4096], BF16)
            tm0 = self.sbt(ph, "x2m0", [128, 8], F32)
            tm1 = self.sbt(ph, "x2m1", [128, 8], F32)
            ev = [None]
            def gc(gg):
                p.wait(gg, ev_x2b)
                gg.dma_start(out=t0[:], in_=self.cc2b_out[0:128, 0:4096]
                             ).then_inc(p.sem["SP"], 16)
                gg.dma_start(out=t1[:], in_=self.cc2b_out[128:256, 0:4096]
                             ).then_inc(p.sem["SP"], 16)
                gg.dma_start(out=tm0[:], in_=self.cc2b_out[0:128, 4096:4104]
                             ).then_inc(p.sem["SP"], 16)
                gg.dma_start(out=tm1[:], in_=self.cc2b_out[128:256, 4096:4104]
                             ).then_inc(p.sem["SP"], 16)
                p.cnt["SP"] += 64
                ev[0] = p.now("SP")
            p.G(gc)
            def vc(vv):
                p.wait(vv, ev[0])
                F = self.flags_sb
                vv.tensor_scalar(out=t0[:], in0=t0[:], scalar1=F[:, 2:3],
                                 scalar2=None, op0=ALU.mult)
                vv.scalar_tensor_tensor(out=t0[:], in0=t1[:], scalar=F[:, 3:4],
                                        in1=t0[:], op0=ALU.mult, op1=ALU.add)
                vv.tensor_scalar(out=tm0[:], in0=tm0[:], scalar1=F[:, 2:3],
                                 scalar2=None, op0=ALU.mult)
                vv.scalar_tensor_tensor(out=tm0[:], in0=tm1[:], scalar=F[:, 3:4],
                                        in1=tm0[:], op0=ALU.mult, op1=ALU.add)
                for t4 in range(4):
                    vv.tensor_copy(KH[:, t4 * 2560:t4 * 2560 + 512],
                                   t0[:, t4 * 512:(t4 + 1) * 512])
                for kt in range(4):
                    vv.tensor_copy(VT[:, kt * 512:(kt + 1) * 512],
                                   t0[:, 2048 + kt * 512:2048 + (kt + 1) * 512])
                p.incl(vv, 'V')
            p.V(vc)
            def gmid(gg):
                p.wait(gg, p.now("V"))
                for rt in range(8):
                    p.inc(gg.dma_start(
                        out=self.sp_s1[rt * 128:(rt + 1) * 128, 0:1],
                        in_=tm0[:, rt:rt + 1],
                        allow_slow_non_contiguous=True), "SP")
            p.G(gmid)

            self.ffn_attn(1, ph, [0], QH, KH, VT, stream, wo_bf, AO,
                          ebuf, recipb, self.mf)

            def gsp(gg):
                p.wait(gg, p.now("V"))
                for rt in range(8):
                    p.inc(gg.dma_start(
                        out=self.sp_s1[rt * 128:(rt + 1) * 128, 1:1 + TOK],
                        in_=stream[:, rt * SEG + 1:rt * SEG + 1 + TOK]), "SP")
            p.G(gsp)
            self.dump("stream2", stream, 8, TOK, cols0=1, src_width=SEG)
            p.barrier()

        ph2 = contextlib.ExitStack()
        with ph2:
            self.b1_sb = self.sbt(ph2, "b1_sb2", [128, 44], F32)
            self.b2_sb = self.sbt(ph2, "b2_sb2", [128, 8], F32)
            self.lng_sb = self.sbt(ph2, "lng_sb2", [128, 8], F32)
            self.lnb2_sb = self.sbt(ph2, "lnb2_sb2", [128, 8], F32)
            def y(yq):
                yq.dma_start(out=self.b1_sb[:], in_=self.b1p[1, :, :]).then_inc(
                    p.sem["LD"], 16)
                yq.dma_start(out=self.b2_sb[:], in_=self.b2v[1, :, :]).then_inc(
                    p.sem["LD"], 16)
                yq.dma_start(out=self.lng_sb[:], in_=self.lng[1, :, :]).then_inc(
                    p.sem["LD"], 16)
                yq.dma_start(out=self.lnb2_sb[:], in_=self.lnb2[1, :, :]).then_inc(
                    p.sem["LD"], 16)
                p.cnt["LD"] += 64
            p.Y(y)
            gated = self.sbt(ph2, "gated2", [128, 22 * 2048], BF16)
            ph_a = contextlib.ExitStack()
            with ph_a:
                LNIN = self.sbt(ph_a, "LNIN2", [128, 8 * 2048], BF16)
                ph_ln = contextlib.ExitStack()
                with ph_ln:
                    self.mlp_ln(1, ph_ln, self.sp_s1, LNIN, [0, 1, 2, 3])
                    self.barrier()
                w1s = self.sbt(ph_a, "w1s2", [128, 8 * 256], F32)
                w1b = self.sbt(ph_a, "w1b2", [128, 8 * 256], BF16)
                ggf = self.sbt(ph_a, "ggf2", [128, 4 * 512], F32)
                tmpa = self.sbt(ph_a, "tmpa2", [128, 512], F32)
                ps_a = self.pst(ph_a, "ps_a2", [128, 2048])
                ps_g = self.pst(ph_a, "ps_g2", [128, 2048])
                self.mlp_a(1, ph_a, LNIN, gated, [0, 1, 2, 3],
                           (w1s, w1b, ggf, tmpa, ps_a, ps_g, self.b1_sb))
                p.barrier()
            self.mlp_b(1, ph2, gated, "l2")
            def gf(gg):
                p.wait(gg, self.ev_mlpb_out)
                gg.nop()
            p.G(gf)
            p.barrier()


def build(dbg=(), stop_after=None):
    nc = bass.Bass("TRN2", target_bir_lowering=False,
                   detect_race_conditions=False)
    es = contextlib.ExitStack()
    with es:
        k = Kern(nc, es, dbg)
        k.declare_io()
        k.persistent()
        k.p0()
        if stop_after == "P0":
            k.barrier()
            k.ph_h2.close()
            _finish_dummy(k)
            return nc, k.dbg_out
        if "f" in P1DIRS:
            k.p1("f", rev=False)
        if "r" in P1DIRS:
            k.p1("r", rev=True)
        if stop_after == "P1":
            k.barrier()
            k.ph_h2.close()
            for d in "fr":
                k.dump_dram(f"xT_{d}", k.sp_xT[d], (TOK, D_INNER))
                k.dump_dram(f"xD_{d}", k.sp_xD[d], (D_INNER, TOK))
                k.dump_dram(f"B_{d}", k.sp_B[d], (64, TOK))
                k.dump_dram(f"C_{d}", k.sp_C[d], (64, TOK))
                k.dump_dram(f"ctl_{d}", k.sp_ctl[d], (TOK, 64))
                k.dump_dram(f"zs_{d}", k.sp_zs[d], (D_INNER, TOK))
                k.dump_dram(f"sadd_{d}", k.sp_sadd[d], (64, NCH * D_INNER))
                k.dump_dram(f"cl_{d}", k.sp_cl[d], (32, TOK))
                k.dump_dram(f"gam_{d}", k.sp_gam[d], (32, TOK))
                k.dump_dram(f"dr_{d}", k.sp_dr[d], (NCH, D_INNER))
            _finish_dummy(k)
            return nc, k.dbg_out
        k.p2ab()
        if stop_after == "P2AB":
            k.barrier()
            _finish_dummy(k)
            return nc, k.dbg_out
        ph = contextlib.ExitStack()
        with ph:
            k.x1_consume(ph)
            k.barrier()
        if stop_after == "X1":
            _finish_dummy(k)
            return nc, k.dbg_out
        if "f" in os.environ.get("KB_P3_DIRS", "fr"):
            k.p3("f", rev=False)
        if "r" in os.environ.get("KB_P3_DIRS", "fr"):
            k.p3("r", rev=True)
        if stop_after == "P3":
            k.dump_dram("accm", k.acc_m, (D, TOK))
            _finish_dummy(k)
            return nc, k.dbg_out
        k.p2c()
        if stop_after == "P2C":
            k.dump_dram("s2", k.sp_s2, (D, SEG))
            _finish_dummy(k)
            return nc, k.dbg_out
        k.x2a()
        k.p4()
    return nc, k.dbg_out


def _finish_dummy(k):
    p = k
    nc = k.nc
    with nc.sbuf_tensor("zout", [128, TOK], F32) as z:
        with nc.semaphore("stf") as stf:
            def v(vv):
                p.inc(vv.memset(z[:], 0.0), "V")
            p.V(v)
            def g(gg):
                p.wait(gg, p.now("V"), p.now("SP"))
                n = 0
                for rt in range(8):
                    gg.dma_start(out=k.y_fm[rt * 128:(rt + 1) * 128, :],
                                 in_=z[:]).then_inc(stf, 16)
                    n += 16
                gg.wait_ge(stf, n)
            p.G(g)


def _rotary_tables():
    n = L
    half = np.arange(0, HEADDIM, 2, dtype=np.float32)
    inv_freq = 1.0 / 10000.0 ** (half / HEADDIM)
    t = np.arange(-512, n, dtype=np.float32)
    freqs = np.concatenate([t[:, None] * inv_freq[None, :]] * 2, 1)
    base = (half + 0.4 * HEADDIM) / (1.4 * HEADDIM)
    power = (t - n // 2) / float(WIN // 2)
    scale = np.concatenate([base[None, :] ** power[:, None]] * 2, 1)
    cos, sin = np.cos(freqs), np.sin(freqs)
    sgn = np.where(np.arange(HEADDIM) < 32, -1.0, 1.0).astype(np.float32)[None, :]
    Aq = (cos * scale * 8.0).T
    Bq = (sin * scale * 8.0 * sgn).T
    Ak = (cos / scale).T
    Bk = (sin / scale * sgn).T
    return [np.ascontiguousarray(np.tile(x, (2, 1)), dtype=np.float32)
            for x in (Aq, Bq, Ak, Bk)]


def _tile_pp(v, nt):
    return np.ascontiguousarray(v.reshape(nt, 128).T, dtype=np.float32)


def host_prep(inputs):
    x = inputs["L0_attn_x"]
    rotf = _rotary_tables()
    common = {}
    for d, pre in (("f", "L0_attn_f_"), ("r", "L0_attn_r_")):
        ip = inputs[pre + "in_proj"]
        common[f"wz_{d}"] = np.ascontiguousarray(ip[:D_INNER].T)
        common[f"wx_{d}"] = np.ascontiguousarray(ip[D_INNER:].T)
        common[f"wout_{d}"] = np.ascontiguousarray(inputs[pre + "out_proj"].T)
        cw = inputs[pre + "conv_w"]
        common[f"convw_{d}"] = np.ascontiguousarray(
            cw.reshape(17, 128, 4).transpose(1, 0, 2).reshape(128, 68))
        common[f"convb_{d}"] = _tile_pp(inputs[pre + "conv_b"], 17)
        common[f"mv_{d}"] = np.ascontiguousarray(np.stack(
            [inputs[pre + "dt_bias"], inputs[pre + "A_log"],
             inputs[pre + "D"]], 1), dtype=np.float32)
        common[f"dexp_{d}"] = _tile_pp(np.repeat(inputs[pre + "D"], 64), 16)
        common[f"rmsw_{d}"] = _tile_pp(inputs[pre + "rms_w"], 16)
    common["lnwb"] = np.ascontiguousarray(np.concatenate(
        [_tile_pp(inputs["L0_attn_ln_w"], 8), _tile_pp(inputs["L0_attn_ln_b"], 8)], 1))
    common["wqkv"] = np.ascontiguousarray(inputs["L0_ffn_wqkv"])
    common["wo2"] = np.ascontiguousarray(inputs["L0_ffn_wo"])
    w1 = inputs["L0_ffn_w1"]
    w1p = np.zeros((2, D, 2 * HIDP), np.float32)
    w1p[:, :, :HID] = w1[:, :, :HID]
    w1p[:, :, HIDP:HIDP + HID] = w1[:, :, HID:]
    common["w1p"] = w1p
    b1 = inputs["L0_ffn_b1"]
    b1p = np.zeros((2, 2 * HIDP), np.float32)
    b1p[:, :HID] = b1[:, :HID]
    b1p[:, HIDP:HIDP + HID] = b1[:, HID:]
    common["b1p"] = np.stack([_tile_pp(b1p[i], 44) for i in range(2)])
    w2 = inputs["L0_ffn_w2"]
    w2p = np.zeros((2, HIDP, D), np.float32)
    w2p[:, :HID] = w2
    common["w2p"] = w2p
    common["b2v"] = np.stack([_tile_pp(inputs["L0_ffn_b2"][i], 8) for i in range(2)])
    common["lng"] = np.stack([_tile_pp(inputs["L0_ffn_ln_g"][i], 8) for i in range(2)])
    common["lnb2"] = np.stack([_tile_pp(inputs["L0_ffn_ln_b"][i], 8) for i in range(2)])
    common["qsc"] = np.ascontiguousarray(
        np.tile(inputs["L0_ffn_q_scale"], (1, 2))[:, :, None])
    common["ksc"] = np.ascontiguousarray(
        np.tile(inputs["L0_ffn_k_scale"], (1, 2))[:, :, None])

    in_maps = []
    for c in range(8):
        b, h = c // 2, c % 2
        start = h * TOK
        lo, hi = start - OWN0, start + TOK + 128
        xs = np.zeros((XW, D), np.float32)
        lo_c, hi_c = max(lo, 0), min(hi, L)
        xs[lo_c - lo:hi_c - lo] = x[b, lo_c:hi_c]
        m = dict(common)
        m["x_fm"] = np.ascontiguousarray(xs.T)
        c0 = start - 512 + 512
        m["rot"] = np.ascontiguousarray(
            np.stack([rt[:, c0:c0 + RW] for rt in rotf]))
        fl = np.zeros((128, 8), np.float32)
        m_front = 1.0 if h == 1 else 0.0
        m_back = 1.0 if h == 0 else 0.0
        wsel0, wsel1 = float(h), 1.0 - float(h)
        fl[:, 0] = m_front
        fl[:, 1] = m_back
        fl[:, 2] = wsel0 * m_front
        fl[:, 3] = wsel1 * m_front
        fl[:, 4] = wsel0 * m_back
        fl[:, 5] = wsel1 * m_back
        m["flags"] = fl
        in_maps.append(m)
    return in_maps


_NC_CACHE = {}


def kernel(**inputs):
    in_maps = host_prep(inputs)
    nc, _ = build()
    res = run_bass_kernel_spmd(nc, in_maps, core_ids=list(range(8)))
    out = np.empty((B, L, D), np.float32)
    for c in range(8):
        b, h = c // 2, c % 2
        out[b, h * TOK:(h + 1) * TOK] = res.results[c]["y_fm"].T
    return out


# version 02
# speedup vs baseline: 1.0198x; 1.0198x over previous
import contextlib
import os
import numpy as np

P1LVL = int(os.environ.get("KB_P1_LVL", "9"))
P1NST = int(os.environ.get("KB_P1_NST", str(8)))
P1DIRS = os.environ.get("KB_P1_DIRS", "fr")

import concourse.bass as bass
import concourse.mybir as mybir
from concourse.bass_utils import run_bass_kernel_spmd

F32 = mybir.dt.float32
BF16 = mybir.dt.bfloat16
AF = mybir.ActivationFunctionType
ALU = mybir.AluOpType

B, L, D = 4, 4096, 1024
TOK = 2048
Q = 128
NCH = 16
ST1 = 256
NST1 = TOK // ST1
WIN = 512
D_INNER, NHEADS, HEADDIM, D_STATE = 2048, 32, 64, 64
CONV_DIM = 2176
XW = 2816
OWN0 = 640
KV0 = 128
HID, HIDP = 2730, 2816
EPS = 1e-5
RW = 2560
SEG = 2049
PAIRS = [[0, 1], [2, 3], [4, 5], [6, 7]]


def bcast_rows_ap(row_ap, nparts):
    return row_ap.broadcast_to([nparts, row_ap.shape[1]])


def rep_outer(ap, reps):
    a = ap.rearrange("p (a b) -> p a b", a=1)
    return a.broadcast_to([a.shape[0], reps, a.shape[2]])


def rep_inner(ap, reps):
    a = ap.rearrange("p (a b) -> p a b", b=1)
    return a.broadcast_to([a.shape[0], a.shape[1], reps])



class _Track:

    def __init__(self, eng, kern=None, qkey=None):
        self._eng = eng
        self.last = None
        self._kern = kern
        self._qkey = qkey
        self._fenced = False

    def __getattr__(self, name):
        fn = getattr(self._eng, name)
        if callable(fn):
            def wrap(*a, **kw):
                if (name == "dma_start" and self._kern is not None
                        and not self._fenced and not _Track.no_fence):
                    self._fenced = True
                    self._eng.wait_ge(self._kern.sem[self._qkey],
                                      self._kern.cnt[self._qkey])
                r = fn(*a, **kw)
                if hasattr(r, "then_inc"):
                    self.last = r
                if name == "dma_start" and self._kern is not None:
                    self._kern.dma_audit[self._qkey] = \
                        self._kern.dma_audit.get(self._qkey, 0) + 16
                return r
            return wrap
        return fn

    no_fence = False


class Kern:
    def __init__(self, nc, es, dbg=()):
        self.nc = nc
        self.es = es
        self.dbg = set(dbg)
        self.dbg_out = {}
        self.block = es.enter_context(nc.Block())
        self.sem = {}
        self.cnt = {}
        self.dma_audit = {}
        for k in ("T", "V", "S", "G", "LD", "SP"):
            self.sem[k] = es.enter_context(nc.semaphore("sem" + k))
            self.cnt[k] = 0

    def inc(self, instr, k):
        n = 16 if k in ("LD", "SP") else 1
        self.cnt[k] += n
        instr.then_inc(self.sem[k], n)
        return (k, self.cnt[k])

    def now(self, k):
        return (k, self.cnt[k])

    def wait(self, eng, *evs):
        for k, v in evs:
            if k in ("LD", "SP"):
                v = self.cnt[k]
            if v > 0:
                eng.wait_ge(self.sem[k], v)

    def T(self, fn): self.block.tensor(lambda e: fn(_Track(e, self, "T")))
    def V(self, fn): self.block.vector(lambda e: fn(_Track(e, self, "V")))
    def S(self, fn): self.block.scalar(lambda e: fn(_Track(e, self, "S")))
    def G(self, fn): self.block.gpsimd(lambda e: fn(_Track(e, self, "SP")))
    def Y(self, fn): self.block.sync(lambda e: fn(_Track(e, self, "LD")))

    def incl(self, eng, k):
        assert eng.last is not None
        return self.inc(eng.last, k)

    def barrier(self):
        for qk in ("LD", "SP"):
            a, c = self.dma_audit.get(qk, 0), self.cnt[qk]
            assert a == c, f"DMA audit mismatch {qk}: emitted incs {a} != tracked cnt {c}"
        evs = [self.now(k) for k in ("T", "V", "S", "G", "LD", "SP")]
        for sec in (self.T, self.V, self.S, self.G, self.Y):
            sec(lambda e, evs=evs: self.wait(e, *evs))

    _uid = 0

    alloc_log = []

    def sbt(self, scope, name, shape, dt):
        Kern._uid += 1
        b0 = self.nc.sbuf_base
        t = scope.enter_context(
            self.nc.sbuf_tensor(f"{name}_u{Kern._uid}", shape, dt))
        Kern.alloc_log.append((name, b0, self.nc.sbuf_base, tuple(shape), str(dt)))
        return t

    def pst(self, scope, name, shape, dt=F32):
        Kern._uid += 1
        return scope.enter_context(
            self.nc.psum_tensor(f"{name}_u{Kern._uid}", shape, dt))

    def mkdbg(self, name, shape):
        if name in self.dbg and name not in self.dbg_out:
            self.dbg_out[name] = self.nc.dram_tensor(
                "dbg_" + name, shape, F32, kind="ExternalOutput").ap()
        return self.dbg_out.get(name)

    def dump(self, name, buf, nrt, width, cols0=0, src_width=None):
        d = self.mkdbg(name, [nrt * 128, width])
        if d is None:
            return
        sw = src_width or width
        def g(gg):
            self.wait(gg, self.now("V"), self.now("S"), self.now("T"),
                      self.now("G") if False else (0, 0))
            for rt in range(nrt):
                self.inc(gg.dma_start(
                    out=d[rt * 128:(rt + 1) * 128, :],
                    in_=buf[:, rt * sw + cols0: rt * sw + cols0 + width]), "SP")
        self.G(g)

    def declare_io(self):
        DT = self.nc.dram_tensor
        ap = lambda *a, **kw: DT(*a, **kw).ap()
        self.x_fm = ap("x_fm", [D, XW], F32, kind="ExternalInput")
        self.wx = {d: ap(f"wx_{d}", [D, 2208], F32, kind="ExternalInput") for d in "fr"}
        self.wz = {d: ap(f"wz_{d}", [D, D_INNER], F32, kind="ExternalInput") for d in "fr"}
        self.wout = {d: ap(f"wout_{d}", [D_INNER, D], F32, kind="ExternalInput") for d in "fr"}
        self.convw = {d: ap(f"convw_{d}", [128, 68], F32, kind="ExternalInput") for d in "fr"}
        self.convb = {d: ap(f"convb_{d}", [128, 17], F32, kind="ExternalInput") for d in "fr"}
        self.mv = {d: ap(f"mv_{d}", [32, 3], F32, kind="ExternalInput") for d in "fr"}
        self.dexp = {d: ap(f"dexp_{d}", [128, 16], F32, kind="ExternalInput") for d in "fr"}
        self.rmsw = {d: ap(f"rmsw_{d}", [128, 16], F32, kind="ExternalInput") for d in "fr"}
        self.lnwb = ap("lnwb", [128, 16], F32, kind="ExternalInput")
        self.wqkv = ap("wqkv", [2, D, 1536], F32, kind="ExternalInput")
        self.wo2 = ap("wo2", [2, 512, D], F32, kind="ExternalInput")
        self.w1p = ap("w1p", [2, D, 2 * HIDP], F32, kind="ExternalInput")
        self.b1p = ap("b1p", [2, 128, 44], F32, kind="ExternalInput")
        self.w2p = ap("w2p", [2, HIDP, D], F32, kind="ExternalInput")
        self.b2v = ap("b2v", [2, 128, 8], F32, kind="ExternalInput")
        self.lng = ap("lng", [2, 128, 8], F32, kind="ExternalInput")
        self.lnb2 = ap("lnb2", [2, 128, 8], F32, kind="ExternalInput")
        self.qsc = ap("qsc", [2, 128, 1], F32, kind="ExternalInput")
        self.ksc = ap("ksc", [2, 128, 1], F32, kind="ExternalInput")
        self.rot = ap("rot", [4, 128, RW], F32, kind="ExternalInput")
        self.flags = ap("flags", [128, 8], F32, kind="ExternalInput")
        self.y_fm = ap("y_fm", [D, TOK], F32, kind="ExternalOutput")

        self.sp_xnbf = ap("sp_xnbf", [D, XW], BF16)
        self.sp_xn32 = ap("sp_xn32", [D, TOK], F32)
        self.sp_xD = {d: ap(f"sp_xD_{d}", [D_INNER, TOK], BF16) for d in "fr"}
        self.sp_xT = {d: ap(f"sp_xT_{d}", [TOK, D_INNER], BF16) for d in "fr"}
        self.sp_B = {d: ap(f"sp_B_{d}", [64, TOK], BF16) for d in "fr"}
        self.sp_C = {d: ap(f"sp_C_{d}", [64, TOK], BF16) for d in "fr"}
        self.sp_sadd = {d: ap(f"sp_sadd_{d}", [64, NCH * D_INNER], F32) for d in "fr"}
        self.sp_cl = {d: ap(f"sp_cl_{d}", [32, TOK], F32) for d in "fr"}
        self.sp_gam = {d: ap(f"sp_gam_{d}", [32, TOK], F32) for d in "fr"}
        self.sp_ctl = {d: ap(f"sp_ctl_{d}", [TOK, 64], F32) for d in "fr"}
        self.sp_dr = {d: ap(f"sp_dr_{d}", [NCH, D_INNER], F32) for d in "fr"}
        self.sp_zs = {d: ap(f"sp_zs_{d}", [D_INNER, TOK], F32) for d in "fr"}
        self.acc_m = ap("acc_m", [D, TOK], F32)
        self.sp_s1 = ap("sp_s1", [D, SEG], F32)
        self.sp_s2 = ap("sp_s2", [D, SEG], F32)
        self.sp_rows = ap("sp_rows", [16, 4096], F32)
        self.cc1_in = ap("cc1_in", [128, 2056], F32)
        self.cc1_out = ap("cc1_out", [256, 2056], F32)
        self.cc2a_in = ap("cc2a_in", [128, 8], F32)
        self.cc2a_out = ap("cc2a_out", [256, 8], F32)
        self.cc2b_in = ap("cc2b_in", [128, 4104], F32)
        self.cc2b_out = ap("cc2b_out", [256, 4104], F32)

    def persistent(self):
        es, p = self.es, self
        self.consts = self.sbt(es, "consts", [128, 10], F32)
        self.ones_bf = self.sbt(es, "ones_bf", [128, 1], BF16)
        self.ident_bf = self.sbt(es, "ident_bf", [128, 128], BF16)
        self.ident_f = self.sbt(es, "ident_f", [128, 128], F32)
        self.blk2 = self.sbt(es, "blk2", [128, 2], BF16)
        self.flags_sb = self.sbt(es, "flags_sb", [128, 8], F32)
        self.lnwb_sb = self.sbt(es, "lnwb_sb", [128, 16], F32)
        self.s_state = self.sbt(es, "s_state", [64, 2 * D_INNER], F32)
        tmpf = self.sbt(es, "tmpf", [128, 128], F32)
        self.ph_h2 = contextlib.ExitStack()
        self.h2 = self.sbt(self.ph_h2, "h2", [128, 8 * XW], BF16)

        def v0(vv):
            for i, v in enumerate([EPS, -0.5, 1.0, -8.0, 1.0 / 2048.0, 0.0,
                                   1e12, -1.0, 1e-30, 0.0]):
                vv.memset(self.consts[:, i:i + 1], v)
            vv.memset(self.ones_bf[:], 1.0)
            vv.memset(self.blk2[:], 0.0)
            vv.memset(self.blk2[0:64, 0:1], 1.0)
            vv.memset(self.blk2[64:128, 1:2], 1.0)
            p.inc(vv.memset(tmpf[:], 1.0), "V")
        p.V(v0)

        def g(gg):
            p.wait(gg, p.now("V"))
            p.inc(gg.affine_select(
                out=tmpf[:], in_=tmpf[:], pattern=[[-1, 128]],
                compare_op=ALU.is_equal, fill=0.0, base=0,
                channel_multiplier=1), "G")
            gg.dma_start(out=self.flags_sb[:], in_=self.flags[:, :]).then_inc(
                p.sem["SP"], 16)
            gg.dma_start(out=self.lnwb_sb[:], in_=self.lnwb[:, :]).then_inc(
                p.sem["SP"], 16)
        p.G(g)
        p.cnt["SP"] += 32

        def v(vv):
            p.wait(vv, p.now("G"), p.now("SP"))
            vv.tensor_copy(self.ident_bf[:], tmpf[:])
            p.inc(vv.tensor_copy(self.ident_f[:], tmpf[:]), "V")
        p.V(v)
        self.mf = self.flags_sb[:, 0:1]
        self.mb = self.flags_sb[:, 1:2]
        self.C_EPS, self.C_NH, self.C_ONE, self.C_M8 = 0, 1, 2, 3
        self.C_IDIM, self.C_ZERO, self.C_BIG, self.C_M1 = 4, 5, 6, 7
        self.C_TINY = 8

    def cc(self, col):
        return self.consts[:, col:col + 1]

    def p0(self):
        p, nc = self, self.nc
        ph = contextlib.ExitStack()
        with ph:
            xrot = [self.sbt(ph, f"xrot{i}", [128, XW], F32) for i in range(2)]
            xbf = self.sbt(ph, "xbf", [128, 8 * XW], BF16)
            rows0 = self.sbt(ph, "rows0", [1, XW], F32)
            rows1 = self.sbt(ph, "rows1", [1, XW], F32)
            mub = self.sbt(ph, "mub", [128, XW], F32)
            rb = self.sbt(ph, "rb", [128, XW], F32)
            x32r = [self.sbt(ph, f"x32r{i}", [128, TOK], F32) for i in range(2)]
            sqs = [self.sbt(ph, f"sqs{i}", [128, XW], BF16) for i in range(2)]
            ps_st = self.pst(ph, "ps_st", [1, XW])
            MCH = [(i * 512, min(512, XW - i * 512)) for i in range(6)]

            done = [None] * 8
            for rt in range(8):
                ldev = [None]
                sev = [None]
                def y(yq, rt=rt, ldev=ldev):
                    if rt >= 2:
                        p.wait(yq, *done[rt - 2])
                    ldev[0] = p.inc(yq.dma_start(
                        out=xrot[rt % 2][:],
                        in_=self.x_fm[rt * 128:(rt + 1) * 128, :]), "LD")
                p.Y(y)
                def s_(ss, rt=rt, ldev=ldev, sev=sev):
                    p.wait(ss, ldev[0])
                    if rt >= 2:
                        p.wait(ss, p.now("T"))
                    sev[0] = p.inc(ss.activation(out=sqs[rt % 2][:],
                                                 in_=xrot[rt % 2][:],
                                                 func=AF.Square), "S")
                p.S(s_)
                def t(tt, rt=rt, sev=sev):
                    p.wait(tt, sev[0])
                    for c0, w in MCH:
                        tt.matmul(ps_st[0:1, c0:c0 + w], self.ones_bf[:],
                                  sqs[rt % 2][:, c0:c0 + w],
                                  start=(rt == 0), stop=(rt == 7))
                    p.incl(tt, 'T')
                p.T(t)
                def v(vv, rt=rt, ldev=ldev):
                    p.wait(vv, ldev[0])
                    p.inc(vv.tensor_copy(xbf[:, rt * XW:(rt + 1) * XW],
                                         xrot[rt % 2][:]), "V")
                p.V(v)
                done[rt] = [p.now("S"), p.now("V")]

            rowsl = [None, None]
            def evac_row(dst_row):
                rowsl[dst_row] = rows0 if dst_row == 0 else rows1
                def s_(ss):
                    p.wait(ss, p.now("T"))
                    p.inc(ss.copy((rows0 if dst_row == 0 else rows1)[0:1, :],
                                  ps_st[0:1, :]), "S")
                p.S(s_)

            def sum_pass(dst_row):
                def t(tt):
                    p.wait(tt, p.now("V"), p.now("S"))
                    for c0, w in MCH:
                        for rt in range(8):
                            tt.matmul(ps_st[0:1, c0:c0 + w], self.ones_bf[:],
                                      xbf[:, rt * XW + c0:rt * XW + c0 + w],
                                      start=(rt == 0), stop=(rt == 7))
                    p.incl(tt, 'T')
                p.T(t)
                evac_row(dst_row)

            def sq_pass(dst_row):
                for rt in range(8):
                    sev = [None]
                    def s_(ss, rt=rt, sev=sev):
                        if rt >= 2:
                            p.wait(ss, p.now("T"))
                        sev[0] = p.inc(ss.activation(
                            out=sqs[rt % 2][:], in_=xbf[:, rt * XW:(rt + 1) * XW],
                            func=AF.Square), "S")
                    p.S(s_)
                    def t(tt, rt=rt, sev=sev):
                        p.wait(tt, sev[0])
                        for c0, w in MCH:
                            tt.matmul(ps_st[0:1, c0:c0 + w], self.ones_bf[:],
                                      sqs[rt % 2][:, c0:c0 + w],
                                      start=(rt == 0), stop=(rt == 7))
                        p.incl(tt, 'T')
                    p.T(t)
                evac_row(dst_row)

            def rows_math():
                def v(vv):
                    p.wait(vv, p.now("S"))
                    vv.tensor_scalar_mul(rows0[0:1, :], rows0[0:1, :], 1.0 / 1024.0)
                    vv.tensor_scalar_mul(rows1[0:1, :], rows1[0:1, :], 1.0 / 1024.0)
                    vv.tensor_tensor(out=mub[0:1, :], in0=rows0[0:1, :],
                                     in1=rows0[0:1, :], op=ALU.mult)
                    p.inc(vv.tensor_tensor(out=rows1[0:1, :], in0=rows1[0:1, :],
                                           in1=mub[0:1, :], op=ALU.subtract), "V")
                p.V(v)
                def s_(ss):
                    p.wait(ss, p.now("V"))
                    ss.activation(out=rows1[0:1, :], in_=rows1[0:1, :], func=AF.Ln,
                                  bias=self.cc(self.C_EPS)[0:1, :])
                    p.inc(ss.activation(out=rows1[0:1, :], in_=rows1[0:1, :],
                                        func=AF.Exp,
                                        scale=self.cc(self.C_NH)[0:1, :]), "S")
                p.S(s_)

            def bcast_rows(off):
                def g(gg):
                    p.wait(gg, p.now("S"), p.now("V"))
                    gg.dma_start(out=self.sp_rows[off:off + 1, 0:XW],
                                 in_=rows0[0:1, :]).then_inc(p.sem["SP"], 16)
                    gg.dma_start(out=self.sp_rows[off + 1:off + 2, 0:XW],
                                 in_=rows1[0:1, :]).then_inc(p.sem["SP"], 16)
                    p.cnt["SP"] += 32
                    gg.wait_ge(p.sem["SP"], p.cnt["SP"])
                    gg.dma_start(out=mub[:], in_=self.sp_rows[
                        off:off + 1, 0:XW].broadcast_to([128, XW])
                    ).then_inc(p.sem["SP"], 16)
                    gg.dma_start(out=rb[:], in_=self.sp_rows[
                        off + 1:off + 2, 0:XW].broadcast_to([128, XW])
                    ).then_inc(p.sem["SP"], 16)
                    p.cnt["SP"] += 32
                p.G(g)

            evac_row(1)
            sum_pass(0)
            rows_math()
            bcast_rows(0)
            ev_bc = p.now("SP")

            for rt in range(8):
                ldev = [None]
                vev = [None]
                def y(yq, rt=rt, ldev=ldev):
                    if rt < 2:
                        p.wait(yq, p.now("V"), p.now("S"))
                    else:
                        p.wait(yq, done[rt - 2][0])
                    ldev[0] = p.inc(yq.dma_start(
                        out=xrot[rt % 2][:],
                        in_=self.x_fm[rt * 128:(rt + 1) * 128, :]), "LD")
                p.Y(y)
                def v(vv, rt=rt, ldev=ldev, vev=vev):
                    p.wait(vv, ldev[0], ev_bc)
                    if rt >= 2 and len(done[rt - 2]) > 1:
                        p.wait(vv, done[rt - 2][1])
                    sl = slice(rt * XW, (rt + 1) * XW)
                    vv.tensor_tensor(out=xrot[rt % 2][:], in0=xrot[rt % 2][:],
                                     in1=mub[:], op=ALU.subtract)
                    vv.tensor_tensor(out=xbf[:, sl], in0=xrot[rt % 2][:],
                                     in1=rb[:], op=ALU.mult)
                    vev[0] = p.inc(vv.tensor_tensor(
                        out=x32r[rt % 2][:],
                        in0=xrot[rt % 2][:, OWN0:OWN0 + TOK],
                        in1=rb[:, OWN0:OWN0 + TOK], op=ALU.mult), "V")
                    done[rt] = [p.now("V")]
                p.V(v)
                def g(gg, rt=rt, vev=vev):
                    p.wait(gg, vev[0])
                    p.inc(gg.dma_start(
                        out=self.sp_xn32[rt * 128:(rt + 1) * 128, :],
                        in_=x32r[rt % 2][:]), "SP")
                p.G(g)
                done[rt].append(p.now("SP"))

            sum_pass(0)
            sq_pass(1)
            rows_math()
            bcast_rows(2)
            ev_bc2 = p.now("SP")
            def v2(vv):
                p.wait(vv, ev_bc2)
                for rt in range(8):
                    sl = slice(rt * XW, (rt + 1) * XW)
                    vv.tensor_tensor(out=xrot[rt % 2][:], in0=xbf[:, sl],
                                     in1=mub[:], op=ALU.subtract)
                    vv.tensor_tensor(out=xrot[rt % 2][:], in0=xrot[rt % 2][:],
                                     in1=rb[:], op=ALU.mult)
                    p.inc(vv.tensor_scalar(
                        out=self.h2[:, sl], in0=xrot[rt % 2][:],
                        scalar1=self.lnwb_sb[:, rt:rt + 1],
                        scalar2=self.lnwb_sb[:, 8 + rt:9 + rt],
                        op0=ALU.mult, op1=ALU.add), "V")
            p.V(v2)

            def g2(gg):
                p.wait(gg, p.now("V"))
                for rt in range(8):
                    p.inc(gg.dma_start(
                        out=self.sp_xnbf[rt * 128:(rt + 1) * 128, :],
                        in_=xbf[:, rt * XW:(rt + 1) * XW]), "SP")
            p.G(g2)

            self.dump("h2", self.h2, 8, XW, src_width=XW)
            self.dump("xn", xbf, 8, XW, src_width=XW)
            p.barrier()

    def p1(self, d, rev):
        p, nc = self, self.nc
        sdir = slice(0, D_INNER) if d == "f" else slice(D_INNER, 2 * D_INNER)
        ph = contextlib.ExitStack()
        with ph:
            wxb = self.sbt(ph, "wxb", [128, 8 * 2208], BF16)
            wstg0 = self.sbt(ph, "wstg0", [128, 2208], F32)
            wstg = [wstg0, wstg0]
            convw_sb = self.sbt(ph, "convw_sb", [128, 68], F32)
            convb_sb = self.sbt(ph, "convb_sb", [128, 17], F32)
            mv_sb = self.sbt(ph, "mv_sb", [32, 4], F32)
            dexp_sb = self.sbt(ph, "dexp_sb", [128, 16], F32)
            WID = ST1 + 6
            XPRE = self.sbt(ph, "XPRE", [128, 17 * WID], F32)
            DTR = self.sbt(ph, "DTR", [32, WID], F32)
            XBC = self.sbt(ph, "XBC", [128, 17 * ST1], BF16)
            xDst = self.sbt(ph, "xDst", [128, 16 * ST1], BF16)
            ctmpa = self.sbt(ph, "ctmpa", [128, ST1], F32)
            ctmpb = self.sbt(ph, "ctmpb", [128, ST1], F32)
            caccf = self.sbt(ph, "caccf", [128, 2 * ST1], F32)
            dtv = self.sbt(ph, "dtv", [32, ST1], F32)
            lam = self.sbt(ph, "lam", [32, ST1], F32)
            lndt = self.sbt(ph, "lndt", [32, ST1], F32)
            zero32 = self.sbt(ph, "zero32", [32, Q], F32)
            clg = self.sbt(ph, "clg", [32, 2 * Q], F32)
            clx = self.sbt(ph, "clx", [32, 2 * Q], F32)
            gmr = self.sbt(ph, "gmr", [32, 2 * Q], F32)
            wdec = self.sbt(ph, "wdec", [32, Q], F32)
            dtw = self.sbt(ph, "dtw", [32, Q], F32)
            pk4 = self.sbt(ph, "pk4", [128, Q], F32)
            tctl = self.sbt(ph, "tctl", [128, 128], F32)
            trow12 = self.sbt(ph, "trow12", [1, 32], F32)
            drow = self.sbt(ph, "drow", [1, D_INNER], F32)
            xt_sb = self.sbt(ph, "xt_sb", [128, 2 * D_INNER], BF16)
            xwt_sb = self.sbt(ph, "xwt_sb", [128, D_INNER], BF16)
            bt_sb = self.sbt(ph, "bt_sb", [128, 64], BF16)
            sadd_sb = self.sbt(ph, "sadd_sb", [64, D_INNER], F32)
            dtile = self.sbt(ph, "dtile", [64, D_INNER], F32)
            ps_ip = self.pst(ph, "ps_ip", [128, 1024])
            ps_tpb = self.pst(ph, "ps_tpb", [128, 2048], BF16)
            ps_tpb2 = self.pst(ph, "ps_tpb2", [128, 64], BF16)
            ps_tctl = self.pst(ph, "ps_tctl", [128, 128])
            ps_sa = self.pst(ph, "ps_sa", [64, 1024])

            cast_ev = [None] * 8
            for kt in range(8):
                ldev = [None]
                def y(yq, kt=kt, ldev=ldev):
                    if kt >= 1:
                        p.wait(yq, cast_ev[kt - 1])
                    ldev[0] = p.inc(yq.dma_start(
                        out=wstg[kt % 2][:],
                        in_=self.wx[d][kt * 128:(kt + 1) * 128, :]), "LD")
                p.Y(y)
                def v(vv, kt=kt, ldev=ldev):
                    p.wait(vv, ldev[0])
                    cast_ev[kt] = p.inc(vv.tensor_copy(
                        wxb[:, kt * 2208:(kt + 1) * 2208], wstg[kt % 2][:]), "V")
                p.V(v)
            ev_wx = cast_ev[7]

            def y1(yq):
                for dst, src in ((convw_sb, self.convw[d]),
                                 (convb_sb, self.convb[d]),
                                 (dexp_sb, self.dexp[d])):
                    yq.dma_start(out=dst[:], in_=src[:, :]).then_inc(
                        p.sem["LD"], 16)
                yq.dma_start(out=mv_sb[:, 0:3], in_=self.mv[d][:, :]).then_inc(
                    p.sem["LD"], 16)
                p.cnt["LD"] += 64
            p.Y(y1)
            ev_small = p.now("LD")

            def s0(ss):
                p.wait(ss, ev_small)
                p.inc(ss.activation(out=mv_sb[:, 3:4], in_=mv_sb[:, 1:2],
                                    func=AF.Exp), "S")
            p.S(s0)
            def v0(vv):
                p.wait(vv, p.now("S"))
                vv.memset(zero32[:], 0.0)
                vv.tensor_scalar_mul(mv_sb[:, 3:4], mv_sb[:, 3:4], -1.0)
                p.inc(vv.memset(self.s_state[:, sdir], 0.0), "V")
            p.V(v0)

            sts = (list(range(P1NST)) if not rev else list(range(P1NST - 1, -1, -1))) if P1LVL >= 1 else []
            chain_ev = [("V", 0)]
            xtsp_ev = {0: ("SP", 0), 1: ("SP", 0)}
            for sti, st in enumerate(sts):
                c0 = OWN0 + st * ST1 - 3
                ipev = {}
                for nt in range(int(os.environ.get("KB_P1_NT", "18"))):
                    nw = 128 if nt < 17 else 32
                    def t1(tt, nt=nt, nw=nw):
                        if nt < 2:
                            p.wait(tt, ev_wx, p.now("V"), p.now("S"))
                        if nt >= 2:
                            p.wait(tt, ipev[nt - 2][1])
                        bk = (nt % 2) * 512
                        if not os.environ.get("KB_P1_NOMM"):
                            for kt in range(8):
                                tt.matmul(
                                    ps_ip[0:nw, bk:bk + WID],
                                    wxb[:, kt * 2208 + nt * 128:kt * 2208 + nt * 128 + nw],
                                    self.h2[:, kt * XW + c0:kt * XW + c0 + WID],
                                    start=(kt == 0), stop=(kt == 7))
                        ipev[nt] = [p.incl(tt, 'T'), None]
                    p.T(t1)
                    def s1(ss, nt=nt, nw=nw):
                        p.wait(ss, ipev[nt][0])
                        bk = (nt % 2) * 512
                        if nt < 17:
                            e = p.inc(ss.copy(XPRE[:, nt * WID:(nt + 1) * WID],
                                              ps_ip[:, bk:bk + WID]), "S")
                        else:
                            e = p.inc(ss.copy(DTR[:, :], ps_ip[0:32, bk:bk + WID]), "S")
                        ipev[nt][1] = e
                    p.S(s1)
                ev_xpre = p.now("S")

                if P1LVL < 2:
                    continue
                first_st = (st == 0)
                last_st = (st == NST1 - 1)
                def vmask(vv):
                    p.wait(vv, ev_xpre)
                    if first_st and 18 == int(os.environ.get("KB_P1_NT", "18")):
                        for ntt in range(17):
                            vv.tensor_scalar_mul(
                                XPRE[:, ntt * WID:ntt * WID + 3],
                                XPRE[:, ntt * WID:ntt * WID + 3], self.mf)
                    if last_st:
                        for ntt in range(17):
                            vv.tensor_scalar_mul(
                                XPRE[:, ntt * WID + WID - 3:(ntt + 1) * WID],
                                XPRE[:, ntt * WID + WID - 3:(ntt + 1) * WID],
                                self.mb)
                    p.incl(vv, 'V')
                p.V(vmask)
                ev_m = p.now("V")

                convev = {}
                offs = [0, 1, 2, 3] if not rev else [6, 5, 4, 3]
                for ct in range(17):
                    def vc(vv, ct=ct):
                        if ct == 0:
                            p.wait(vv, ev_m)
                        if ct >= 2:
                            p.wait(vv, convev[ct - 2])
                        base = ct * WID
                        vv.tensor_scalar(
                            out=ctmpa[:], in0=XPRE[:, base + offs[0]:base + offs[0] + ST1],
                            scalar1=convw_sb[:, ct * 4:ct * 4 + 1], scalar2=None,
                            op0=ALU.mult)
                        vv.scalar_tensor_tensor(
                            out=ctmpb[:], in0=XPRE[:, base + offs[1]:base + offs[1] + ST1],
                            scalar=convw_sb[:, ct * 4 + 1:ct * 4 + 2],
                            in1=ctmpa[:], op0=ALU.mult, op1=ALU.add)
                        vv.scalar_tensor_tensor(
                            out=ctmpa[:], in0=XPRE[:, base + offs[2]:base + offs[2] + ST1],
                            scalar=convw_sb[:, ct * 4 + 2:ct * 4 + 3],
                            in1=ctmpb[:], op0=ALU.mult, op1=ALU.add)
                        p.inc(vv.scalar_tensor_tensor(
                            out=caccf[:, (ct % 2) * ST1:(ct % 2 + 1) * ST1],
                            in0=XPRE[:, base + offs[3]:base + offs[3] + ST1],
                            scalar=convw_sb[:, ct * 4 + 3:ct * 4 + 4],
                            in1=ctmpa[:], op0=ALU.mult, op1=ALU.add), "V")
                    p.V(vc)
                    def sc(ss, ct=ct):
                        p.wait(ss, p.now("V"))
                        e = p.inc(ss.activation(
                            out=XBC[:, ct * ST1:(ct + 1) * ST1],
                            in_=caccf[:, (ct % 2) * ST1:(ct % 2 + 1) * ST1],
                            func=AF.Silu, bias=convb_sb[:, ct:ct + 1]), "S")
                        convev[ct] = e
                    p.S(sc)
                ev_xbc = p.now("S")

                def sdt(ss):
                    p.wait(ss, ipev[17][1], p.now("V"))
                    ss.activation(out=dtv[:], in_=DTR[:, 3:3 + ST1], func=AF.Exp,
                                  bias=mv_sb[:, 0:1])
                    ss.activation(out=dtv[:], in_=dtv[:], func=AF.Ln,
                                  bias=self.cc(self.C_ONE)[0:32, :])
                    p.inc(ss.activation(out=lndt[:], in_=dtv[:], func=AF.Ln,
                                        bias=self.cc(self.C_TINY)[0:32, :]), "S")
                p.S(sdt)
                def vdt(vv):
                    p.wait(vv, p.now("S"))
                    p.inc(vv.tensor_scalar(out=lam[:], in0=dtv[:],
                                           scalar1=mv_sb[:, 3:4], scalar2=None,
                                           op0=ALU.mult), "V")
                p.V(vdt)
                ev_dt = p.now("V")

                def vxd(vv):
                    p.wait(vv, ev_xbc)
                    for ct in range(16):
                        vv.tensor_scalar(
                            out=xDst[:, ct * ST1:(ct + 1) * ST1],
                            in0=XBC[:, ct * ST1:(ct + 1) * ST1],
                            scalar1=dexp_sb[:, ct:ct + 1], scalar2=None,
                            op0=ALU.mult)
                    p.incl(vv, 'V')
                p.V(vxd)
                ev_xd = p.now("V")

                def gxd(gg, st=st):
                    p.wait(gg, ev_xd, ev_xbc)
                    for ct in range(16):
                        p.inc(gg.dma_start(
                            out=self.sp_xD[d][ct * 128:(ct + 1) * 128,
                                              st * ST1:(st + 1) * ST1],
                            in_=xDst[:, ct * ST1:(ct + 1) * ST1]), "SP")
                    p.inc(gg.dma_start(
                        out=self.sp_B[d][:, st * ST1:(st + 1) * ST1],
                        in_=XBC[0:64, 16 * ST1:17 * ST1]), "SP")
                    p.inc(gg.dma_start(
                        out=self.sp_C[d][:, st * ST1:(st + 1) * ST1],
                        in_=XBC[64:128, 16 * ST1:17 * ST1]), "SP")
                p.G(gxd)

                qs_l = ([0, 1] if not rev else [1, 0]) if P1LVL >= 3 else []
                for qi in qs_l:
                    cg = st * 2 + qi
                    qsl = slice(qi * Q, (qi + 1) * Q)
                    csl = slice(qi * Q, qi * Q + Q)
                    def vch(vv, qi=qi):
                        p.wait(vv, ev_dt)
                        vv.tensor_tensor_scan(
                            out=clg[:, qsl], data0=lam[:, qsl], data1=zero32[:],
                            initial=0.0, op0=ALU.add, op1=ALU.add)
                        if rev:
                            vv.tensor_tensor(out=clx[:, qsl], in0=clg[:, qsl],
                                             in1=lam[:, qsl], op=ALU.subtract)
                        else:
                            vv.tensor_copy(clx[:, qsl], clg[:, qsl])
                        if not rev:
                            vv.tensor_scalar(
                                out=wdec[:], in0=clg[:, qsl],
                                scalar1=clg[:, qi * Q + Q - 1:qi * Q + Q],
                                scalar2=None, op0=ALU.subtract)
                            vv.tensor_copy(gmr[:, qsl], clx[:, qsl])
                        else:
                            vv.tensor_copy(wdec[:], clx[:, qsl])
                            vv.tensor_scalar(
                                out=gmr[:, qsl], in0=clx[:, qsl],
                                scalar1=clg[:, qi * Q + Q - 1:qi * Q + Q],
                                scalar2=None, op0=ALU.subtract)
                        p.inc(vv.tensor_tensor(
                            out=pk4[0:32, :], in0=clx[:, qsl], in1=lndt[:, qsl],
                            op=(ALU.subtract if not rev else ALU.add)), "V")
                    p.V(vch)
                    def sch(ss, qi=qi):
                        p.wait(ss, p.now("V"))
                        if not rev:
                            ss.activation(out=wdec[:], in_=wdec[:], func=AF.Exp,
                                          scale=self.cc(self.C_M1)[0:32, :])
                            p.inc(ss.activation(out=gmr[:, qsl], in_=gmr[:, qsl],
                                                func=AF.Exp), "S")
                        else:
                            ss.activation(out=wdec[:], in_=wdec[:], func=AF.Exp)
                            p.inc(ss.activation(
                                out=gmr[:, qsl], in_=gmr[:, qsl], func=AF.Exp,
                                scale=self.cc(self.C_M1)[0:32, :]), "S")
                    p.S(sch)
                    def vch2(vv, qi=qi):
                        p.wait(vv, p.now("S"))
                        vv.tensor_tensor(out=dtw[:], in0=dtv[:, qsl],
                                         in1=wdec[:], op=ALU.mult)
                        vv.tensor_copy(pk4[32:64, :], lndt[:, qsl])
                        vv.tensor_copy(pk4[64:96, :], dtw[:])
                        p.inc(vv.tensor_copy(pk4[96:128, :], lam[:, qsl]), "V")
                    p.V(vch2)
                    ev_pk = p.now("V")
                    def ttc(tt, qi=qi):
                        p.wait(tt, ev_pk, p.now("S"))
                        p.inc(tt.transpose(ps_tctl[:], pk4[:], self.ident_f[:]), "T")
                    p.T(ttc)
                    def stc(ss, qi=qi):
                        p.wait(ss, p.now("T"))
                        p.inc(ss.copy(tctl[:], ps_tctl[:]), "S")
                    p.S(stc)
                    ev_tctl = p.now("S")

                    def gdr(gg, qi=qi):
                        p.wait(gg, p.now("V"), p.now("S"))
                        p.inc(gg.dma_start(
                            out=trow12[:],
                            in_=clg[:, qi * Q + Q - 1:qi * Q + Q],
                            allow_slow_non_contiguous=True), "SP")
                    p.G(gdr)
                    ev_tr = p.now("SP")
                    def sdr(ss, qi=qi):
                        p.wait(ss, ev_tr)
                        p.inc(ss.activation(
                            out=drow[:].rearrange("p (a b) -> p a b", b=64),
                            in_=rep_inner(trow12[:], 64),
                            func=AF.Exp), "S")
                    p.S(sdr)
                    ev_drow = p.now("S")

                    def gsp(gg, cg=cg, qi=qi):
                        p.wait(gg, ev_drow, p.now("V"))
                        gg.dma_start(out=self.sp_cl[d][:, cg * Q:(cg + 1) * Q],
                                     in_=clx[:, qsl]).then_inc(p.sem["SP"], 16)
                        gg.dma_start(out=self.sp_gam[d][:, cg * Q:(cg + 1) * Q],
                                     in_=gmr[:, qsl]).then_inc(p.sem["SP"], 16)
                        gg.dma_start(out=self.sp_ctl[d][cg * Q:(cg + 1) * Q, :],
                                     in_=tctl[:, 0:64]).then_inc(p.sem["SP"], 16)
                        gg.dma_start(out=self.sp_dr[d][cg:cg + 1, :],
                                     in_=drow[:]).then_inc(p.sem["SP"], 16)
                        p.cnt["SP"] += 64
                    p.G(gsp)

                    tev = [None]
                    def ttp(tt, qi=qi, tev=tev):
                        p.wait(tt, ev_xbc, p.now("V"))
                        for hp in range(16):
                            tt.transpose(
                                ps_tpb[:, hp * 128:(hp + 1) * 128],
                                XBC[:, hp * ST1 + qi * Q: hp * ST1 + qi * Q + Q],
                                self.ident_bf[:])
                        tev[0] = p.inc(tt.transpose(
                            ps_tpb2[:, 0:64],
                            XBC[0:64, 16 * ST1 + qi * Q:16 * ST1 + qi * Q + Q],
                            self.ident_bf[0:64, 0:64]), "T")
                    p.T(ttp)
                    def vev_(vv, qi=qi, tev=tev):
                        p.wait(vv, tev[0], xtsp_ev[qi % 2])
                        xts = xt_sb[:, (qi % 2) * D_INNER:(qi % 2 + 1) * D_INNER]
                        vv.tensor_copy(xts, ps_tpb[:])
                        vv.tensor_copy(bt_sb[:, 0:64], ps_tpb2[:, 0:64])
                        p.inc(vv.tensor_tensor(
                            out=xwt_sb[:].rearrange("p (a b) -> p a b", b=64),
                            in0=xts.rearrange("p (a b) -> p a b", b=64),
                            in1=rep_inner(tctl[:, 64:96], 64),
                            op=ALU.mult), "V")
                    p.V(vev_)
                    ev_xw = p.now("V")

                    def gxt(gg, cg=cg, qi=qi):
                        p.wait(gg, ev_xw)
                        xtsp_ev[qi % 2] = p.inc(gg.dma_start(
                            out=self.sp_xT[d][cg * Q:(cg + 1) * Q, :],
                            in_=xt_sb[:, (qi % 2) * D_INNER:(qi % 2 + 1) * D_INNER]), "SP")
                    p.G(gxt)

                    for half in range(2):
                        def tsa(tt, qi=qi, half=half):
                            p.wait(tt, ev_xw)
                            if half == 1:
                                p.wait(tt, p.now("S"))
                            for mi in range(2):
                                m = half * 2 + mi
                                tt.matmul(ps_sa[:, mi * 512:(mi + 1) * 512],
                                          bt_sb[:, 0:64],
                                          xwt_sb[:, m * 512:(m + 1) * 512],
                                          start=True, stop=True)
                            p.incl(tt, 'T')
                        p.T(tsa)
                        def ssa(ss, qi=qi, half=half):
                            p.wait(ss, p.now("T"), chain_ev[0])
                            p.inc(ss.copy(
                                sadd_sb[:, half * 1024:(half + 1) * 1024],
                                ps_sa[:]), "S")
                        p.S(ssa)
                    ev_sadd = p.now("S")
                    def gsa(gg, cg=cg):
                        p.wait(gg, ev_sadd)
                        p.inc(gg.dma_start(
                            out=self.sp_sadd[d][:, cg * D_INNER:(cg + 1) * D_INNER],
                            in_=sadd_sb[:]), "SP")
                        gg.wait_ge(p.sem["SP"], p.cnt["SP"])
                        p.inc(gg.dma_start(
                            out=dtile[:], in_=self.sp_dr[d][cg:cg + 1, :]
                            .broadcast_to([64, D_INNER])), "SP")
                    p.G(gsa)
                    ev_dtile = p.now("SP")

                    def vchain(vv, qi=qi):
                        p.wait(vv, ev_dtile)
                        vv.tensor_tensor(out=self.s_state[:, sdir],
                                         in0=self.s_state[:, sdir],
                                         in1=dtile[:], op=ALU.mult)
                        chain_ev[0] = p.inc(vv.tensor_tensor(
                            out=self.s_state[:, sdir], in0=self.s_state[:, sdir],
                            in1=sadd_sb[:], op=ALU.add), "V")
                    p.V(vchain)

            for kt in range(8):
                ldev = [None]
                def y(yq, kt=kt, ldev=ldev):
                    p.wait(yq, cast_ev[kt - 1] if kt >= 1 else ev_wx)
                    ldev[0] = p.inc(yq.dma_start(
                        out=wstg[kt % 2][:, 0:D_INNER],
                        in_=self.wz[d][kt * 128:(kt + 1) * 128, :]), "LD")
                p.Y(y)
                def v(vv, kt=kt, ldev=ldev):
                    p.wait(vv, ldev[0], p.now("T"))
                    cast_ev[kt] = p.inc(vv.tensor_copy(
                        wxb[:, kt * D_INNER:(kt + 1) * D_INNER],
                        wstg[kt % 2][:, 0:D_INNER]), "V")
                p.V(v)
            ev_wz = cast_ev[7]

            zrot = self.sbt(ph, "zrot", [128, 1024], F32)
            zsev = {}
            zgev = {}
            for zst in range(4 if P1LVL >= 4 else 0):
                zc0 = OWN0 + zst * 512
                for nt in range(16):
                    gi = zst * 16 + nt
                    zev = [None]
                    def t(tt, zst=zst, nt=nt, zev=zev, gi=gi):
                        if gi == 0:
                            p.wait(tt, ev_wz)
                        if gi >= 2:
                            p.wait(tt, zsev[gi - 2])
                        bk = (nt % 2) * 512
                        for kt in range(8):
                            tt.matmul(ps_ip[:, bk:bk + 512],
                                      wxb[:, kt * D_INNER + nt * 128:
                                          kt * D_INNER + nt * 128 + 128],
                                      self.h2[:, kt * XW + zc0:kt * XW + zc0 + 512],
                                      start=(kt == 0), stop=(kt == 7))
                        zev[0] = p.incl(tt, 'T')
                    p.T(t)
                    def s_(ss, zst=zst, nt=nt, zev=zev, gi=gi):
                        p.wait(ss, zev[0])
                        if gi >= 2:
                            p.wait(ss, zgev[gi - 2])
                        zsev[gi] = p.inc(ss.activation(
                            out=zrot[:, (nt % 2) * 512:(nt % 2 + 1) * 512],
                            in_=ps_ip[:, (nt % 2) * 512:(nt % 2) * 512 + 512],
                            func=AF.Silu), "S")
                    p.S(s_)
                    def g(gg, zst=zst, nt=nt, gi=gi):
                        p.wait(gg, zsev[gi])
                        zgev[gi] = p.inc(gg.dma_start(
                            out=self.sp_zs[d][nt * 128:(nt + 1) * 128,
                                              zst * 512:(zst + 1) * 512],
                            in_=zrot[:, (nt % 2) * 512:(nt % 2 + 1) * 512]), "SP")
                    p.G(g)

            rowsl = slice(0, 64) if d == "f" else slice(64, 128)
            def gsend(gg):
                p.wait(gg, p.now("V"))
                p.inc(gg.dma_start(out=self.cc1_in[rowsl, 0:D_INNER],
                                   in_=self.s_state[:, sdir]), "SP")
            p.G(gsend)

            if f"xbc_{d}" in self.dbg:
                self.dump(f"xbc_{d}", XBC, 17, ST1, src_width=ST1)
            p.barrier()

    def dump_dram(self, name, src, shape):
        d = self.mkdbg(name, list(shape))
        if d is None:
            return
        p = self
        ph = contextlib.ExitStack()
        with ph:
            bb = self.sbt(ph, "dbg_b_" + name, [128, shape[1]], F32)
            nrt = (shape[0] + 127) // 128
            for rt in range(nrt):
                r0 = rt * 128
                r1 = min(shape[0], r0 + 128)
                ev = [None]
                def g1(gg, r0=r0, r1=r1, ev=ev):
                    p.wait(gg, p.now("SP"), p.now("V"), p.now("S"), p.now("T"))
                    ev[0] = p.inc(gg.dma_start(out=bb[0:r1 - r0, :],
                                               in_=src[r0:r1, :]), "SP")
                p.G(g1)
                def g2(gg, r0=r0, r1=r1, ev=ev):
                    p.wait(gg, ev[0])
                    p.inc(gg.dma_start(out=d[r0:r1, :], in_=bb[0:r1 - r0, :]), "SP")
                p.G(g2)
            self.barrier()

    def x1_issue(self, stream):
        p = self
        def g(gg):
            p.wait(gg, p.now("V"))
            p.inc(gg.dma_start(
                out=self.cc1_in[:, 2048:2056],
                in_=stream[:].rearrange("p (a b) -> p a b", b=SEG)[:, :, SEG - 1:SEG],
                allow_slow_non_contiguous=True), "SP")
            gg.wait_ge(p.sem["SP"], p.cnt["SP"])
            p.inc(gg.collective_compute(
                "AllGather", ALU.bypass, replica_groups=PAIRS,
                ins=[self.cc1_in[:, :]], outs=[self.cc1_out[:, :]]), "G")
        p.G(g)
        self.ev_x1 = p.now("G")

    def x1_consume(self, ph):
        p = self
        tf = [self.sbt(ph, f"x1tf{i}", [64, D_INNER], F32) for i in range(2)]
        tr = [self.sbt(ph, f"x1tr{i}", [64, D_INNER], F32) for i in range(2)]
        tm = [self.sbt(ph, f"x1tm{i}", [128, 8], F32) for i in range(2)]
        ev = [None]
        def g(gg):
            p.wait(gg, self.ev_x1)
            for sl, base in ((0, 0), (1, 128)):
                gg.dma_start(out=tf[sl][:],
                             in_=self.cc1_out[base:base + 64, 0:D_INNER]
                             ).then_inc(p.sem["SP"], 16)
                gg.dma_start(out=tr[sl][:],
                             in_=self.cc1_out[base + 64:base + 128, 0:D_INNER]
                             ).then_inc(p.sem["SP"], 16)
                gg.dma_start(out=tm[sl][:],
                             in_=self.cc1_out[base:base + 128, 2048:2056]
                             ).then_inc(p.sem["SP"], 16)
            p.cnt["SP"] += 96
            ev[0] = p.now("SP")
        p.G(g)
        x1c_sec = int(os.environ.get("KB_X1C_SEC", "99"))
        if x1c_sec <= 1:
            return
        def v(vv):
            p.wait(vv, ev[0])
            F = self.flags_sb
            vv.tensor_scalar(out=self.s_state[:, 0:D_INNER], in0=tf[0][:],
                             scalar1=F[0:64, 2:3], scalar2=None, op0=ALU.mult)
            vv.scalar_tensor_tensor(
                out=self.s_state[:, 0:D_INNER], in0=tf[1][:],
                scalar=F[0:64, 3:4], in1=self.s_state[:, 0:D_INNER],
                op0=ALU.mult, op1=ALU.add)
            vv.tensor_scalar(out=self.s_state[:, D_INNER:], in0=tr[0][:],
                             scalar1=F[0:64, 4:5], scalar2=None, op0=ALU.mult)
            vv.scalar_tensor_tensor(
                out=self.s_state[:, D_INNER:], in0=tr[1][:],
                scalar=F[0:64, 5:6], in1=self.s_state[:, D_INNER:],
                op0=ALU.mult, op1=ALU.add)
            vv.tensor_scalar(out=tm[0][:], in0=tm[0][:],
                             scalar1=F[:, 2:3], scalar2=None, op0=ALU.mult)
            p.inc(vv.scalar_tensor_tensor(
                out=tm[0][:], in0=tm[1][:], scalar=F[:, 3:4],
                in1=tm[0][:], op0=ALU.mult, op1=ALU.add), "V")
        p.V(v)
        if x1c_sec <= 2:
            return
        t0 = tm[0]
        def g2(gg):
            p.wait(gg, p.now("V"))
            for rt in range(8):
                p.inc(gg.dma_start(
                    out=self.sp_s1[rt * 128:(rt + 1) * 128, 0:1],
                    in_=t0[:, rt:rt + 1],
                    allow_slow_non_contiguous=True), "SP")
        p.G(g2)

    def p3(self, d, rev):
        p, nc = self, self.nc
        sdir = slice(0, D_INNER) if d == "f" else slice(D_INNER, 2 * D_INNER)
        STP = 256
        ph = contextlib.ExitStack()
        with ph:
            zs = self.sbt(ph, "zs", [128, 16 * STP], F32)
            xD = self.sbt(ph, "xD", [128, 16 * STP], BF16)
            xt0 = self.sbt(ph, "xt0", [128, D_INNER], BF16)
            xt = [xt0, xt0]
            Bc = self.sbt(ph, "Bc", [64, STP], BF16)
            Cc = self.sbt(ph, "Cc", [64, STP], BF16)
            ctl = [self.sbt(ph, f"ctl{i}", [128, 64], F32) for i in range(2)]
            CLB = self.sbt(ph, "CLB", [128, NHEADS * Q], F32)
            gmb = self.sbt(ph, "gmb", [64, NHEADS * Q], F32)
            tprime = CLB
            Ee = self.sbt(ph, "Ee", [128, NHEADS * Q], BF16)
            Sreal = self.sbt(ph, "Sreal", [128, NHEADS * Q], BF16)
            Svirt = self.sbt(ph, "Svirt", [64, NHEADS * Q], BF16)
            Gm = self.sbt(ph, "Gm", [128, Q], F32)
            y_sb = self.sbt(ph, "y_sb", [128, 16 * STP], F32)
            sadd20 = self.sbt(ph, "sadd20", [64, D_INNER], F32)
            s_bf = self.sbt(ph, "s_bf", [64, D_INNER], BF16)
            sadd2 = [sadd20, sadd20]
            dtile = self.sbt(ph, "dtile", [64, D_INNER], F32)
            gbf = self.sbt(ph, "gbf", [128, 16 * STP], BF16)
            woutb = self.sbt(ph, "woutb", [128, 16 * 1024], BF16)
            guardhi = self.sbt(ph, "guardhi", [128, 1024], BF16)
            wstg = [self.sbt(ph, f"wstg3{i}", [128, 1024], F32) for i in range(2)]
            rnrow = self.sbt(ph, "rnrow", [1, STP], F32)
            rnb = self.sbt(ph, "rnb", [128, STP], F32)
            oacc = self.sbt(ph, "oacc", [128, STP], F32)
            ps_y = self.pst(ph, "ps_y", [128, 2048])
            ps_g = self.pst(ph, "ps_g", [128, 512])
            ps_r = self.pst(ph, "ps_r", [1, 512])
            ps_o = self.pst(ph, "ps_o", [128, 1024])

            rmsw_sb = self.sbt(ph, "rmsw_sb", [128, 16], F32)
            cast_ev = [None] * 16
            def y0(yq):
                p.inc(yq.dma_start(out=rmsw_sb[:], in_=self.rmsw[d][:, :]), "LD")
            p.Y(y0)
            ev_rm = p.now("LD")
            for kt in range(16):
                ldev = [None]
                def y(yq, kt=kt, ldev=ldev):
                    if kt >= 2:
                        p.wait(yq, cast_ev[kt - 2])
                    ldev[0] = p.inc(yq.dma_start(
                        out=wstg[kt % 2][:],
                        in_=self.wout[d][kt * 128:(kt + 1) * 128, :]), "LD")
                p.Y(y)
                def v(vv, kt=kt, ldev=ldev):
                    p.wait(vv, ldev[0], ev_rm)
                    cast_ev[kt] = p.inc(vv.tensor_scalar(
                        out=woutb[:, kt * 1024:(kt + 1) * 1024],
                        in0=wstg[kt % 2][:], scalar1=rmsw_sb[:, kt:kt + 1],
                        scalar2=None, op0=ALU.mult), "V")
                p.V(v)
            ev_wo = cast_ev[15]
            dwb0 = self.mkdbg(f"woutb0_{d}", [128, 16 * 1024])
            if dwb0 is not None:
                def gdwb0(gg, dwb0=dwb0):
                    p.wait(gg, ev_wo)
                    p.inc(gg.dma_start(out=dwb0[:, :], in_=woutb[:]), "SP")
                p.G(gdwb0)
                p.barrier()

            p3s = int(os.environ.get("KB_P3_SEC", "99"))
            prev_ep = {"v": None, "t": None}
            sts = list(range(8)) if not rev else list(range(7, -1, -1))
            chain_ev = [p.now("V")]
            ldch = {}
            prev = {"yeev": (0, 0), "yev": (0, 0), "eev": (0, 0),
                    "svev": (0, 0), "goev": (0, 0)}
            for sti, st in enumerate(sts):
                scol = slice(st * STP, (st + 1) * STP)
                ldst = [None]
                def yst(yq, st=st, ldst=ldst):
                    if sti > 0:
                        if prev_ep["v"] is not None:
                            p.wait(yq, prev_ep["v"], prev_ep["t"], p.now("S"))
                        else:
                            p.wait(yq, p.now("V"), p.now("T"), p.now("S"))
                    for ct in range(16):
                        yq.dma_start(
                            out=zs[:, ct * STP:(ct + 1) * STP],
                            in_=self.sp_zs[d][ct * 128:(ct + 1) * 128, scol]
                        ).then_inc(p.sem["LD"], 16)
                        yq.dma_start(
                            out=xD[:, ct * STP:(ct + 1) * STP],
                            in_=self.sp_xD[d][ct * 128:(ct + 1) * 128, scol]
                        ).then_inc(p.sem["LD"], 16)
                    yq.dma_start(out=Bc[:], in_=self.sp_B[d][:, scol]).then_inc(
                        p.sem["LD"], 16)
                    yq.dma_start(out=Cc[:], in_=self.sp_C[d][:, scol]).then_inc(
                        p.sem["LD"], 16)
                    p.cnt["LD"] += 34 * 16
                    ldst[0] = p.now("LD")
                p.Y(yst)
                if p3s <= 1:
                    continue

                qs_l = [0, 1] if not rev else [1, 0]
                for qi in qs_l:
                    cg = st * 2 + qi
                    ldq = [None]
                    def yq_(yq, cg=cg, qi=qi, ldq=ldq):
                        p.wait(yq, chain_ev[0])
                        yq.dma_start(out=xt[qi % 2][:],
                                     in_=self.sp_xT[d][cg * Q:(cg + 1) * Q, :]
                                     ).then_inc(p.sem["LD"], 16)
                        yq.dma_start(out=ctl[qi % 2][:],
                                     in_=self.sp_ctl[d][cg * Q:(cg + 1) * Q, :]
                                     ).then_inc(p.sem["LD"], 16)
                        yq.dma_start(out=sadd2[qi % 2][:],
                                     in_=self.sp_sadd[d][:, cg * D_INNER:(cg + 1) * D_INNER]
                                     ).then_inc(p.sem["LD"], 16)
                        yq.dma_start(out=CLB[:], in_=rep_outer(
                            self.sp_cl[d][:, cg * Q:(cg + 1) * Q].transpose([1, 0])
                            if False else self.sp_cl[d][0:1, 0:1], 1)
                        ).then_inc(p.sem["LD"], 16) if False else None
                        p.cnt["LD"] += 48
                        ldq[0] = p.now("LD")
                    p.Y(yq_)
                    ldb = [None]
                    def gq(gg, cg=cg, ldb=ldb):
                        p.wait(gg, p.now("V"), prev["eev"])
                        src = self.sp_cl[d].rearrange(
                            "h (c i) -> h c i", i=Q)[:, cg, :]
                        src = src.rearrange(
                            "h (a i) -> a h i", a=1).broadcast_to([128, 32, Q])
                        gg.dma_start(out=CLB[:].rearrange("p (h i) -> p h i", i=Q),
                                     in_=src).then_inc(p.sem["SP"], 16)
                        src2 = self.sp_gam[d].rearrange(
                            "h (c i) -> h c i", i=Q)[:, cg, :]
                        src2 = src2.rearrange("h (a i) -> a h i", a=1).broadcast_to([64, 32, Q])
                        gg.dma_start(out=gmb[:].rearrange("p (h i) -> p h i", i=Q),
                                     in_=src2).then_inc(p.sem["SP"], 16)
                        gg.dma_start(
                            out=dtile[:], in_=self.sp_dr[d][cg:cg + 1, :]
                            .broadcast_to([64, D_INNER])
                        ).then_inc(p.sem["SP"], 16)
                        p.cnt["SP"] += 48
                        ldb[0] = p.now("SP")
                    p.G(gq)
                    if p3s <= 2:
                        continue

                    gev = [None]
                    def tg(tt, qi=qi, gev=gev, ldst=ldst):
                        p.wait(tt, ldst[0], prev.get("gsev", (0, 0)))
                        gev[0] = p.inc(tt.matmul(
                            ps_g[:, 0:Q], Bc[:, qi * Q:(qi + 1) * Q],
                            Cc[:, qi * Q:(qi + 1) * Q], start=True, stop=True), "T")
                    p.T(tg)
                    gsev = [None]
                    def sg2(ss, gev=gev, gsev=gsev):
                        p.wait(ss, gev[0], prev["svev"])
                        gsev[0] = p.inc(ss.copy(Gm[:], ps_g[:, 0:Q]), "S")
                    p.S(sg2)
                    gmev = [None]
                    def gg2(gg, gsev=gsev, gmev=gmev):
                        p.wait(gg, gsev[0])
                        if not rev:
                            gmev[0] = p.inc(gg.affine_select(
                                out=Gm[:], in_=Gm[:], pattern=[[1, Q]],
                                compare_op=ALU.is_ge, fill=0.0, base=0,
                                channel_multiplier=-1), "G")
                        else:
                            gmev[0] = p.inc(gg.affine_select(
                                out=Gm[:], in_=Gm[:], pattern=[[-1, Q]],
                                compare_op=ALU.is_ge, fill=0.0, base=0,
                                channel_multiplier=1), "G")
                    p.G(gg2)

                    vtev = [None]
                    def vt(vv, qi=qi, ldq=ldq, ldb=ldb, vtev=vtev):
                        p.wait(vv, ldq[0], ldb[0], prev["eev"])
                        cfT = rep_inner(ctl[qi % 2][:, 0:32], Q)
                        clb3 = CLB[:].rearrange("p (h i) -> p h i", i=Q)
                        tp3 = tprime[:].rearrange("p (h i) -> p h i", i=Q)
                        if not rev:
                            vv.tensor_tensor(out=tp3, in0=clb3, in1=cfT,
                                             op=ALU.subtract)
                        else:
                            vv.tensor_tensor(out=tp3, in0=cfT, in1=clb3,
                                             op=ALU.subtract)
                        vtev[0] = p.inc(vv.tensor_tensor(
                            out=tp3, in0=tp3,
                            in1=rep_inner(ctl[qi % 2][:, 32:64], Q),
                            op=ALU.min), "V")
                    p.V(vt)
                    eev = [None]
                    def se(ss, vtev=vtev, eev=eev):
                        p.wait(ss, vtev[0])
                        eev[0] = p.inc(ss.activation(out=Ee[:], in_=tprime[:],
                                                     func=AF.Exp), "S")
                    p.S(se)
                    svev = [None]
                    def vs(vv, gmev=gmev, eev=eev, ldb=ldb, svev=svev):
                        p.wait(vv, gmev[0], eev[0])
                        vv.tensor_tensor(
                            out=Sreal[:].rearrange("p (h i) -> p h i", i=Q),
                            in0=rep_outer(Gm[:], NHEADS),
                            in1=Ee[:].rearrange("p (h i) -> p h i", i=Q),
                            op=ALU.mult)
                        svev[0] = p.inc(vv.tensor_tensor(
                            out=Svirt[:].rearrange("p (h i) -> p h i", i=Q),
                            in0=rep_outer(Cc[0:64, (0 if True else 0):Q] if False
                                          else Cc[:, 0:Q], NHEADS) if False else
                            rep_outer(Cc[:, 0:Q], NHEADS),
                            in1=gmb[:].rearrange("p (h i) -> p h i", i=Q),
                            op=ALU.mult), "V")
                    def vs_fixed(vv, qi=qi, gmev=gmev, eev=eev, ldb=ldb, svev=svev):
                        p.wait(vv, gmev[0], eev[0], ldb[0], prev["yev"])
                        vv.tensor_tensor(
                            out=Sreal[:].rearrange("p (h i) -> p h i", i=Q),
                            in0=rep_outer(Gm[:], NHEADS),
                            in1=Ee[:].rearrange("p (h i) -> p h i", i=Q),
                            op=ALU.mult)
                        vv.tensor_tensor(
                            out=Svirt[:].rearrange("p (h i) -> p h i", i=Q),
                            in0=rep_outer(Cc[:, qi * Q:(qi + 1) * Q], NHEADS),
                            in1=gmb[:].rearrange("p (h i) -> p h i", i=Q),
                            op=ALU.mult)
                        p.wait(vv, chain_ev[0])
                        svev[0] = p.inc(vv.tensor_copy(
                            s_bf[:], self.s_state[:, sdir]), "V")
                    p.V(vs_fixed)
                    if p3s <= 3:
                        continue

                    if cg == 0 and d == "f":
                        dsr = self.mkdbg("sreal0", [128, NHEADS * Q])
                        dclb = self.mkdbg("clb0", [128, NHEADS * Q])
                        dgm = self.mkdbg("gm0", [128, Q])
                        dxt = self.mkdbg("xt0d", [128, D_INNER])
                        if dsr is not None:
                            def gdsr(gg):
                                p.wait(gg, p.now("V"), p.now("S"), p.now("G"))
                                p.inc(gg.dma_start(out=dsr[:, :], in_=Sreal[:]), "SP")
                                p.inc(gg.dma_start(out=dclb[:, :], in_=CLB[:]), "SP")
                                p.inc(gg.dma_start(out=dgm[:, :], in_=Gm[:]), "SP")
                                p.inc(gg.dma_start(out=dxt[:, :], in_=xt[qi % 2][:]), "SP")
                            p.G(gdsr)
                            p.barrier()
                    yev = [None, None]
                    yeev = [None, None]
                    for grp in range(2):
                        def tyg(tt, qi=qi, grp=grp):
                            if grp == 0:
                                p.wait(tt, svev[0], ldq[0], chain_ev[0],
                                       prev["yeev"])
                            else:
                                p.wait(tt, yeev[0])
                            novirt = bool(os.environ.get("KB_P3_NOVIRT"))
                            for hp in range(grp * 8, grp * 8 + 8):
                                st_ap = xt[qi % 2][:, hp * 128:(hp + 1) * 128]
                                hl = hp % 8
                                first = (hl % 4 == 0)
                                last = (hl % 4 == 3)
                                for hh in range(2):
                                    h = hp * 2 + hh
                                    pslice = ps_y[
                                        hh * 64:(hh + 1) * 64,
                                        hh * 1024 + hl * 128:
                                        hh * 1024 + hl * 128 + 128]
                                    tt.matmul(
                                        pslice,
                                        st_ap[:, hh * 64:(hh + 1) * 64],
                                        Sreal[:, h * Q:(h + 1) * Q],
                                        start=first,
                                        stop=(last and novirt),
                                        tile_position=(0, hh * 64))
                                    if not novirt:
                                        tt.matmul(
                                            pslice,
                                            s_bf[:, h * 64:(h + 1) * 64],
                                            Svirt[:, h * Q:(h + 1) * Q],
                                            start=False, stop=last,
                                            tile_position=(0, hh * 64))
                            yev[grp] = p.incl(tt, 'T')
                        p.T(tyg)

                        def vyg(vv, qi=qi, grp=grp):
                            p.wait(vv, yev[grp])
                            for hh in range(2):
                                hps = slice(grp * 8, grp * 8 + 8)
                                xAP = xD[hh * 64:(hh + 1) * 64, :].rearrange(
                                    "p (c w) -> p c w", w=STP)[
                                    :, hps, qi * Q:(qi + 1) * Q]
                                yAP = y_sb[hh * 64:(hh + 1) * 64, :].rearrange(
                                    "p (c w) -> p c w", w=STP)[
                                    :, hps, qi * Q:(qi + 1) * Q]
                                pAP = ps_y[hh * 64:(hh + 1) * 64,
                                           hh * 1024:hh * 1024 + 1024].rearrange(
                                    "p (c w) -> p c w", w=128)
                                if os.environ.get("KB_P3_NOD"):
                                    vv.scalar_tensor_tensor(
                                        out=yAP, in0=pAP, scalar=1.0, in1=pAP,
                                        op0=ALU.mult, op1=ALU.bypass)
                                else:
                                    vv.scalar_tensor_tensor(
                                        out=yAP, in0=xAP, scalar=1.0, in1=pAP,
                                        op0=ALU.mult, op1=ALU.add)
                            yeev[grp] = p.incl(vv, 'V')
                        p.V(vyg)

                    def vchain(vv, qi=qi, ldb=ldb, yev=yev):
                        p.wait(vv, ldb[0], yev[1])
                        vv.tensor_tensor(out=self.s_state[:, sdir],
                                         in0=self.s_state[:, sdir],
                                         in1=dtile[:], op=ALU.mult)
                        chain_ev[0] = p.inc(vv.tensor_tensor(
                            out=self.s_state[:, sdir], in0=self.s_state[:, sdir],
                            in1=sadd2[qi % 2][:], op=ALU.add), "V")
                    p.V(vchain)
                    ldch[qi % 2] = chain_ev[0]
                    prev["yeev"] = yeev[1]
                    prev["yev"] = yev[1]
                    prev["eev"] = eev[0]
                    prev["svev"] = svev[0]
                    prev["gsev"] = gsev[0]

                if p3s <= 4:
                    continue
                dy = self.mkdbg(f"y_{d}", [D_INNER, TOK])
                if dy is not None:
                    def gdy(gg, st=st, dy=dy):
                        p.wait(gg, p.now("V"))
                        for ct in range(16):
                            p.inc(gg.dma_start(
                                out=dy[ct * 128:(ct + 1) * 128,
                                       st * STP:(st + 1) * STP],
                                in_=y_sb[:, ct * STP:(ct + 1) * STP]), "SP")
                    p.G(gdy)
                    p.barrier()
                def vgate(vv, st=st):
                    p.wait(vv, p.now("V"), p.now("T"))
                    vv.tensor_tensor(out=y_sb[:], in0=y_sb[:], in1=zs[:],
                                     op=ALU.mult)
                    prev_ep["v"] = p.inc(vv.tensor_copy(gbf[:], y_sb[:]), "V")
                p.V(vgate)
                dg = self.mkdbg(f"gate_{d}", [D_INNER, TOK])
                if dg is not None:
                    def gdg(gg, st=st, dg=dg):
                        p.wait(gg, p.now("V"))
                        for ct in range(16):
                            p.inc(gg.dma_start(
                                out=dg[ct * 128:(ct + 1) * 128,
                                       st * STP:(st + 1) * STP],
                                in_=y_sb[:, ct * STP:(ct + 1) * STP]), "SP")
                    p.G(gdg)
                    p.barrier()
                def sgate(ss, st=st):
                    p.wait(ss, p.now("V"))
                    p.inc(ss.activation(out=xD[:], in_=y_sb[:], func=AF.Square), "S")
                p.S(sgate)
                def trms(tt, st=st):
                    p.wait(tt, p.now("S"))
                    for ct in range(16):
                        tt.matmul(ps_r[:, 0:STP], self.ones_bf[:],
                                  xD[:, ct * STP:(ct + 1) * STP],
                                  start=(ct == 0), stop=(ct == 15))
                    prev_ep["t"] = p.incl(tt, 'T')
                p.T(trms)
                def srms(ss, st=st):
                    p.wait(ss, p.now("T"))
                    ss.activation(out=rnrow[:], in_=ps_r[0:1, 0:STP], func=AF.Ln,
                                  scale=self.cc(self.C_IDIM)[0:1, :],
                                  bias=self.cc(self.C_EPS)[0:1, :])
                    p.inc(ss.activation(out=rnrow[:], in_=rnrow[:], func=AF.Exp,
                                        scale=self.cc(self.C_NH)[0:1, :]), "S")
                p.S(srms)
                def grn(gg, st=st):
                    p.wait(gg, p.now("S"))
                    gg.dma_start(out=self.sp_rows[8:9, 0:STP], in_=rnrow[:]
                                 ).then_inc(p.sem["SP"], 16)
                    p.cnt["SP"] += 16
                    gg.wait_ge(p.sem["SP"], p.cnt["SP"])
                    p.inc(gg.dma_start(out=rnb[:], in_=self.sp_rows[
                        8:9, 0:STP].broadcast_to([128, STP])), "SP")
                p.G(grn)
                ev_rnb = p.now("SP")
                dr_ = self.mkdbg(f"rnb_{d}", [128, 8 * STP])
                if dr_ is not None:
                    def gdr(gg, st=st, dr_=dr_):
                        p.wait(gg, ev_rnb)
                        p.inc(gg.dma_start(
                            out=dr_[:, st * STP:(st + 1) * STP],
                            in_=rnb[:]), "SP")
                    p.G(gdr)
                    p.barrier()

                if p3s <= 5:
                    continue
                for ont in range(8):
                    oev = [None]
                    def to(tt, ont=ont, st=st, oev=oev):
                        if ont == 0:
                            p.wait(tt, ev_wo, p.now("V"))
                        if ont >= 2:
                            p.wait(tt, p.now("SP"))
                        bk = (ont % 2) * 512
                        for kt in range(16):
                            tt.matmul(ps_o[:, bk:bk + STP],
                                      woutb[:, kt * 1024 + ont * 128:
                                            kt * 1024 + ont * 128 + 128],
                                      gbf[:, kt * STP:(kt + 1) * STP],
                                      start=(kt == 0), stop=(kt == 15))
                        oev[0] = p.incl(tt, 'T')
                    p.T(to)
                    sev = [None]
                    if os.environ.get("KB_P3_STOUCH"):
                        def sto(ss, ont=ont, oev=oev, sev=sev):
                            p.wait(ss, oev[0])
                            bk = (ont % 2) * 512
                            sev[0] = p.inc(ss.copy(rnrow[0:1, 0:STP],
                                                   ps_o[0:1, bk:bk + STP]), "S")
                        p.S(sto)
                    def vo(vv, ont=ont, oev=oev, sev=sev):
                        p.wait(vv, oev[0], ev_rnb, prev["goev"])
                        if sev[0] is not None:
                            p.wait(vv, sev[0])
                        bk = (ont % 2) * 512
                        p.inc(vv.tensor_tensor(out=oacc[:], in0=ps_o[:, bk:bk + STP],
                                               in1=rnb[:], op=ALU.mult), "V")
                    p.V(vo)
                    def go(gg, ont=ont, st=st):
                        p.wait(gg, p.now("V"))
                        if d == "f":
                            p.inc(gg.dma_start(
                                out=self.acc_m[ont * 128:(ont + 1) * 128, scol],
                                in_=oacc[:]), "SP")
                        else:
                            p.inc(gg.dma_start(
                                out=self.acc_m[ont * 128:(ont + 1) * 128, scol],
                                in_=oacc[:], accum_op=ALU.add), "SP")
                        prev["goev"] = p.now("SP")
                    p.G(go)
                if os.environ.get("KB_P3_OPBAR"):
                    p.barrier()
            dwb = self.mkdbg(f"woutb_{d}", [128, 16 * 1024])
            if dwb is not None:
                def gdwb(gg, dwb=dwb):
                    p.wait(gg, p.now("V"), p.now("T"))
                    p.inc(gg.dma_start(out=dwb[:, :], in_=woutb[:]), "SP")
                p.G(gdwb)
            p.barrier()

    def ffn_qkv(self, lidx, ph, QH, KH, VT, src_dram, src_dt, n_mc, kw0, tc0):
        p, nc = self, self.nc
        qsl = self.sbt(ph, "qsl", [128, 1], F32)
        ksl = self.sbt(ph, "ksl", [128, 1], F32)
        wqb = self.sbt(ph, "wqb", [128, 8 * 1536], BF16)
        wstq = [self.sbt(ph, f"wstq{i}", [128, 1536], F32) for i in range(2)]
        sbf = self.sbt(ph, "sbf", [128, 8 * 513], BF16)
        sstg = [self.sbt(ph, f"sstg{i}", [128, 513], F32) for i in range(2)]
        qkst = self.sbt(ph, "qkst", [128, 8 * 512], F32)
        vstg = self.sbt(ph, "vstg", [128, 4 * 512], BF16)
        sqb = self.sbt(ph, "sqb", [128, 512], BF16)
        nrw = self.sbt(ph, "nrw", [2, 2048], F32)
        invb = self.sbt(ph, "invb", [128, 512], F32)
        tq = self.sbt(ph, "tq", [128, 512], F32)
        tperm = self.sbt(ph, "tperm", [128, 512], F32)
        r1 = self.sbt(ph, "r1", [128, 512], F32)
        rtb = self.sbt(ph, "rtb", [128, 4 * 512], F32)
        ps_qk = self.pst(ph, "ps_qk", [128, 1024])
        ps_n = self.pst(ph, "ps_n", [2, 2048])
        ps_vt = self.pst(ph, "ps_vt", [128, 2048], BF16)

        def ysc(yq):
            yq.dma_start(out=qsl[:], in_=self.qsc[lidx, :, :]).then_inc(
                p.sem["LD"], 16)
            yq.dma_start(out=ksl[:], in_=self.ksc[lidx, :, :]).then_inc(
                p.sem["LD"], 16)
            p.cnt["LD"] += 32
        p.Y(ysc)
        ev_sc = p.now("LD")

        cast_ev = [None] * 8
        for kt in range(8):
            ldev = [None]
            def y(yq, kt=kt, ldev=ldev):
                if kt >= 2:
                    p.wait(yq, cast_ev[kt - 2])
                ldev[0] = p.inc(yq.dma_start(
                    out=wstq[kt % 2][:],
                    in_=self.wqkv[lidx, kt * 128:(kt + 1) * 128, :]), "LD")
            p.Y(y)
            def v(vv, kt=kt, ldev=ldev):
                p.wait(vv, ldev[0])
                cast_ev[kt] = p.inc(vv.tensor_copy(
                    wqb[:, kt * 1536:(kt + 1) * 1536], wstq[kt % 2][:]), "V")
            p.V(v)
        ev_wq = cast_ev[7]

        qkv_sec = int(os.environ.get("KB_QKV_SEC", "99"))
        prevm = {"stg": ("V", 0), "use": [("T", 0)] * 8}
        for m in range(n_mc):
            if qkv_sec == 0:
                break
            kw = kw0 + m
            stev = [None] * 8
            for rt in range(8):
                ldev = [None]
                def y(yq, m=m, rt=rt, ldev=ldev):
                    p.wait(yq, prevm["use"][rt])
                    if src_dram.dtype == F32 and rt >= 2 and stev[rt - 2]:
                        p.wait(yq, stev[rt - 2])
                    c0 = src_dt(m)
                    ldev[0] = p.inc(yq.dma_start(
                        out=sstg[rt % 2][:] if src_dram.dtype == F32 else
                        sbf[:, rt * 513:(rt + 1) * 513],
                        in_=src_dram[rt * 128:(rt + 1) * 128, c0:c0 + 513]), "LD")
                p.Y(y)
                if src_dram.dtype == F32:
                    def v(vv, rt=rt, ldev=ldev):
                        p.wait(vv, ldev[0])
                        stev[rt] = p.inc(vv.tensor_copy(
                            sbf[:, rt * 513:(rt + 1) * 513], sstg[rt % 2][:]), "V")
                    p.V(v)
                else:
                    stev[rt] = ldev[0]
            rtev = [None]
            def yr(yq, m=m, rtev=rtev):
                p.wait(yq, prevm["stg"])
                t0 = tc0(m)
                for i in range(4):
                    yq.dma_start(out=rtb[:, i * 512:(i + 1) * 512],
                                 in_=self.rot[i, :, t0:t0 + 512]).then_inc(
                        p.sem["LD"], 16)
                p.cnt["LD"] += 64
                rtev[0] = p.now("LD")
            p.Y(yr)

            if qkv_sec <= 1:
                prevm["stg"] = p.now("V")
                continue
            ntev = {}
            for nt in range(12):
                def t(tt, m=m, nt=nt):
                    if nt < 2:
                        p.wait(tt, ev_wq, *stev)
                        p.wait(tt, p.now("V"), p.now("S"))
                    else:
                        p.wait(tt, ntev[nt - 2][1])
                    bk = (nt % 2) * 512
                    for kt in range(8):
                        off = 1 if kt < 4 else 0
                        tt.matmul(ps_qk[:, bk:bk + 512],
                                  wqb[:, kt * 1536 + nt * 128:
                                      kt * 1536 + nt * 128 + 128],
                                  sbf[:, rt_off(kt, off)],
                                  start=(kt == 0), stop=(kt == 7))
                    ntev[nt] = [p.incl(tt, 'T'), None]
                def rt_off(kt, off):
                    return slice(kt * 513 + off, kt * 513 + off + 512)
                p.T(t)
                def s_(ss, m=m, nt=nt):
                    p.wait(ss, ntev[nt][0])
                    bk = (nt % 2) * 512
                    if nt < 8:
                        e = p.inc(ss.copy(qkst[:, nt * 512:(nt + 1) * 512],
                                          ps_qk[:, bk:bk + 512]), "S")
                    else:
                        e = p.inc(ss.activation(
                            out=vstg[:, (nt - 8) * 512:(nt - 7) * 512],
                            in_=ps_qk[:, bk:bk + 512], func=AF.Copy), "S")
                    ntev[nt][1] = e
                p.S(s_)
            ev_qkv = p.now("S")

            if qkv_sec <= 2:
                prevm["stg"] = p.now("V")
                prevm["use"] = [ntev[11][0]] * 8
                continue
            vt_evs = []
            vt_n = int(os.environ.get("KB_VT_N", "16"))
            for vt4 in range(4):
                for blk in range(4):
                    if vt4 * 4 + blk >= vt_n:
                        continue
                    tev = [None]
                    pb = (len(vt_evs) % 2) * 1024
                    def t(tt, m=m, vt4=vt4, blk=blk, tev=tev, pb=pb):
                        p.wait(tt, ntev[8 + vt4][1])
                        if len(vt_evs) >= 2:
                            p.wait(tt, vt_evs[-2])
                        tev[0] = p.inc(tt.transpose(
                            ps_vt[:, pb:pb + 128],
                            vstg[:, vt4 * 512 + blk * 128:
                                 vt4 * 512 + blk * 128 + 128],
                            self.ident_bf[:]), "T")
                    p.T(t)
                    def v(vv, m=m, vt4=vt4, blk=blk, tev=tev, pb=pb):
                        p.wait(vv, tev[0])
                        ktg = (kw0 + m) * 4 + blk
                        vt_evs.append(p.inc(vv.tensor_copy(
                            VT[:, ktg * 512 + vt4 * 128:ktg * 512 + vt4 * 128 + 128],
                            ps_vt[:, pb:pb + 128]), "V"))
                    p.V(v)

            if qkv_sec <= 3:
                prevm["stg"] = p.now("V")
                prevm["use"] = [ntev[11][0]] * 8
                continue
            use_evs = [None] * 8
            for iq, isq in ((0, False), (1, True)):
                if iq == 1 and kw == 0:
                    continue
                for t4 in range(4):
                    base = (4 if iq == 0 else 0) + t4
                    ti = (0 + t4) if iq == 1 else (4 + t4)
                    sq_ev = [None]
                    def s_(ss, m=m, ti=ti, sq_ev=sq_ev):
                        p.wait(ss, ev_qkv, p.now("V"))
                        sq_ev[0] = p.inc(ss.activation(
                            out=sqb[:], in_=qkst[:, ti * 512:(ti + 1) * 512],
                            func=AF.Square), "S")
                    p.S(s_)
                    nev = [None]
                    def t(tt, m=m, ti=ti, t4=t4, sq_ev=sq_ev, nev=nev):
                        p.wait(tt, sq_ev[0])
                        nev[0] = p.inc(tt.matmul(
                            ps_n[:, t4 * 512:(t4 + 1) * 512], self.blk2[:],
                            sqb[:], start=True, stop=True), "T")
                    p.T(t)
                    sinv = [None]
                    def s2(ss, m=m, t4=t4, nev=nev, sinv=sinv):
                        p.wait(ss, nev[0])
                        sl = slice(t4 * 512, (t4 + 1) * 512)
                        ss.activation(out=nrw[:, sl], in_=ps_n[:, sl], func=AF.Ln,
                                      bias=self.cc(self.C_TINY)[0:2, :])
                        sinv[0] = p.inc(ss.activation(
                            out=nrw[:, sl], in_=nrw[:, sl], func=AF.Exp,
                            scale=self.cc(self.C_NH)[0:2, :]), "S")
                    p.S(s2)
                    vcl = [None]
                    def v0(vv, t4=t4, sinv=sinv, vcl=vcl):
                        p.wait(vv, sinv[0])
                        sl = slice(t4 * 512, (t4 + 1) * 512)
                        vcl[0] = p.inc(vv.tensor_scalar_min(
                            nrw[:, sl], nrw[:, sl], 1e12), "V")
                    p.V(v0)
                    bev = [None]
                    def g(gg, m=m, t4=t4, vcl=vcl, bev=bev):
                        p.wait(gg, vcl[0])
                        gg.dma_start(out=self.sp_rows[10:12, 0:512],
                                     in_=nrw[:, t4 * 512:(t4 + 1) * 512]
                                     ).then_inc(p.sem["SP"], 16)
                        p.cnt["SP"] += 16
                        gg.wait_ge(p.sem["SP"], p.cnt["SP"])
                        src = self.sp_rows[10:12, 0:512].rearrange(
                            "a (b w) -> a b w", b=1).broadcast_to([2, 64, 512])
                        bev[0] = p.inc(gg.dma_start(out=invb[:], in_=src), "SP")
                    p.G(g)
                    def v1(vv, m=m, iq=iq, ti=ti, t4=t4, bev=bev):
                        p.wait(vv, bev[0], rtev[0])
                        qk = qkst[:, ti * 512:(ti + 1) * 512]
                        scv = qsl if iq == 1 else ksl
                        vv.scalar_tensor_tensor(out=tq[:], in0=qk, scalar=1.0,
                                                in1=invb[:], op0=ALU.mult,
                                                op1=ALU.mult)
                        vv.tensor_scalar(out=tq[:], in0=tq[:], scalar1=scv[:, 0:1],
                                         scalar2=None, op0=ALU.mult)
                        A = rtb[:, 0:512] if iq == 1 else rtb[:, 1024:1536]
                        Bs = rtb[:, 512:1024] if iq == 1 else rtb[:, 1536:2048]
                        vv.tensor_tensor(out=r1[:], in0=tq[:], in1=A, op=ALU.mult)
                        for half in range(2):
                            b0 = half * 64
                            vv.tensor_copy(tperm[b0:b0 + 32, :], tq[b0 + 32:b0 + 64, :])
                            vv.tensor_copy(tperm[b0 + 32:b0 + 64, :], tq[b0:b0 + 32, :])
                        vv.tensor_tensor(out=tperm[:], in0=tperm[:], in1=Bs,
                                         op=ALU.mult)
                        dst = QH if iq == 1 else KH
                        dw = 2048 if iq == 1 else 2560
                        dcol = (t4 * (dw) + (m - (1 - kw0)) * 512) if iq == 1 \
                            else (t4 * dw + (kw0 + m) * 512)
                        use_evs[ti] = p.inc(vv.tensor_tensor(
                            out=dst[:, dcol:dcol + 512], in0=r1[:], in1=tperm[:],
                            op=ALU.add), "V")
                    p.V(v1)
            prevm["stg"] = p.now("V")
            prevm["use"] = [ntev[11][0]] * 8
        self.ev_qkv_done = [p.now("V"), p.now("S"), p.now("T")]

    def ffn_attn(self, lidx, ph, wins, QH, KH, VT, stream, wo_bf, AO,
                 ebuf, recipb, first_kv_mask):
        p, nc = self, self.nc
        ph = self.es_attn = contextlib.ExitStack()
        self._attn_stack = ph
        ps_sc = self.pst(ph, "ps_sc", [128, 2 * 512])
        ps_s0 = self.pst(ph, "ps_s0", [1, 512])
        ps_s1 = self.pst(ph, "ps_s1", [1, 512])
        ps_av = self.pst(ph, "ps_av", [128, 1024])
        srow0 = self.sbt(ph, "srow0", [1, 512], F32)
        srow1 = self.sbt(ph, "srow1", [1, 512], F32)

        for w in wins:
            for hp in range(4):
                mev = {}
                for blk in range(2):
                    kw = w + blk
                    for kt in range(4):
                        bi = blk * 4 + kt
                        scev = [None]
                        def t(tt, w=w, hp=hp, blk=blk, kt=kt, bi=bi, scev=scev):
                            if bi < 1:
                                p.wait(tt, *self.ev_qkv_done)
                                p.wait(tt, p.now("G"), p.now("V"))
                            if bi >= 1:
                                p.wait(tt, mev.get(bi - 1, (0, 0)))
                            kcol = hp * 2560 + (w + blk) * 512 + kt * 128
                            for hh in range(2):
                                tt.matmul(
                                    ps_sc[:, hh * 512:(hh + 1) * 512],
                                    KH[hh * 64:(hh + 1) * 64, kcol:kcol + 128],
                                    QH[hh * 64:(hh + 1) * 64,
                                       hp * 2048 + w * 512:hp * 2048 + w * 512 + 512],
                                    start=True, stop=True,
                                    tile_position=(hh * 64, 0))
                            scev[0] = p.incl(tt, 'T')
                        p.T(t)
                        eev = [None]
                        def s_(ss, hp=hp, blk=blk, kt=kt, bi=bi, scev=scev, eev=eev):
                            p.wait(ss, scev[0])
                            for hh in range(2):
                                ss.activation(
                                    out=ebuf[:, (hh * 8 + bi) * 512:
                                             (hh * 8 + bi) * 512 + 512],
                                    in_=ps_sc[:, hh * 512:(hh + 1) * 512],
                                    func=AF.Exp, bias=self.cc(self.C_M8))
                            eev[0] = p.incl(ss, 'S')
                        p.S(s_)
                        def g(gg, w=w, blk=blk, kt=kt, bi=bi, eev=eev):
                            p.wait(gg, eev[0])
                            for hh in range(2):
                                esl = ebuf[:, (hh * 8 + bi) * 512:
                                           (hh * 8 + bi) * 512 + 512]
                                if blk == 1:
                                    gg.affine_select(
                                        out=esl, in_=esl, pattern=[[1, 512]],
                                        compare_op=ALU.is_ge, fill=0.0,
                                        base=-128 * kt, channel_multiplier=-1)
                                else:
                                    gg.affine_select(
                                        out=esl, in_=esl, pattern=[[-1, 512]],
                                        compare_op=ALU.is_ge, fill=0.0,
                                        base=128 * kt, channel_multiplier=1)
                            mev[bi] = p.incl(gg, 'G')
                        p.G(g)
                if w == 0:
                    def vkm(vv, hp=hp):
                        p.wait(vv, mev[3])
                        for hh in range(2):
                            for kt in range(4):
                                bi = kt
                                esl = ebuf[:, (hh * 8 + bi) * 512:
                                           (hh * 8 + bi) * 512 + 512]
                                vv.tensor_scalar_mul(esl, esl, first_kv_mask)
                        p.incl(vv, 'V')
                    p.V(vkm)
                    ev_km = p.now("V")
                else:
                    ev_km = (0, 0)

                smev = [None]
                def t2(tt, hp=hp, smev=smev):
                    p.wait(tt, mev[7], ev_km, p.now("S"))
                    for hh, psn in ((0, ps_s0), (1, ps_s1)):
                        for bi in range(8):
                            tt.matmul(psn[:], self.ones_bf[:],
                                      ebuf[:, (hh * 8 + bi) * 512:
                                           (hh * 8 + bi) * 512 + 512],
                                      start=(bi == 0), stop=(bi == 7))
                    smev[0] = p.incl(tt, 'T')
                p.T(t2)
                srev = [None]
                def s2(ss, smev=smev, srev=srev):
                    p.wait(ss, smev[0])
                    ss.copy(srow0[0:1, :], ps_s0[:])
                    srev[0] = p.inc(ss.copy(srow1[0:1, :], ps_s1[:]), "S")
                p.S(s2)
                vrev = [None]
                def v2(vv, srev=srev, vrev=vrev):
                    p.wait(vv, srev[0])
                    vv.reciprocal(srow0[:], srow0[:])
                    vrev[0] = p.inc(vv.reciprocal(srow1[:], srow1[:]), "V")
                p.V(v2)
                bev = [None]
                def g2(gg, vrev=vrev, bev=bev):
                    p.wait(gg, vrev[0], p.now("V"))
                    gg.dma_start(out=self.sp_rows[12:13, 0:512], in_=srow0[:]
                                 ).then_inc(p.sem["SP"], 16)
                    gg.dma_start(out=self.sp_rows[13:14, 0:512], in_=srow1[:]
                                 ).then_inc(p.sem["SP"], 16)
                    p.cnt["SP"] += 32
                    gg.wait_ge(p.sem["SP"], p.cnt["SP"])
                    src = self.sp_rows[12:14, 0:512].rearrange(
                        "a (b w) -> a b w", b=1).broadcast_to([2, 64, 512])
                    bev[0] = p.inc(gg.dma_start(out=recipb[:], in_=src), "SP")
                p.G(g2)

                avev = [None]
                def t3(tt, w=w, hp=hp, avev=avev):
                    p.wait(tt, mev[7], ev_km, p.now("V"))
                    for bi in range(8):
                        blk, kt = bi // 4, bi % 4
                        ktg = (w + blk) * 4 + kt
                        for hh in range(2):
                            tt.matmul(
                                ps_av[hh * 64:(hh + 1) * 64,
                                      hh * 512:hh * 512 + 512],
                                VT[:, ktg * 512 + hp * 128 + hh * 64:
                                   ktg * 512 + hp * 128 + hh * 64 + 64],
                                ebuf[:, (hh * 8 + bi) * 512:
                                     (hh * 8 + bi) * 512 + 512],
                                start=(bi == 0), stop=(bi == 7),
                                tile_position=(0, hh * 64))
                    avev[0] = p.incl(tt, 'T')
                p.T(t3)
                def v3(vv, w=w, hp=hp, avev=avev, bev=bev):
                    p.wait(vv, avev[0], bev[0])
                    vv.scalar_tensor_tensor(
                        out=AO[0:64, hp * 2048 + w * 512:hp * 2048 + w * 512 + 512],
                        in0=ps_av[0:64, 0:512], scalar=1.0, in1=recipb[0:64, :],
                        op0=ALU.mult, op1=ALU.mult)
                    p.inc(vv.scalar_tensor_tensor(
                        out=AO[64:128, hp * 2048 + w * 512:hp * 2048 + w * 512 + 512],
                        in0=ps_av[64:128, 512:1024], scalar=1.0,
                        in1=recipb[64:128, :],
                        op0=ALU.mult, op1=ALU.mult), "V")
                p.V(v3)
            for ont in range(8):
                oev = [None]
                def t4(tt, w=w, ont=ont, oev=oev):
                    p.wait(tt, p.now("V"))
                    if ont >= 2:
                        p.wait(tt, p.now("V"))
                    bk = (ont % 2) * 512
                    for k4 in range(4):
                        tt.matmul(ps_sc[:, bk:bk + 512],
                                  wo_bf[:, k4 * 1024 + ont * 128:
                                        k4 * 1024 + ont * 128 + 128],
                                  AO[:, k4 * 2048 + w * 512:k4 * 2048 + w * 512 + 512],
                                  start=(k4 == 0), stop=(k4 == 3))
                    oev[0] = p.incl(tt, 'T')
                p.T(t4)
                def v4(vv, w=w, ont=ont, oev=oev):
                    p.wait(vv, oev[0])
                    sl = slice(ont * SEG + 1 + w * 512, ont * SEG + 1 + w * 512 + 512)
                    p.inc(vv.scalar_tensor_tensor(
                        out=stream[:, sl], in0=ps_sc[:, (ont % 2) * 512:
                                                     (ont % 2) * 512 + 512],
                        scalar=1.0, in1=stream[:, sl],
                        op0=ALU.mult, op1=ALU.add), "V")
                p.V(v4)
        self.barrier()
        ph.close()

    def mlp_ln(self, lidx, ph, stream_dram, LNIN, mcs):
        p = self
        stf = self.sbt(ph, "stf", [128, 8 * 513], F32)
        sbfm = self.sbt(ph, "sbfm", [128, 512], BF16)
        sqm = self.sbt(ph, "sqm", [128, 512], BF16)
        rows0 = self.sbt(ph, "rows_m0", [1, 512], F32)
        rows1 = self.sbt(ph, "rows_m1", [1, 512], F32)
        rows2 = self.sbt(ph, "rows_m2", [1, 512], F32)
        mub = self.sbt(ph, "mub_m", [128, 512], F32)
        rb = self.sbt(ph, "rb_m", [128, 512], F32)
        tq = self.sbt(ph, "tq_m", [128, 512], F32)
        ps_st = self.pst(ph, "ps_stm", [1, 512])
        ps_st2 = self.pst(ph, "ps_stm2", [1, 512])

        for mc in mcs:
            ldev = [None]
            def y(yq, mc=mc, ldev=ldev):
                p.wait(yq, p.now("T"), p.now("V"))
                for rt in range(8):
                    yq.dma_start(out=stf[:, rt * 513:(rt + 1) * 513],
                                 in_=stream_dram[rt * 128:(rt + 1) * 128,
                                                 mc * 512:mc * 512 + 513]
                                 ).then_inc(p.sem["LD"], 16)
                p.cnt["LD"] += 128
                ldev[0] = p.now("LD")
            p.Y(y)
            stev = [None]
            for pi, (dstrow, which) in enumerate(((0, "v"), (1, "s"))):
                for rt in range(8):
                    off = 1 if rt < 4 else 0
                    src = stf[:, rt * 513 + off: rt * 513 + off + 512]
                    cev = [None]
                    if which == "v":
                        def v(vv, src=src, ldev=ldev, cev=cev):
                            p.wait(vv, ldev[0], p.now("T"))
                            cev[0] = p.inc(vv.tensor_copy(sbfm[:], src), "V")
                        p.V(v)
                        buf = sbfm
                    else:
                        def s_(ss, src=src, ldev=ldev, cev=cev):
                            p.wait(ss, ldev[0], p.now("T"))
                            cev[0] = p.inc(ss.activation(out=sqm[:], in_=src,
                                                         func=AF.Square), "S")
                        p.S(s_)
                        buf = sqm
                    def t(tt, rt=rt, dstrow=dstrow, buf=buf, cev=cev):
                        p.wait(tt, cev[0])
                        pst = ps_st if dstrow == 0 else ps_st2
                        p.inc(tt.matmul(pst[0:1, :],
                                        self.ones_bf[:], buf[:],
                                        start=(rt == 0), stop=(rt == 7)), "T")
                    p.T(t)
            def sev(ss):
                p.wait(ss, p.now("T"))
                ss.copy(rows0[0:1, :], ps_st[0:1, :])
                p.inc(ss.copy(rows1[0:1, :], ps_st2[0:1, :]), "S")
            p.S(sev)
            def vr(vv):
                p.wait(vv, p.now("S"))
                vv.tensor_scalar_mul(rows0[0:1, :], rows0[0:1, :], 1.0 / 1024.0)
                vv.tensor_scalar_mul(rows1[0:1, :], rows1[0:1, :], 1.0 / 1024.0)
                vv.tensor_tensor(out=rows2[0:1, :], in0=rows0[0:1, :],
                                 in1=rows0[0:1, :], op=ALU.mult)
                p.inc(vv.tensor_tensor(out=rows1[0:1, :], in0=rows1[0:1, :],
                                       in1=rows2[0:1, :], op=ALU.subtract), "V")
            p.V(vr)
            def sr(ss):
                p.wait(ss, p.now("V"))
                ss.activation(out=rows1[0:1, :], in_=rows1[0:1, :], func=AF.Ln,
                              bias=self.cc(self.C_EPS)[0:1, :])
                p.inc(ss.activation(out=rows1[0:1, :], in_=rows1[0:1, :],
                                    func=AF.Exp, scale=self.cc(self.C_NH)[0:1, :]), "S")
            p.S(sr)
            bev = [None]
            def g(gg, bev=bev):
                p.wait(gg, p.now("S"), p.now("V"))
                gg.dma_start(out=self.sp_rows[14:15, 0:512], in_=rows0[0:1, :]
                             ).then_inc(p.sem["SP"], 16)
                gg.dma_start(out=self.sp_rows[15:16, 0:512], in_=rows1[0:1, :]
                             ).then_inc(p.sem["SP"], 16)
                p.cnt["SP"] += 32
                gg.wait_ge(p.sem["SP"], p.cnt["SP"])
                gg.dma_start(out=mub[:], in_=self.sp_rows[
                    14:15, 0:512].broadcast_to([128, 512])
                ).then_inc(p.sem["SP"], 16)
                gg.dma_start(out=rb[:], in_=self.sp_rows[
                    15:16, 0:512].broadcast_to([128, 512])
                ).then_inc(p.sem["SP"], 16)
                p.cnt["SP"] += 32
                bev[0] = p.now("SP")
            p.G(g)
            def va(vv, mc=mc, ldev=ldev, bev=bev):
                p.wait(vv, bev[0])
                for rt in range(8):
                    off = 1 if rt < 4 else 0
                    src = stf[:, rt * 513 + off: rt * 513 + off + 512]
                    vv.tensor_tensor(out=tq[:], in0=src, in1=mub[:],
                                     op=ALU.subtract)
                    vv.tensor_tensor(out=tq[:], in0=tq[:], in1=rb[:], op=ALU.mult)
                    vv.tensor_scalar(
                        out=LNIN[:, rt * 2048 + mc * 512:rt * 2048 + mc * 512 + 512],
                        in0=tq[:], scalar1=self.lng_sb[:, rt:rt + 1],
                        scalar2=self.lnb2_sb[:, rt:rt + 1], op0=ALU.mult,
                        op1=ALU.add)
                p.incl(vv, 'V')
            p.V(va)

    def mlp_a(self, lidx, ph, LNIN, gated, mcs, bufs):
        p = self
        w1s, w1b, ggf, tmpa, ps_a, ps_g, b1sb = bufs
        prevk = {"ev": ("V", 0)}
        for k in range(22):
            ldev = [None]
            def y(yq, k=k, ldev=ldev):
                p.wait(yq, prevk["ev"])
                for rt in range(8):
                    yq.dma_start(
                        out=w1s[:, rt * 256:rt * 256 + 128],
                        in_=self.w1p[lidx, rt * 128:(rt + 1) * 128,
                                     k * 128:(k + 1) * 128]).then_inc(p.sem["LD"], 16)
                    yq.dma_start(
                        out=w1s[:, rt * 256 + 128:rt * 256 + 256],
                        in_=self.w1p[lidx, rt * 128:(rt + 1) * 128,
                                     HIDP + k * 128:HIDP + (k + 1) * 128]
                    ).then_inc(p.sem["LD"], 16)
                p.cnt["LD"] += 256
                ldev[0] = p.now("LD")
            p.Y(y)
            cev = [None]
            def v(vv, ldev=ldev, cev=cev):
                p.wait(vv, ldev[0])
                cev[0] = p.inc(vv.tensor_copy(w1b[:], w1s[:]), "V")
            p.V(v)
            mmev = [None]
            def t(tt, k=k, mcs=mcs, cev=cev, mmev=mmev):
                p.wait(tt, cev[0], p.now("V"))
                for mi, mc in enumerate(mcs):
                    for ag in range(2):
                        ps = ps_a if ag == 0 else ps_g
                        for kt in range(8):
                            tt.matmul(
                                ps[:, mi * 512:(mi + 1) * 512],
                                w1b[:, kt * 256 + ag * 128:kt * 256 + ag * 128 + 128],
                                LNIN[:, kt * 2048 + mc * 512:kt * 2048 + mc * 512 + 512],
                                start=(kt == 0), stop=(kt == 7))
                mmev[0] = p.incl(tt, 'T')
            p.T(t)
            gev = [None]
            def s_(ss, k=k, mcs=mcs, mmev=mmev, gev=gev):
                p.wait(ss, mmev[0])
                for mi, mc in enumerate(mcs):
                    ss.activation(out=ggf[:, mi * 512:(mi + 1) * 512],
                                  in_=ps_g[:, mi * 512:(mi + 1) * 512],
                                  func=AF.Gelu, bias=b1sb[:, 22 + k:23 + k])
                gev[0] = p.incl(ss, 'S')
            p.S(s_)
            def v2(vv, k=k, mcs=mcs, mmev=mmev, gev=gev):
                p.wait(vv, gev[0])
                for mi, mc in enumerate(mcs):
                    vv.tensor_scalar(out=tmpa[:], in0=ps_a[:, mi * 512:(mi + 1) * 512],
                                     scalar1=b1sb[:, k:k + 1], scalar2=None,
                                     op0=ALU.add)
                    vv.tensor_tensor(
                        out=gated[:, k * 2048 + mc * 512:k * 2048 + mc * 512 + 512],
                        in0=tmpa[:], in1=ggf[:, mi * 512:(mi + 1) * 512],
                        op=ALU.mult)
                prevk["ev"] = p.incl(vv, 'V')
            p.V(v2)

    def mlp_b(self, lidx, ph, gated, mode):
        p = self
        w2s = [self.sbt(ph, f"w2s{i}", [128, 1024], F32) for i in range(2)]
        w2b = self.sbt(ph, "w2b", [128, 22 * 1024], BF16)
        t1 = self.sbt(ph, "t1_m", [128, 512], F32)
        t2 = self.sbt(ph, "t2_m", [128, 512], F32)
        rl1 = [self.sbt(ph, f"rl1_{i}", [128, 512], F32) for i in range(2)]
        rl2 = [self.sbt(ph, f"rl2_{i}", [128, 512], F32) for i in range(2)]
        rl3 = [self.sbt(ph, f"rl3_{i}", [128, 512], F32) for i in range(2)]
        rl4 = [self.sbt(ph, f"rl4_{i}", [128, 512], F32) for i in range(2)]
        ps_w2 = self.pst(ph, "ps_w2", [128, 1024])

        cast_ev = [None] * 22
        for kt in range(22):
            ldev = [None]
            def y(yq, kt=kt, ldev=ldev):
                if kt >= 2:
                    p.wait(yq, cast_ev[kt - 2])
                ldev[0] = p.inc(yq.dma_start(
                    out=w2s[kt % 2][:],
                    in_=self.w2p[lidx, kt * 128:(kt + 1) * 128, :]), "LD")
            p.Y(y)
            def v(vv, kt=kt, ldev=ldev):
                p.wait(vv, ldev[0])
                cast_ev[kt] = p.inc(vv.tensor_copy(
                    w2b[:, kt * 1024:(kt + 1) * 1024], w2s[kt % 2][:]), "V")
            p.V(v)
        ev_w2 = cast_ev[21]

        prevev = [("SP", 0), ("SP", 0)]
        for ont in range(8):
            for mc in range(4):
                i = (ont * 4 + mc) % 2
                ldev = [None]
                def y(yq, ont=ont, mc=mc, i=i, ldev=ldev):
                    p.wait(yq, prevev[i])
                    rsl = slice(ont * 128, (ont + 1) * 128)
                    csl = slice(1 + mc * 512, 1 + mc * 512 + 512)
                    yq.dma_start(out=rl1[i][:], in_=self.sp_s1[rsl, csl]
                                 ).then_inc(p.sem["LD"], 16)
                    p.cnt["LD"] += 16
                    if mode == "l2":
                        yq.dma_start(out=rl2[i][:],
                                     in_=self.x_fm[rsl, OWN0 + mc * 512:
                                                   OWN0 + mc * 512 + 512]
                                     ).then_inc(p.sem["LD"], 16)
                        yq.dma_start(out=rl3[i][:],
                                     in_=self.sp_xn32[rsl, mc * 512:mc * 512 + 512]
                                     ).then_inc(p.sem["LD"], 16)
                        yq.dma_start(out=rl4[i][:],
                                     in_=self.acc_m[rsl, mc * 512:mc * 512 + 512]
                                     ).then_inc(p.sem["LD"], 16)
                        p.cnt["LD"] += 48
                    ldev[0] = p.now("LD")
                p.Y(y)
                mmev = [None]
                def t(tt, ont=ont, mc=mc, i=i, mmev=mmev):
                    if ont == 0 and mc < 2:
                        p.wait(tt, ev_w2, p.now("V"))
                    p.wait(tt, prevev[i])
                    bk = i * 512
                    for kt in range(22):
                        tt.matmul(ps_w2[:, bk:bk + 512],
                                  w2b[:, kt * 1024 + ont * 128:
                                      kt * 1024 + ont * 128 + 128],
                                  gated[:, kt * 2048 + mc * 512:
                                        kt * 2048 + mc * 512 + 512],
                                  start=(kt == 0), stop=(kt == 21))
                    mmev[0] = p.incl(tt, 'T')
                p.T(t)
                vev = [None]
                def v(vv, ont=ont, mc=mc, i=i, ldev=ldev, mmev=mmev, vev=vev):
                    p.wait(vv, ldev[0], mmev[0])
                    bk = i * 512
                    vv.tensor_scalar(out=t1[:], in0=ps_w2[:, bk:bk + 512],
                                     scalar1=self.b2_sb[:, ont:ont + 1],
                                     scalar2=None, op0=ALU.add)
                    vv.tensor_tensor(out=t1[:], in0=t1[:], in1=rl1[i][:],
                                     op=ALU.add)
                    if mode == "l2":
                        vv.tensor_tensor(out=t2[:], in0=rl2[i][:], in1=rl3[i][:],
                                         op=ALU.add)
                        vv.tensor_tensor(out=t2[:], in0=t2[:], in1=rl4[i][:],
                                         op=ALU.add)
                        vv.tensor_tensor(out=t1[:], in0=t1[:], in1=t2[:],
                                         op=ALU.add)
                    vev[0] = p.inc(vv.tensor_copy(
                        (rl1[i][:] if True else t1[:]), t1[:]), "V")
                p.V(v)
                def g(gg, ont=ont, mc=mc, i=i, vev=vev):
                    p.wait(gg, vev[0])
                    if mode == "l1":
                        e = p.inc(gg.dma_start(
                            out=self.sp_s2[ont * 128:(ont + 1) * 128,
                                           1 + mc * 512:1 + mc * 512 + 512],
                            in_=rl1[i][:]), "SP")
                    else:
                        e = p.inc(gg.dma_start(
                            out=self.y_fm[ont * 128:(ont + 1) * 128,
                                          mc * 512:mc * 512 + 512],
                            in_=rl1[i][:]), "SP")
                    prevev[i] = e
                p.G(g)
        self.ev_mlpb_out = p.now("SP")

    def p2ab(self):
        p = self
        self.barrier()
        self.ph_h2.close()
        ph = contextlib.ExitStack()
        with ph:
            QH = self.sbt(ph, "QH", [128, 4 * 2048], BF16)
            KH = self.sbt(ph, "KH", [128, 4 * 2560], BF16)
            VT = self.sbt(ph, "VT", [128, 20 * 512], BF16)
            ph_q = contextlib.ExitStack()
            cut = os.environ.get("KB_P2AB_CUT", "")
            with ph_q:
                self.ffn_qkv(0, ph_q, QH, KH, VT, self.sp_xnbf,
                             lambda m: KV0 - 1 + m * 512, 5, 0,
                             lambda m: m * 512)
                p.barrier()
            if cut == "qkv":
                return

            stream = self.sbt(ph, "stream1", [128, 8 * SEG], F32)
            AO = self.sbt(ph, "AO", [128, 4 * 2048], BF16)
            ebuf = self.sbt(ph, "ebuf", [128, 16 * 512], BF16)
            recipb = self.sbt(ph, "recipb", [128, 512], F32)
            wo_bf = self.sbt(ph, "wo_bf", [128, 4 * 1024], BF16)
            wos = self.sbt(ph, "wos", [128, 1024], F32)

            for kt in range(4):
                ldev = [None]
                def y(yq, kt=kt, ldev=ldev):
                    p.wait(yq, p.now("V"))
                    ldev[0] = p.inc(yq.dma_start(
                        out=wos[:], in_=self.wo2[0, kt * 128:(kt + 1) * 128, :]), "LD")
                p.Y(y)
                def v(vv, kt=kt, ldev=ldev):
                    p.wait(vv, ldev[0])
                    p.inc(vv.tensor_copy(wo_bf[:, kt * 1024:(kt + 1) * 1024],
                                         wos[:]), "V")
                p.V(v)
            def yst(yq):
                for rt in range(8):
                    yq.dma_start(out=stream[:, rt * SEG + 1:rt * SEG + 1 + TOK],
                                 in_=self.sp_xn32[rt * 128:(rt + 1) * 128, :]
                                 ).then_inc(p.sem["LD"], 16)
                p.cnt["LD"] += 128
            p.Y(yst)
            def vst(vv):
                p.wait(vv, p.now("LD"))
                p.incl(vv, 'V')
            p.V(vst)

            self.ffn_attn(0, ph, [0, 1, 2, 3], QH, KH, VT, stream, wo_bf, AO,
                          ebuf, recipb, self.mf)
            if cut == "attn":
                p.barrier()
                return
            self.x1_issue(stream)
            def gsp(gg):
                p.wait(gg, p.now("V"))
                for rt in range(8):
                    p.inc(gg.dma_start(
                        out=self.sp_s1[rt * 128:(rt + 1) * 128, 1:1 + TOK],
                        in_=stream[:, rt * SEG + 1:rt * SEG + 1 + TOK]), "SP")
            p.G(gsp)
            self.dump("stream1", stream, 8, TOK, cols0=1, src_width=SEG)
            p.barrier()

    def p2c(self):
        p = self
        ph = contextlib.ExitStack()
        with ph:
            self.b1_sb = self.sbt(ph, "b1_sb", [128, 44], F32)
            self.b2_sb = self.sbt(ph, "b2_sb", [128, 8], F32)
            self.lng_sb = self.sbt(ph, "lng_sb", [128, 8], F32)
            self.lnb2_sb = self.sbt(ph, "lnb2_sb", [128, 8], F32)
            def y(yq):
                yq.dma_start(out=self.b1_sb[:], in_=self.b1p[0, :, :]).then_inc(
                    p.sem["LD"], 16)
                yq.dma_start(out=self.b2_sb[:], in_=self.b2v[0, :, :]).then_inc(
                    p.sem["LD"], 16)
                yq.dma_start(out=self.lng_sb[:], in_=self.lng[0, :, :]).then_inc(
                    p.sem["LD"], 16)
                yq.dma_start(out=self.lnb2_sb[:], in_=self.lnb2[0, :, :]).then_inc(
                    p.sem["LD"], 16)
                p.cnt["LD"] += 64
            p.Y(y)
            gated = self.sbt(ph, "gated", [128, 22 * 2048], BF16)
            ph_a = contextlib.ExitStack()
            with ph_a:
                LNIN = self.sbt(ph_a, "LNIN", [128, 8 * 2048], BF16)
                ph_ln = contextlib.ExitStack()
                with ph_ln:
                    self.mlp_ln(0, ph_ln, self.sp_s1, LNIN, [0, 1, 2, 3])
                    self.barrier()
                w1s = self.sbt(ph_a, "w1s", [128, 8 * 256], F32)
                w1b = self.sbt(ph_a, "w1b", [128, 8 * 256], BF16)
                ggf = self.sbt(ph_a, "ggf", [128, 4 * 512], F32)
                tmpa = self.sbt(ph_a, "tmpa", [128, 512], F32)
                ps_a = self.pst(ph_a, "ps_a", [128, 2048])
                ps_g = self.pst(ph_a, "ps_g", [128, 2048])
                self.mlp_a(0, ph_a, LNIN, gated, [0, 1, 2, 3],
                           (w1s, w1b, ggf, tmpa, ps_a, ps_g, self.b1_sb))
                p.barrier()
            self.mlp_b(0, ph, gated, "l1")
            p.barrier()

    def x2a(self):
        p = self
        ph = contextlib.ExitStack()
        with ph:
            hb = self.sbt(ph, "hb2a", [128, 8], F32)
            h0 = self.sbt(ph, "h02a", [128, 8], F32)
            h1 = self.sbt(ph, "h12a", [128, 8], F32)
            def g(gg):
                p.wait(gg, self.ev_mlpb_out)
                gg.dma_start(
                    out=hb[:],
                    in_=self.sp_s2[:, 2048:2049].rearrange(
                        "(a q) w -> q (a w)", q=128),
                    allow_slow_non_contiguous=True).then_inc(p.sem["SP"], 16)
                p.cnt["SP"] += 16
                gg.wait_ge(p.sem["SP"], p.cnt["SP"])
                gg.dma_start(out=self.cc2a_in[:, :], in_=hb[:]).then_inc(
                    p.sem["SP"], 16)
                p.cnt["SP"] += 16
                gg.wait_ge(p.sem["SP"], p.cnt["SP"])
                p.inc(gg.collective_compute(
                    "AllGather", ALU.bypass, replica_groups=PAIRS,
                    ins=[self.cc2a_in[:, :]], outs=[self.cc2a_out[:, :]]), "G")
                gg.wait_ge(p.sem["G"], p.cnt["G"])
                gg.dma_start(out=h0[:], in_=self.cc2a_out[0:128, :]).then_inc(
                    p.sem["SP"], 16)
                gg.dma_start(out=h1[:], in_=self.cc2a_out[128:256, :]).then_inc(
                    p.sem["SP"], 16)
                p.cnt["SP"] += 32
            p.G(g)
            ev = p.now("SP")
            def v(vv):
                p.wait(vv, ev)
                F = self.flags_sb
                vv.tensor_scalar(out=h0[:], in0=h0[:], scalar1=F[:, 2:3],
                                 scalar2=None, op0=ALU.mult)
                p.inc(vv.scalar_tensor_tensor(
                    out=h0[:], in0=h1[:], scalar=F[:, 3:4], in1=h0[:],
                    op0=ALU.mult, op1=ALU.add), "V")
            p.V(v)
            def g2(gg):
                p.wait(gg, p.now("V"))
                for rt in range(8):
                    p.inc(gg.dma_start(
                        out=self.sp_s2[rt * 128:(rt + 1) * 128, 0:1],
                        in_=h0[:, rt:rt + 1],
                        allow_slow_non_contiguous=True), "SP")
            p.G(g2)
            p.barrier()

    def p4(self):
        p = self
        ph = contextlib.ExitStack()
        with ph:
            QH = self.sbt(ph, "QH2", [128, 4 * 2048], BF16)
            KH = self.sbt(ph, "KH2", [128, 4 * 2560], BF16)
            VT = self.sbt(ph, "VT2", [128, 20 * 512], BF16)
            ph_q = contextlib.ExitStack()
            with ph_q:
                self.ffn_qkv(1, ph_q, QH, KH, VT, self.sp_s2,
                             lambda m: m * 512, 4, 1,
                             lambda m: 512 + m * 512)
                p.barrier()

            stream = self.sbt(ph, "stream2", [128, 8 * SEG], F32)
            AO = self.sbt(ph, "AO2", [128, 4 * 2048], BF16)
            ebuf = self.sbt(ph, "ebuf2", [128, 16 * 512], BF16)
            recipb = self.sbt(ph, "recipb2", [128, 512], F32)
            wo_bf = self.sbt(ph, "wo_bf2", [128, 4 * 1024], BF16)
            wos = self.sbt(ph, "wos2", [128, 1024], F32)
            for kt in range(4):
                ldev = [None]
                def y(yq, kt=kt, ldev=ldev):
                    p.wait(yq, p.now("V"))
                    ldev[0] = p.inc(yq.dma_start(
                        out=wos[:], in_=self.wo2[1, kt * 128:(kt + 1) * 128, :]), "LD")
                p.Y(y)
                def v(vv, kt=kt, ldev=ldev):
                    p.wait(vv, ldev[0])
                    p.inc(vv.tensor_copy(wo_bf[:, kt * 1024:(kt + 1) * 1024],
                                         wos[:]), "V")
                p.V(v)
            def yst(yq):
                for rt in range(8):
                    yq.dma_start(out=stream[:, rt * SEG:rt * SEG + SEG],
                                 in_=self.sp_s2[rt * 128:(rt + 1) * 128, :]
                                 ).then_inc(p.sem["LD"], 16)
                p.cnt["LD"] += 128
            p.Y(yst)
            def vst(vv):
                p.wait(vv, p.now("LD"))
                p.incl(vv, 'V')
            p.V(vst)

            def vz(vv):
                for t4 in range(4):
                    vv.memset(KH[:, t4 * 2560:t4 * 2560 + 512], 0.0)
                for ktg in range(4):
                    vv.memset(VT[:, ktg * 512:(ktg + 1) * 512], 0.0)
                p.incl(vv, 'V')
            p.V(vz)

            self.ffn_attn(1, ph, [1, 2, 3], QH, KH, VT, stream, wo_bf, AO,
                          ebuf, recipb, self.mf)

            def gx(gg):
                p.wait(gg, p.now("V"), *self.ev_qkv_done)
                for t4 in range(4):
                    gg.dma_start(
                        out=self.cc2b_in[:, t4 * 512:(t4 + 1) * 512],
                        in_=KH[:, t4 * 2560 + 2048:t4 * 2560 + 2560]).then_inc(
                        p.sem["SP"], 16)
                for kt in range(4):
                    gg.dma_start(
                        out=self.cc2b_in[:, 2048 + kt * 512:2048 + (kt + 1) * 512],
                        in_=VT[:, (16 + kt) * 512:(17 + kt) * 512]).then_inc(
                        p.sem["SP"], 16)
                gg.dma_start(
                    out=self.cc2b_in[:, 4096:4104],
                    in_=stream[:].rearrange("p (a b) -> p a b", b=SEG)
                    [:, :, SEG - 1:SEG],
                    allow_slow_non_contiguous=True).then_inc(p.sem["SP"], 16)
                p.cnt["SP"] += 144
                gg.wait_ge(p.sem["SP"], p.cnt["SP"])
                p.inc(gg.collective_compute(
                    "AllGather", ALU.bypass, replica_groups=PAIRS,
                    ins=[self.cc2b_in[:, :]], outs=[self.cc2b_out[:, :]]), "G")
            p.G(gx)
            ev_x2b = p.now("G")

            t0 = self.sbt(ph, "x2t0", [128, 4096], BF16)
            t1 = self.sbt(ph, "x2t1", [128, 4096], BF16)
            tm0 = self.sbt(ph, "x2m0", [128, 8], F32)
            tm1 = self.sbt(ph, "x2m1", [128, 8], F32)
            ev = [None]
            def gc(gg):
                p.wait(gg, ev_x2b)
                gg.dma_start(out=t0[:], in_=self.cc2b_out[0:128, 0:4096]
                             ).then_inc(p.sem["SP"], 16)
                gg.dma_start(out=t1[:], in_=self.cc2b_out[128:256, 0:4096]
                             ).then_inc(p.sem["SP"], 16)
                gg.dma_start(out=tm0[:], in_=self.cc2b_out[0:128, 4096:4104]
                             ).then_inc(p.sem["SP"], 16)
                gg.dma_start(out=tm1[:], in_=self.cc2b_out[128:256, 4096:4104]
                             ).then_inc(p.sem["SP"], 16)
                p.cnt["SP"] += 64
                ev[0] = p.now("SP")
            p.G(gc)
            def vc(vv):
                p.wait(vv, ev[0])
                F = self.flags_sb
                vv.tensor_scalar(out=t0[:], in0=t0[:], scalar1=F[:, 2:3],
                                 scalar2=None, op0=ALU.mult)
                vv.scalar_tensor_tensor(out=t0[:], in0=t1[:], scalar=F[:, 3:4],
                                        in1=t0[:], op0=ALU.mult, op1=ALU.add)
                vv.tensor_scalar(out=tm0[:], in0=tm0[:], scalar1=F[:, 2:3],
                                 scalar2=None, op0=ALU.mult)
                vv.scalar_tensor_tensor(out=tm0[:], in0=tm1[:], scalar=F[:, 3:4],
                                        in1=tm0[:], op0=ALU.mult, op1=ALU.add)
                for t4 in range(4):
                    vv.tensor_copy(KH[:, t4 * 2560:t4 * 2560 + 512],
                                   t0[:, t4 * 512:(t4 + 1) * 512])
                for kt in range(4):
                    vv.tensor_copy(VT[:, kt * 512:(kt + 1) * 512],
                                   t0[:, 2048 + kt * 512:2048 + (kt + 1) * 512])
                p.incl(vv, 'V')
            p.V(vc)
            def gmid(gg):
                p.wait(gg, p.now("V"))
                for rt in range(8):
                    p.inc(gg.dma_start(
                        out=self.sp_s1[rt * 128:(rt + 1) * 128, 0:1],
                        in_=tm0[:, rt:rt + 1],
                        allow_slow_non_contiguous=True), "SP")
            p.G(gmid)

            self.ffn_attn(1, ph, [0], QH, KH, VT, stream, wo_bf, AO,
                          ebuf, recipb, self.mf)

            def gsp(gg):
                p.wait(gg, p.now("V"))
                for rt in range(8):
                    p.inc(gg.dma_start(
                        out=self.sp_s1[rt * 128:(rt + 1) * 128, 1:1 + TOK],
                        in_=stream[:, rt * SEG + 1:rt * SEG + 1 + TOK]), "SP")
            p.G(gsp)
            self.dump("stream2", stream, 8, TOK, cols0=1, src_width=SEG)
            p.barrier()

        ph2 = contextlib.ExitStack()
        with ph2:
            self.b1_sb = self.sbt(ph2, "b1_sb2", [128, 44], F32)
            self.b2_sb = self.sbt(ph2, "b2_sb2", [128, 8], F32)
            self.lng_sb = self.sbt(ph2, "lng_sb2", [128, 8], F32)
            self.lnb2_sb = self.sbt(ph2, "lnb2_sb2", [128, 8], F32)
            def y(yq):
                yq.dma_start(out=self.b1_sb[:], in_=self.b1p[1, :, :]).then_inc(
                    p.sem["LD"], 16)
                yq.dma_start(out=self.b2_sb[:], in_=self.b2v[1, :, :]).then_inc(
                    p.sem["LD"], 16)
                yq.dma_start(out=self.lng_sb[:], in_=self.lng[1, :, :]).then_inc(
                    p.sem["LD"], 16)
                yq.dma_start(out=self.lnb2_sb[:], in_=self.lnb2[1, :, :]).then_inc(
                    p.sem["LD"], 16)
                p.cnt["LD"] += 64
            p.Y(y)
            gated = self.sbt(ph2, "gated2", [128, 22 * 2048], BF16)
            ph_a = contextlib.ExitStack()
            with ph_a:
                LNIN = self.sbt(ph_a, "LNIN2", [128, 8 * 2048], BF16)
                ph_ln = contextlib.ExitStack()
                with ph_ln:
                    self.mlp_ln(1, ph_ln, self.sp_s1, LNIN, [0, 1, 2, 3])
                    self.barrier()
                w1s = self.sbt(ph_a, "w1s2", [128, 8 * 256], F32)
                w1b = self.sbt(ph_a, "w1b2", [128, 8 * 256], BF16)
                ggf = self.sbt(ph_a, "ggf2", [128, 4 * 512], F32)
                tmpa = self.sbt(ph_a, "tmpa2", [128, 512], F32)
                ps_a = self.pst(ph_a, "ps_a2", [128, 2048])
                ps_g = self.pst(ph_a, "ps_g2", [128, 2048])
                self.mlp_a(1, ph_a, LNIN, gated, [0, 1, 2, 3],
                           (w1s, w1b, ggf, tmpa, ps_a, ps_g, self.b1_sb))
                p.barrier()
            self.mlp_b(1, ph2, gated, "l2")
            def gf(gg):
                p.wait(gg, self.ev_mlpb_out)
                gg.nop()
            p.G(gf)
            p.barrier()


def build(dbg=(), stop_after=None):
    nc = bass.Bass("TRN2", target_bir_lowering=False,
                   detect_race_conditions=False)
    es = contextlib.ExitStack()
    with es:
        k = Kern(nc, es, dbg)
        k.declare_io()
        k.persistent()
        k.p0()
        if stop_after == "P0":
            k.barrier()
            k.ph_h2.close()
            _finish_dummy(k)
            return nc, k.dbg_out
        if "f" in P1DIRS:
            k.p1("f", rev=False)
        if "r" in P1DIRS:
            k.p1("r", rev=True)
        if stop_after == "P1":
            k.barrier()
            k.ph_h2.close()
            for d in "fr":
                k.dump_dram(f"xT_{d}", k.sp_xT[d], (TOK, D_INNER))
                k.dump_dram(f"xD_{d}", k.sp_xD[d], (D_INNER, TOK))
                k.dump_dram(f"B_{d}", k.sp_B[d], (64, TOK))
                k.dump_dram(f"C_{d}", k.sp_C[d], (64, TOK))
                k.dump_dram(f"ctl_{d}", k.sp_ctl[d], (TOK, 64))
                k.dump_dram(f"zs_{d}", k.sp_zs[d], (D_INNER, TOK))
                k.dump_dram(f"sadd_{d}", k.sp_sadd[d], (64, NCH * D_INNER))
                k.dump_dram(f"cl_{d}", k.sp_cl[d], (32, TOK))
                k.dump_dram(f"gam_{d}", k.sp_gam[d], (32, TOK))
                k.dump_dram(f"dr_{d}", k.sp_dr[d], (NCH, D_INNER))
            _finish_dummy(k)
            return nc, k.dbg_out
        k.p2ab()
        if stop_after == "P2AB":
            k.barrier()
            _finish_dummy(k)
            return nc, k.dbg_out
        ph = contextlib.ExitStack()
        with ph:
            k.x1_consume(ph)
            k.barrier()
        if stop_after == "X1":
            _finish_dummy(k)
            return nc, k.dbg_out
        if "f" in os.environ.get("KB_P3_DIRS", "fr"):
            k.p3("f", rev=False)
        if "r" in os.environ.get("KB_P3_DIRS", "fr"):
            k.p3("r", rev=True)
        if stop_after == "P3":
            k.dump_dram("accm", k.acc_m, (D, TOK))
            _finish_dummy(k)
            return nc, k.dbg_out
        k.p2c()
        if stop_after == "P2C":
            k.dump_dram("s2", k.sp_s2, (D, SEG))
            _finish_dummy(k)
            return nc, k.dbg_out
        k.x2a()
        k.p4()
    return nc, k.dbg_out


def _finish_dummy(k):
    p = k
    nc = k.nc
    with nc.sbuf_tensor("zout", [128, TOK], F32) as z:
        with nc.semaphore("stf") as stf:
            def v(vv):
                p.inc(vv.memset(z[:], 0.0), "V")
            p.V(v)
            def g(gg):
                p.wait(gg, p.now("V"), p.now("SP"))
                n = 0
                for rt in range(8):
                    gg.dma_start(out=k.y_fm[rt * 128:(rt + 1) * 128, :],
                                 in_=z[:]).then_inc(stf, 16)
                    n += 16
                gg.wait_ge(stf, n)
            p.G(g)


def _rotary_tables():
    n = L
    half = np.arange(0, HEADDIM, 2, dtype=np.float32)
    inv_freq = 1.0 / 10000.0 ** (half / HEADDIM)
    t = np.arange(-512, n, dtype=np.float32)
    freqs = np.concatenate([t[:, None] * inv_freq[None, :]] * 2, 1)
    base = (half + 0.4 * HEADDIM) / (1.4 * HEADDIM)
    power = (t - n // 2) / float(WIN // 2)
    scale = np.concatenate([base[None, :] ** power[:, None]] * 2, 1)
    cos, sin = np.cos(freqs), np.sin(freqs)
    sgn = np.where(np.arange(HEADDIM) < 32, -1.0, 1.0).astype(np.float32)[None, :]
    Aq = (cos * scale * 8.0).T
    Bq = (sin * scale * 8.0 * sgn).T
    Ak = (cos / scale).T
    Bk = (sin / scale * sgn).T
    return [np.ascontiguousarray(np.tile(x, (2, 1)), dtype=np.float32)
            for x in (Aq, Bq, Ak, Bk)]


def _tile_pp(v, nt):
    return np.ascontiguousarray(v.reshape(nt, 128).T, dtype=np.float32)


def host_prep(inputs):
    x = inputs["L0_attn_x"]
    rotf = _rotary_tables()
    common = {}
    for d, pre in (("f", "L0_attn_f_"), ("r", "L0_attn_r_")):
        ip = inputs[pre + "in_proj"]
        common[f"wz_{d}"] = np.ascontiguousarray(ip[:D_INNER].T)
        common[f"wx_{d}"] = np.ascontiguousarray(ip[D_INNER:].T)
        common[f"wout_{d}"] = np.ascontiguousarray(inputs[pre + "out_proj"].T)
        cw = inputs[pre + "conv_w"]
        common[f"convw_{d}"] = np.ascontiguousarray(
            cw.reshape(17, 128, 4).transpose(1, 0, 2).reshape(128, 68))
        common[f"convb_{d}"] = _tile_pp(inputs[pre + "conv_b"], 17)
        common[f"mv_{d}"] = np.ascontiguousarray(np.stack(
            [inputs[pre + "dt_bias"], inputs[pre + "A_log"],
             inputs[pre + "D"]], 1), dtype=np.float32)
        common[f"dexp_{d}"] = _tile_pp(np.repeat(inputs[pre + "D"], 64), 16)
        common[f"rmsw_{d}"] = _tile_pp(inputs[pre + "rms_w"], 16)
    common["lnwb"] = np.ascontiguousarray(np.concatenate(
        [_tile_pp(inputs["L0_attn_ln_w"], 8), _tile_pp(inputs["L0_attn_ln_b"], 8)], 1))
    common["wqkv"] = np.ascontiguousarray(inputs["L0_ffn_wqkv"])
    common["wo2"] = np.ascontiguousarray(inputs["L0_ffn_wo"])
    w1 = inputs["L0_ffn_w1"]
    w1p = np.zeros((2, D, 2 * HIDP), np.float32)
    w1p[:, :, :HID] = w1[:, :, :HID]
    w1p[:, :, HIDP:HIDP + HID] = w1[:, :, HID:]
    common["w1p"] = w1p
    b1 = inputs["L0_ffn_b1"]
    b1p = np.zeros((2, 2 * HIDP), np.float32)
    b1p[:, :HID] = b1[:, :HID]
    b1p[:, HIDP:HIDP + HID] = b1[:, HID:]
    common["b1p"] = np.stack([_tile_pp(b1p[i], 44) for i in range(2)])
    w2 = inputs["L0_ffn_w2"]
    w2p = np.zeros((2, HIDP, D), np.float32)
    w2p[:, :HID] = w2
    common["w2p"] = w2p
    common["b2v"] = np.stack([_tile_pp(inputs["L0_ffn_b2"][i], 8) for i in range(2)])
    common["lng"] = np.stack([_tile_pp(inputs["L0_ffn_ln_g"][i], 8) for i in range(2)])
    common["lnb2"] = np.stack([_tile_pp(inputs["L0_ffn_ln_b"][i], 8) for i in range(2)])
    common["qsc"] = np.ascontiguousarray(
        np.tile(inputs["L0_ffn_q_scale"], (1, 2))[:, :, None])
    common["ksc"] = np.ascontiguousarray(
        np.tile(inputs["L0_ffn_k_scale"], (1, 2))[:, :, None])

    in_maps = []
    for c in range(8):
        b, h = c // 2, c % 2
        start = h * TOK
        lo, hi = start - OWN0, start + TOK + 128
        xs = np.zeros((XW, D), np.float32)
        lo_c, hi_c = max(lo, 0), min(hi, L)
        xs[lo_c - lo:hi_c - lo] = x[b, lo_c:hi_c]
        m = dict(common)
        m["x_fm"] = np.ascontiguousarray(xs.T)
        c0 = start - 512 + 512
        m["rot"] = np.ascontiguousarray(
            np.stack([rt[:, c0:c0 + RW] for rt in rotf]))
        fl = np.zeros((128, 8), np.float32)
        m_front = 1.0 if h == 1 else 0.0
        m_back = 1.0 if h == 0 else 0.0
        wsel0, wsel1 = float(h), 1.0 - float(h)
        fl[:, 0] = m_front
        fl[:, 1] = m_back
        fl[:, 2] = wsel0 * m_front
        fl[:, 3] = wsel1 * m_front
        fl[:, 4] = wsel0 * m_back
        fl[:, 5] = wsel1 * m_back
        m["flags"] = fl
        in_maps.append(m)
    return in_maps


_NC_CACHE = {}


def kernel(**inputs):
    in_maps = host_prep(inputs)
    nc, _ = build()
    res = run_bass_kernel_spmd(nc, in_maps, core_ids=list(range(8)))
    out = np.empty((B, L, D), np.float32)
    for c in range(8):
        b, h = c // 2, c % 2
        out[b, h * TOK:(h + 1) * TOK] = res.results[c]["y_fm"].T
    return out


# version 04
# speedup vs baseline: 1.0254x; 1.0055x over previous
import contextlib
import os
import numpy as np

P1LVL = int(os.environ.get("KB_P1_LVL", "9"))
P1NST = int(os.environ.get("KB_P1_NST", str(8)))
P1DIRS = os.environ.get("KB_P1_DIRS", "fr")

import concourse.bass as bass
import concourse.mybir as mybir
from concourse.bass_utils import run_bass_kernel_spmd

F32 = mybir.dt.float32
BF16 = mybir.dt.bfloat16
AF = mybir.ActivationFunctionType
ALU = mybir.AluOpType

B, L, D = 4, 4096, 1024
TOK = 2048
Q = 128
NCH = 16
ST1 = 256
NST1 = TOK // ST1
WIN = 512
D_INNER, NHEADS, HEADDIM, D_STATE = 2048, 32, 64, 64
CONV_DIM = 2176
XW = 2816
OWN0 = 640
KV0 = 128
HID, HIDP = 2730, 2816
EPS = 1e-5
RW = 2560
SEG = 2049
PAIRS = [[0, 1], [2, 3], [4, 5], [6, 7]]


def bcast_rows_ap(row_ap, nparts):
    return row_ap.broadcast_to([nparts, row_ap.shape[1]])


def rep_outer(ap, reps):
    a = ap.rearrange("p (a b) -> p a b", a=1)
    return a.broadcast_to([a.shape[0], reps, a.shape[2]])


def rep_inner(ap, reps):
    a = ap.rearrange("p (a b) -> p a b", b=1)
    return a.broadcast_to([a.shape[0], a.shape[1], reps])



class _Track:

    def __init__(self, eng, kern=None, qkey=None):
        self._eng = eng
        self.last = None
        self._kern = kern
        self._qkey = qkey
        self._fenced = False

    def __getattr__(self, name):
        fn = getattr(self._eng, name)
        if callable(fn):
            def wrap(*a, **kw):
                if (name == "dma_start" and self._kern is not None
                        and not self._fenced and not _Track.no_fence):
                    self._fenced = True
                    self._eng.wait_ge(self._kern.sem[self._qkey],
                                      self._kern.cnt[self._qkey])
                r = fn(*a, **kw)
                if hasattr(r, "then_inc"):
                    self.last = r
                if name == "dma_start" and self._kern is not None:
                    self._kern.dma_audit[self._qkey] = \
                        self._kern.dma_audit.get(self._qkey, 0) + 16
                return r
            return wrap
        return fn

    no_fence = False


class Kern:
    def __init__(self, nc, es, dbg=()):
        self.nc = nc
        self.es = es
        self.dbg = set(dbg)
        self.dbg_out = {}
        self.block = es.enter_context(nc.Block())
        self.sem = {}
        self.cnt = {}
        self.dma_audit = {}
        for k in ("T", "V", "S", "G", "LD", "SP"):
            self.sem[k] = es.enter_context(nc.semaphore("sem" + k))
            self.cnt[k] = 0

    def inc(self, instr, k):
        n = 16 if k in ("LD", "SP") else 1
        self.cnt[k] += n
        instr.then_inc(self.sem[k], n)
        return (k, self.cnt[k])

    def now(self, k):
        return (k, self.cnt[k])

    def wait(self, eng, *evs):
        for k, v in evs:
            if k in ("LD", "SP"):
                v = self.cnt[k]
            if v > 0:
                eng.wait_ge(self.sem[k], v)

    def T(self, fn): self.block.tensor(lambda e: fn(_Track(e, self, "T")))
    def V(self, fn): self.block.vector(lambda e: fn(_Track(e, self, "V")))
    def S(self, fn): self.block.scalar(lambda e: fn(_Track(e, self, "S")))
    def G(self, fn): self.block.gpsimd(lambda e: fn(_Track(e, self, "SP")))
    def Y(self, fn): self.block.sync(lambda e: fn(_Track(e, self, "LD")))

    def incl(self, eng, k):
        assert eng.last is not None
        return self.inc(eng.last, k)

    def barrier(self):
        for qk in ("LD", "SP"):
            a, c = self.dma_audit.get(qk, 0), self.cnt[qk]
            assert a == c, f"DMA audit mismatch {qk}: emitted incs {a} != tracked cnt {c}"
        evs = [self.now(k) for k in ("T", "V", "S", "G", "LD", "SP")]
        for sec in (self.T, self.V, self.S, self.G, self.Y):
            sec(lambda e, evs=evs: self.wait(e, *evs))

    _uid = 0

    alloc_log = []

    def sbt(self, scope, name, shape, dt):
        Kern._uid += 1
        b0 = self.nc.sbuf_base
        t = scope.enter_context(
            self.nc.sbuf_tensor(f"{name}_u{Kern._uid}", shape, dt))
        Kern.alloc_log.append((name, b0, self.nc.sbuf_base, tuple(shape), str(dt)))
        return t

    def pst(self, scope, name, shape, dt=F32):
        Kern._uid += 1
        return scope.enter_context(
            self.nc.psum_tensor(f"{name}_u{Kern._uid}", shape, dt))

    def mkdbg(self, name, shape):
        if name in self.dbg and name not in self.dbg_out:
            self.dbg_out[name] = self.nc.dram_tensor(
                "dbg_" + name, shape, F32, kind="ExternalOutput").ap()
        return self.dbg_out.get(name)

    def dump(self, name, buf, nrt, width, cols0=0, src_width=None):
        d = self.mkdbg(name, [nrt * 128, width])
        if d is None:
            return
        sw = src_width or width
        def g(gg):
            self.wait(gg, self.now("V"), self.now("S"), self.now("T"),
                      self.now("G") if False else (0, 0))
            for rt in range(nrt):
                self.inc(gg.dma_start(
                    out=d[rt * 128:(rt + 1) * 128, :],
                    in_=buf[:, rt * sw + cols0: rt * sw + cols0 + width]), "SP")
        self.G(g)

    def declare_io(self):
        DT = self.nc.dram_tensor
        ap = lambda *a, **kw: DT(*a, **kw).ap()
        self.x_fm = ap("x_fm", [D, XW], F32, kind="ExternalInput")
        self.wx = {d: ap(f"wx_{d}", [D, 2208], F32, kind="ExternalInput") for d in "fr"}
        self.wz = {d: ap(f"wz_{d}", [D, D_INNER], F32, kind="ExternalInput") for d in "fr"}
        self.wout = {d: ap(f"wout_{d}", [D_INNER, D], F32, kind="ExternalInput") for d in "fr"}
        self.convw = {d: ap(f"convw_{d}", [128, 68], F32, kind="ExternalInput") for d in "fr"}
        self.convb = {d: ap(f"convb_{d}", [128, 17], F32, kind="ExternalInput") for d in "fr"}
        self.mv = {d: ap(f"mv_{d}", [32, 3], F32, kind="ExternalInput") for d in "fr"}
        self.dexp = {d: ap(f"dexp_{d}", [128, 16], F32, kind="ExternalInput") for d in "fr"}
        self.rmsw = {d: ap(f"rmsw_{d}", [128, 16], F32, kind="ExternalInput") for d in "fr"}
        self.lnwb = ap("lnwb", [128, 16], F32, kind="ExternalInput")
        self.wqkv = ap("wqkv", [2, D, 1536], F32, kind="ExternalInput")
        self.wo2 = ap("wo2", [2, 512, D], F32, kind="ExternalInput")
        self.w1p = ap("w1p", [2, D, 2 * HIDP], F32, kind="ExternalInput")
        self.b1p = ap("b1p", [2, 128, 44], F32, kind="ExternalInput")
        self.w2p = ap("w2p", [2, HIDP, D], F32, kind="ExternalInput")
        self.b2v = ap("b2v", [2, 128, 8], F32, kind="ExternalInput")
        self.lng = ap("lng", [2, 128, 8], F32, kind="ExternalInput")
        self.lnb2 = ap("lnb2", [2, 128, 8], F32, kind="ExternalInput")
        self.qsc = ap("qsc", [2, 128, 1], F32, kind="ExternalInput")
        self.ksc = ap("ksc", [2, 128, 1], F32, kind="ExternalInput")
        self.rot = ap("rot", [4, 128, RW], F32, kind="ExternalInput")
        self.flags = ap("flags", [128, 8], F32, kind="ExternalInput")
        self.y_fm = ap("y_fm", [D, TOK], F32, kind="ExternalOutput")

        self.sp_xnbf = ap("sp_xnbf", [D, XW], BF16)
        self.sp_xn32 = ap("sp_xn32", [D, TOK], F32)
        self.sp_xD = {d: ap(f"sp_xD_{d}", [16 * 8 * 128, 256], BF16) for d in "fr"}
        self.sp_xT = {d: ap(f"sp_xT_{d}", [TOK, D_INNER], BF16) for d in "fr"}
        self.sp_B = {d: ap(f"sp_B_{d}", [64, TOK], BF16) for d in "fr"}
        self.sp_C = {d: ap(f"sp_C_{d}", [64, TOK], BF16) for d in "fr"}
        self.sp_sadd = {d: ap(f"sp_sadd_{d}", [64, NCH * D_INNER], F32) for d in "fr"}
        self.sp_cl = {d: ap(f"sp_cl_{d}", [32, TOK], F32) for d in "fr"}
        self.sp_gam = {d: ap(f"sp_gam_{d}", [32, TOK], F32) for d in "fr"}
        self.sp_ctl = {d: ap(f"sp_ctl_{d}", [TOK, 64], F32) for d in "fr"}
        self.sp_dr = {d: ap(f"sp_dr_{d}", [NCH, D_INNER], F32) for d in "fr"}
        self.sp_zs = {d: ap(f"sp_zs_{d}", [16 * 8 * 128, 256], F32) for d in "fr"}
        self.acc_m = ap("acc_m", [D, TOK], F32)
        self.sp_s1 = ap("sp_s1", [D, SEG], F32)
        self.sp_s2 = ap("sp_s2", [D, SEG], F32)
        self.sp_rows = ap("sp_rows", [16, 4096], F32)
        self.cc1_in = ap("cc1_in", [128, 2056], F32)
        self.cc1_out = ap("cc1_out", [256, 2056], F32)
        self.cc2a_in = ap("cc2a_in", [128, 8], F32)
        self.cc2a_out = ap("cc2a_out", [256, 8], F32)
        self.cc2b_in = ap("cc2b_in", [128, 4104], F32)
        self.cc2b_out = ap("cc2b_out", [256, 4104], F32)

    def persistent(self):
        es, p = self.es, self
        self.consts = self.sbt(es, "consts", [128, 10], F32)
        self.ones_bf = self.sbt(es, "ones_bf", [128, 1], BF16)
        self.ident_bf = self.sbt(es, "ident_bf", [128, 128], BF16)
        self.ident_f = self.sbt(es, "ident_f", [128, 128], F32)
        self.blk2 = self.sbt(es, "blk2", [128, 2], BF16)
        self.flags_sb = self.sbt(es, "flags_sb", [128, 8], F32)
        self.lnwb_sb = self.sbt(es, "lnwb_sb", [128, 16], F32)
        self.s_state = self.sbt(es, "s_state", [64, 2 * D_INNER], F32)
        tmpf = self.sbt(es, "tmpf", [128, 128], F32)
        self.ph_h2 = contextlib.ExitStack()
        self.h2 = self.sbt(self.ph_h2, "h2", [128, 8 * XW], BF16)

        def v0(vv):
            for i, v in enumerate([EPS, -0.5, 1.0, -8.0, 1.0 / 2048.0, 0.0,
                                   1e12, -1.0, 1e-30, 0.0]):
                vv.memset(self.consts[:, i:i + 1], v)
            vv.memset(self.ones_bf[:], 1.0)
            vv.memset(self.blk2[:], 0.0)
            vv.memset(self.blk2[0:64, 0:1], 1.0)
            vv.memset(self.blk2[64:128, 1:2], 1.0)
            p.inc(vv.memset(tmpf[:], 1.0), "V")
        p.V(v0)

        def g(gg):
            p.wait(gg, p.now("V"))
            p.inc(gg.affine_select(
                out=tmpf[:], in_=tmpf[:], pattern=[[-1, 128]],
                compare_op=ALU.is_equal, fill=0.0, base=0,
                channel_multiplier=1), "G")
            gg.dma_start(out=self.flags_sb[:], in_=self.flags[:, :]).then_inc(
                p.sem["SP"], 16)
            gg.dma_start(out=self.lnwb_sb[:], in_=self.lnwb[:, :]).then_inc(
                p.sem["SP"], 16)
        p.G(g)
        p.cnt["SP"] += 32

        def v(vv):
            p.wait(vv, p.now("G"), p.now("SP"))
            vv.tensor_copy(self.ident_bf[:], tmpf[:])
            p.inc(vv.tensor_copy(self.ident_f[:], tmpf[:]), "V")
        p.V(v)
        self.mf = self.flags_sb[:, 0:1]
        self.mb = self.flags_sb[:, 1:2]
        self.C_EPS, self.C_NH, self.C_ONE, self.C_M8 = 0, 1, 2, 3
        self.C_IDIM, self.C_ZERO, self.C_BIG, self.C_M1 = 4, 5, 6, 7
        self.C_TINY = 8

    def cc(self, col):
        return self.consts[:, col:col + 1]

    def p0(self):
        p, nc = self, self.nc
        ph = contextlib.ExitStack()
        with ph:
            xrot = [self.sbt(ph, f"xrot{i}", [128, XW], F32) for i in range(2)]
            xbf = self.sbt(ph, "xbf", [128, 8 * XW], BF16)
            rows0 = self.sbt(ph, "rows0", [1, XW], F32)
            rows1 = self.sbt(ph, "rows1", [1, XW], F32)
            mub = self.sbt(ph, "mub", [128, XW], F32)
            rb = self.sbt(ph, "rb", [128, XW], F32)
            x32r = [self.sbt(ph, f"x32r{i}", [128, TOK], F32) for i in range(2)]
            sqs = [self.sbt(ph, f"sqs{i}", [128, XW], BF16) for i in range(2)]
            ps_st = self.pst(ph, "ps_st", [1, XW])
            MCH = [(i * 512, min(512, XW - i * 512)) for i in range(6)]

            done = [None] * 8
            for rt in range(8):
                ldev = [None]
                sev = [None]
                def y(yq, rt=rt, ldev=ldev):
                    if rt >= 2:
                        p.wait(yq, *done[rt - 2])
                    ldev[0] = p.inc(yq.dma_start(
                        out=xrot[rt % 2][:],
                        in_=self.x_fm[rt * 128:(rt + 1) * 128, :]), "LD")
                p.Y(y)
                def s_(ss, rt=rt, ldev=ldev, sev=sev):
                    p.wait(ss, ldev[0])
                    if rt >= 2:
                        p.wait(ss, p.now("T"))
                    sev[0] = p.inc(ss.activation(out=sqs[rt % 2][:],
                                                 in_=xrot[rt % 2][:],
                                                 func=AF.Square), "S")
                p.S(s_)
                def t(tt, rt=rt, sev=sev):
                    p.wait(tt, sev[0])
                    for c0, w in MCH:
                        tt.matmul(ps_st[0:1, c0:c0 + w], self.ones_bf[:],
                                  sqs[rt % 2][:, c0:c0 + w],
                                  start=(rt == 0), stop=(rt == 7))
                    p.incl(tt, 'T')
                p.T(t)
                def v(vv, rt=rt, ldev=ldev):
                    p.wait(vv, ldev[0])
                    p.inc(vv.tensor_copy(xbf[:, rt * XW:(rt + 1) * XW],
                                         xrot[rt % 2][:]), "V")
                p.V(v)
                done[rt] = [p.now("S"), p.now("V")]

            rowsl = [None, None]
            def evac_row(dst_row):
                rowsl[dst_row] = rows0 if dst_row == 0 else rows1
                def s_(ss):
                    p.wait(ss, p.now("T"))
                    p.inc(ss.copy((rows0 if dst_row == 0 else rows1)[0:1, :],
                                  ps_st[0:1, :]), "S")
                p.S(s_)

            def sum_pass(dst_row):
                def t(tt):
                    p.wait(tt, p.now("V"), p.now("S"))
                    for c0, w in MCH:
                        for rt in range(8):
                            tt.matmul(ps_st[0:1, c0:c0 + w], self.ones_bf[:],
                                      xbf[:, rt * XW + c0:rt * XW + c0 + w],
                                      start=(rt == 0), stop=(rt == 7))
                    p.incl(tt, 'T')
                p.T(t)
                evac_row(dst_row)

            def sq_pass(dst_row):
                for rt in range(8):
                    sev = [None]
                    def s_(ss, rt=rt, sev=sev):
                        if rt >= 2:
                            p.wait(ss, p.now("T"))
                        sev[0] = p.inc(ss.activation(
                            out=sqs[rt % 2][:], in_=xbf[:, rt * XW:(rt + 1) * XW],
                            func=AF.Square), "S")
                    p.S(s_)
                    def t(tt, rt=rt, sev=sev):
                        p.wait(tt, sev[0])
                        for c0, w in MCH:
                            tt.matmul(ps_st[0:1, c0:c0 + w], self.ones_bf[:],
                                      sqs[rt % 2][:, c0:c0 + w],
                                      start=(rt == 0), stop=(rt == 7))
                        p.incl(tt, 'T')
                    p.T(t)
                evac_row(dst_row)

            def rows_math():
                def v(vv):
                    p.wait(vv, p.now("S"))
                    vv.tensor_scalar_mul(rows0[0:1, :], rows0[0:1, :], 1.0 / 1024.0)
                    vv.tensor_scalar_mul(rows1[0:1, :], rows1[0:1, :], 1.0 / 1024.0)
                    vv.tensor_tensor(out=mub[0:1, :], in0=rows0[0:1, :],
                                     in1=rows0[0:1, :], op=ALU.mult)
                    p.inc(vv.tensor_tensor(out=rows1[0:1, :], in0=rows1[0:1, :],
                                           in1=mub[0:1, :], op=ALU.subtract), "V")
                p.V(v)
                def s_(ss):
                    p.wait(ss, p.now("V"))
                    ss.activation(out=rows1[0:1, :], in_=rows1[0:1, :], func=AF.Ln,
                                  bias=self.cc(self.C_EPS)[0:1, :])
                    p.inc(ss.activation(out=rows1[0:1, :], in_=rows1[0:1, :],
                                        func=AF.Exp,
                                        scale=self.cc(self.C_NH)[0:1, :]), "S")
                p.S(s_)

            def bcast_rows(off):
                def g(gg):
                    p.wait(gg, p.now("S"), p.now("V"))
                    gg.dma_start(out=self.sp_rows[off:off + 1, 0:XW],
                                 in_=rows0[0:1, :]).then_inc(p.sem["SP"], 16)
                    gg.dma_start(out=self.sp_rows[off + 1:off + 2, 0:XW],
                                 in_=rows1[0:1, :]).then_inc(p.sem["SP"], 16)
                    p.cnt["SP"] += 32
                    gg.wait_ge(p.sem["SP"], p.cnt["SP"])
                    gg.dma_start(out=mub[:], in_=self.sp_rows[
                        off:off + 1, 0:XW].broadcast_to([128, XW])
                    ).then_inc(p.sem["SP"], 16)
                    gg.dma_start(out=rb[:], in_=self.sp_rows[
                        off + 1:off + 2, 0:XW].broadcast_to([128, XW])
                    ).then_inc(p.sem["SP"], 16)
                    p.cnt["SP"] += 32
                p.G(g)

            evac_row(1)
            sum_pass(0)
            rows_math()
            bcast_rows(0)
            ev_bc = p.now("SP")

            for rt in range(8):
                ldev = [None]
                vev = [None]
                def y(yq, rt=rt, ldev=ldev):
                    if rt < 2:
                        p.wait(yq, p.now("V"), p.now("S"))
                    else:
                        p.wait(yq, done[rt - 2][0])
                    ldev[0] = p.inc(yq.dma_start(
                        out=xrot[rt % 2][:],
                        in_=self.x_fm[rt * 128:(rt + 1) * 128, :]), "LD")
                p.Y(y)
                def v(vv, rt=rt, ldev=ldev, vev=vev):
                    p.wait(vv, ldev[0], ev_bc)
                    if rt >= 2 and len(done[rt - 2]) > 1:
                        p.wait(vv, done[rt - 2][1])
                    sl = slice(rt * XW, (rt + 1) * XW)
                    vv.tensor_tensor(out=xrot[rt % 2][:], in0=xrot[rt % 2][:],
                                     in1=mub[:], op=ALU.subtract)
                    vv.tensor_tensor(out=xbf[:, sl], in0=xrot[rt % 2][:],
                                     in1=rb[:], op=ALU.mult)
                    vev[0] = p.inc(vv.tensor_tensor(
                        out=x32r[rt % 2][:],
                        in0=xrot[rt % 2][:, OWN0:OWN0 + TOK],
                        in1=rb[:, OWN0:OWN0 + TOK], op=ALU.mult), "V")
                    done[rt] = [p.now("V")]
                p.V(v)
                def g(gg, rt=rt, vev=vev):
                    p.wait(gg, vev[0])
                    p.inc(gg.dma_start(
                        out=self.sp_xn32[rt * 128:(rt + 1) * 128, :],
                        in_=x32r[rt % 2][:]), "SP")
                p.G(g)
                done[rt].append(p.now("SP"))

            sum_pass(0)
            sq_pass(1)
            rows_math()
            bcast_rows(2)
            ev_bc2 = p.now("SP")
            def v2(vv):
                p.wait(vv, ev_bc2)
                for rt in range(8):
                    sl = slice(rt * XW, (rt + 1) * XW)
                    vv.tensor_tensor(out=xrot[rt % 2][:], in0=xbf[:, sl],
                                     in1=mub[:], op=ALU.subtract)
                    vv.tensor_tensor(out=xrot[rt % 2][:], in0=xrot[rt % 2][:],
                                     in1=rb[:], op=ALU.mult)
                    p.inc(vv.tensor_scalar(
                        out=self.h2[:, sl], in0=xrot[rt % 2][:],
                        scalar1=self.lnwb_sb[:, rt:rt + 1],
                        scalar2=self.lnwb_sb[:, 8 + rt:9 + rt],
                        op0=ALU.mult, op1=ALU.add), "V")
            p.V(v2)

            def g2(gg):
                p.wait(gg, p.now("V"))
                for rt in range(8):
                    p.inc(gg.dma_start(
                        out=self.sp_xnbf[rt * 128:(rt + 1) * 128, :],
                        in_=xbf[:, rt * XW:(rt + 1) * XW]), "SP")
            p.G(g2)

            self.dump("h2", self.h2, 8, XW, src_width=XW)
            self.dump("xn", xbf, 8, XW, src_width=XW)
            p.barrier()

    def p1(self, d, rev):
        p, nc = self, self.nc
        sdir = slice(0, D_INNER) if d == "f" else slice(D_INNER, 2 * D_INNER)
        ph = contextlib.ExitStack()
        with ph:
            wxb = self.sbt(ph, "wxb", [128, 8 * 2208], BF16)
            wstg0 = self.sbt(ph, "wstg0", [128, 2208], F32)
            wstg = [wstg0, wstg0]
            convw_sb = self.sbt(ph, "convw_sb", [128, 68], F32)
            convb_sb = self.sbt(ph, "convb_sb", [128, 17], F32)
            mv_sb = self.sbt(ph, "mv_sb", [32, 4], F32)
            dexp_sb = self.sbt(ph, "dexp_sb", [128, 16], F32)
            WID = ST1 + 6
            XPRE = self.sbt(ph, "XPRE", [128, 17 * WID], F32)
            DTR = self.sbt(ph, "DTR", [32, WID], F32)
            XBC = self.sbt(ph, "XBC", [128, 17 * ST1], BF16)
            xDst = self.sbt(ph, "xDst", [128, 16 * ST1], BF16)
            ctmpa = self.sbt(ph, "ctmpa", [128, ST1], F32)
            ctmpb = self.sbt(ph, "ctmpb", [128, ST1], F32)
            caccf = self.sbt(ph, "caccf", [128, 2 * ST1], F32)
            dtv = self.sbt(ph, "dtv", [32, ST1], F32)
            lam = self.sbt(ph, "lam", [32, ST1], F32)
            lndt = self.sbt(ph, "lndt", [32, ST1], F32)
            zero32 = self.sbt(ph, "zero32", [32, Q], F32)
            clg = self.sbt(ph, "clg", [32, 2 * Q], F32)
            clx = self.sbt(ph, "clx", [32, 2 * Q], F32)
            gmr = self.sbt(ph, "gmr", [32, 2 * Q], F32)
            wdec = self.sbt(ph, "wdec", [32, Q], F32)
            dtw = self.sbt(ph, "dtw", [32, Q], F32)
            pk4 = self.sbt(ph, "pk4", [128, Q], F32)
            tctl = self.sbt(ph, "tctl", [128, 128], F32)
            trow12 = self.sbt(ph, "trow12", [1, 32], F32)
            drow = self.sbt(ph, "drow", [1, D_INNER], F32)
            xt_sb = self.sbt(ph, "xt_sb", [128, 2 * D_INNER], BF16)
            xwt_sb = self.sbt(ph, "xwt_sb", [128, D_INNER], BF16)
            bt_sb = self.sbt(ph, "bt_sb", [128, 64], BF16)
            sadd_sb = self.sbt(ph, "sadd_sb", [64, D_INNER], F32)
            dtile = self.sbt(ph, "dtile", [64, D_INNER], F32)
            ps_ip = self.pst(ph, "ps_ip", [128, 1024])
            ps_tpb = self.pst(ph, "ps_tpb", [128, 2048], BF16)
            ps_tpb2 = self.pst(ph, "ps_tpb2", [128, 64], BF16)
            ps_tctl = self.pst(ph, "ps_tctl", [128, 128])
            ps_sa = self.pst(ph, "ps_sa", [64, 1024])

            cast_ev = [None] * 8
            for kt in range(8):
                ldev = [None]
                def y(yq, kt=kt, ldev=ldev):
                    if kt >= 1:
                        p.wait(yq, cast_ev[kt - 1])
                    ldev[0] = p.inc(yq.dma_start(
                        out=wstg[kt % 2][:],
                        in_=self.wx[d][kt * 128:(kt + 1) * 128, :]), "LD")
                p.Y(y)
                def v(vv, kt=kt, ldev=ldev):
                    p.wait(vv, ldev[0])
                    cast_ev[kt] = p.inc(vv.tensor_copy(
                        wxb[:, kt * 2208:(kt + 1) * 2208], wstg[kt % 2][:]), "V")
                p.V(v)
            ev_wx = cast_ev[7]

            def y1(yq):
                for dst, src in ((convw_sb, self.convw[d]),
                                 (convb_sb, self.convb[d]),
                                 (dexp_sb, self.dexp[d])):
                    yq.dma_start(out=dst[:], in_=src[:, :]).then_inc(
                        p.sem["LD"], 16)
                yq.dma_start(out=mv_sb[:, 0:3], in_=self.mv[d][:, :]).then_inc(
                    p.sem["LD"], 16)
                p.cnt["LD"] += 64
            p.Y(y1)
            ev_small = p.now("LD")

            def s0(ss):
                p.wait(ss, ev_small)
                p.inc(ss.activation(out=mv_sb[:, 3:4], in_=mv_sb[:, 1:2],
                                    func=AF.Exp), "S")
            p.S(s0)
            def v0(vv):
                p.wait(vv, p.now("S"))
                vv.memset(zero32[:], 0.0)
                vv.tensor_scalar_mul(mv_sb[:, 3:4], mv_sb[:, 3:4], -1.0)
                p.inc(vv.memset(self.s_state[:, sdir], 0.0), "V")
            p.V(v0)

            sts = (list(range(P1NST)) if not rev else list(range(P1NST - 1, -1, -1))) if P1LVL >= 1 else []
            chain_ev = [("V", 0)]
            xtsp_ev = {0: ("SP", 0), 1: ("SP", 0)}
            for sti, st in enumerate(sts):
                c0 = OWN0 + st * ST1 - 3
                ipev = {}
                for nt in range(int(os.environ.get("KB_P1_NT", "18"))):
                    nw = 128 if nt < 17 else 32
                    def t1(tt, nt=nt, nw=nw):
                        if nt < 2:
                            p.wait(tt, ev_wx, p.now("V"), p.now("S"))
                        if nt >= 2:
                            p.wait(tt, ipev[nt - 2][1])
                        bk = (nt % 2) * 512
                        if not os.environ.get("KB_P1_NOMM"):
                            for kt in range(8):
                                tt.matmul(
                                    ps_ip[0:nw, bk:bk + WID],
                                    wxb[:, kt * 2208 + nt * 128:kt * 2208 + nt * 128 + nw],
                                    self.h2[:, kt * XW + c0:kt * XW + c0 + WID],
                                    start=(kt == 0), stop=(kt == 7))
                        ipev[nt] = [p.incl(tt, 'T'), None]
                    p.T(t1)
                    def s1(ss, nt=nt, nw=nw):
                        p.wait(ss, ipev[nt][0])
                        bk = (nt % 2) * 512
                        if nt < 17:
                            e = p.inc(ss.copy(XPRE[:, nt * WID:(nt + 1) * WID],
                                              ps_ip[:, bk:bk + WID]), "S")
                        else:
                            e = p.inc(ss.copy(DTR[:, :], ps_ip[0:32, bk:bk + WID]), "S")
                        ipev[nt][1] = e
                    p.S(s1)
                ev_xpre = p.now("S")

                if P1LVL < 2:
                    continue
                first_st = (st == 0)
                last_st = (st == NST1 - 1)
                def vmask(vv):
                    p.wait(vv, ev_xpre)
                    if first_st and 18 == int(os.environ.get("KB_P1_NT", "18")):
                        for ntt in range(17):
                            vv.tensor_scalar_mul(
                                XPRE[:, ntt * WID:ntt * WID + 3],
                                XPRE[:, ntt * WID:ntt * WID + 3], self.mf)
                    if last_st:
                        for ntt in range(17):
                            vv.tensor_scalar_mul(
                                XPRE[:, ntt * WID + WID - 3:(ntt + 1) * WID],
                                XPRE[:, ntt * WID + WID - 3:(ntt + 1) * WID],
                                self.mb)
                    p.incl(vv, 'V')
                p.V(vmask)
                ev_m = p.now("V")

                convev = {}
                offs = [0, 1, 2, 3] if not rev else [6, 5, 4, 3]
                for ct in range(17):
                    def vc(vv, ct=ct):
                        if ct == 0:
                            p.wait(vv, ev_m)
                        if ct >= 2:
                            p.wait(vv, convev[ct - 2])
                        base = ct * WID
                        vv.tensor_scalar(
                            out=ctmpa[:], in0=XPRE[:, base + offs[0]:base + offs[0] + ST1],
                            scalar1=convw_sb[:, ct * 4:ct * 4 + 1], scalar2=None,
                            op0=ALU.mult)
                        vv.scalar_tensor_tensor(
                            out=ctmpb[:], in0=XPRE[:, base + offs[1]:base + offs[1] + ST1],
                            scalar=convw_sb[:, ct * 4 + 1:ct * 4 + 2],
                            in1=ctmpa[:], op0=ALU.mult, op1=ALU.add)
                        vv.scalar_tensor_tensor(
                            out=ctmpa[:], in0=XPRE[:, base + offs[2]:base + offs[2] + ST1],
                            scalar=convw_sb[:, ct * 4 + 2:ct * 4 + 3],
                            in1=ctmpb[:], op0=ALU.mult, op1=ALU.add)
                        p.inc(vv.scalar_tensor_tensor(
                            out=caccf[:, (ct % 2) * ST1:(ct % 2 + 1) * ST1],
                            in0=XPRE[:, base + offs[3]:base + offs[3] + ST1],
                            scalar=convw_sb[:, ct * 4 + 3:ct * 4 + 4],
                            in1=ctmpa[:], op0=ALU.mult, op1=ALU.add), "V")
                    p.V(vc)
                    def sc(ss, ct=ct):
                        p.wait(ss, p.now("V"))
                        e = p.inc(ss.activation(
                            out=XBC[:, ct * ST1:(ct + 1) * ST1],
                            in_=caccf[:, (ct % 2) * ST1:(ct % 2 + 1) * ST1],
                            func=AF.Silu, bias=convb_sb[:, ct:ct + 1]), "S")
                        convev[ct] = e
                    p.S(sc)
                ev_xbc = p.now("S")

                def sdt(ss):
                    p.wait(ss, ipev[17][1], p.now("V"))
                    ss.activation(out=dtv[:], in_=DTR[:, 3:3 + ST1], func=AF.Exp,
                                  bias=mv_sb[:, 0:1])
                    ss.activation(out=dtv[:], in_=dtv[:], func=AF.Ln,
                                  bias=self.cc(self.C_ONE)[0:32, :])
                    p.inc(ss.activation(out=lndt[:], in_=dtv[:], func=AF.Ln,
                                        bias=self.cc(self.C_TINY)[0:32, :]), "S")
                p.S(sdt)
                def vdt(vv):
                    p.wait(vv, p.now("S"))
                    p.inc(vv.tensor_scalar(out=lam[:], in0=dtv[:],
                                           scalar1=mv_sb[:, 3:4], scalar2=None,
                                           op0=ALU.mult), "V")
                p.V(vdt)
                ev_dt = p.now("V")

                def vxd(vv):
                    p.wait(vv, ev_xbc)
                    for ct in range(16):
                        vv.tensor_scalar(
                            out=xDst[:, ct * ST1:(ct + 1) * ST1],
                            in0=XBC[:, ct * ST1:(ct + 1) * ST1],
                            scalar1=dexp_sb[:, ct:ct + 1], scalar2=None,
                            op0=ALU.mult)
                    p.incl(vv, 'V')
                p.V(vxd)
                ev_xd = p.now("V")

                def gxd(gg, st=st):
                    p.wait(gg, ev_xd, ev_xbc)
                    for ct in range(16):
                        p.inc(gg.dma_start(
                            out=self.sp_xD[d][(ct * 8 + st) * 128:
                                              (ct * 8 + st + 1) * 128, :],
                            in_=xDst[:, ct * ST1:(ct + 1) * ST1]), "SP")
                    p.inc(gg.dma_start(
                        out=self.sp_B[d][:, st * ST1:(st + 1) * ST1],
                        in_=XBC[0:64, 16 * ST1:17 * ST1]), "SP")
                    p.inc(gg.dma_start(
                        out=self.sp_C[d][:, st * ST1:(st + 1) * ST1],
                        in_=XBC[64:128, 16 * ST1:17 * ST1]), "SP")
                p.G(gxd)

                qs_l = ([0, 1] if not rev else [1, 0]) if P1LVL >= 3 else []
                for qi in qs_l:
                    cg = st * 2 + qi
                    qsl = slice(qi * Q, (qi + 1) * Q)
                    csl = slice(qi * Q, qi * Q + Q)
                    def vch(vv, qi=qi):
                        p.wait(vv, ev_dt)
                        vv.tensor_tensor_scan(
                            out=clg[:, qsl], data0=lam[:, qsl], data1=zero32[:],
                            initial=0.0, op0=ALU.add, op1=ALU.add)
                        if rev:
                            vv.tensor_tensor(out=clx[:, qsl], in0=clg[:, qsl],
                                             in1=lam[:, qsl], op=ALU.subtract)
                        else:
                            vv.tensor_copy(clx[:, qsl], clg[:, qsl])
                        if not rev:
                            vv.tensor_scalar(
                                out=wdec[:], in0=clg[:, qsl],
                                scalar1=clg[:, qi * Q + Q - 1:qi * Q + Q],
                                scalar2=None, op0=ALU.subtract)
                            vv.tensor_copy(gmr[:, qsl], clx[:, qsl])
                        else:
                            vv.tensor_copy(wdec[:], clx[:, qsl])
                            vv.tensor_scalar(
                                out=gmr[:, qsl], in0=clx[:, qsl],
                                scalar1=clg[:, qi * Q + Q - 1:qi * Q + Q],
                                scalar2=None, op0=ALU.subtract)
                        p.inc(vv.tensor_tensor(
                            out=pk4[0:32, :], in0=clx[:, qsl], in1=lndt[:, qsl],
                            op=(ALU.subtract if not rev else ALU.add)), "V")
                    p.V(vch)
                    def sch(ss, qi=qi):
                        p.wait(ss, p.now("V"))
                        if not rev:
                            ss.activation(out=wdec[:], in_=wdec[:], func=AF.Exp,
                                          scale=self.cc(self.C_M1)[0:32, :])
                            p.inc(ss.activation(out=gmr[:, qsl], in_=gmr[:, qsl],
                                                func=AF.Exp), "S")
                        else:
                            ss.activation(out=wdec[:], in_=wdec[:], func=AF.Exp)
                            p.inc(ss.activation(
                                out=gmr[:, qsl], in_=gmr[:, qsl], func=AF.Exp,
                                scale=self.cc(self.C_M1)[0:32, :]), "S")
                    p.S(sch)
                    def vch2(vv, qi=qi):
                        p.wait(vv, p.now("S"))
                        vv.tensor_tensor(out=dtw[:], in0=dtv[:, qsl],
                                         in1=wdec[:], op=ALU.mult)
                        vv.tensor_copy(pk4[32:64, :], lndt[:, qsl])
                        vv.tensor_copy(pk4[64:96, :], dtw[:])
                        p.inc(vv.tensor_copy(pk4[96:128, :], lam[:, qsl]), "V")
                    p.V(vch2)
                    ev_pk = p.now("V")
                    def ttc(tt, qi=qi):
                        p.wait(tt, ev_pk, p.now("S"))
                        p.inc(tt.transpose(ps_tctl[:], pk4[:], self.ident_f[:]), "T")
                    p.T(ttc)
                    def stc(ss, qi=qi):
                        p.wait(ss, p.now("T"))
                        p.inc(ss.copy(tctl[:], ps_tctl[:]), "S")
                    p.S(stc)
                    ev_tctl = p.now("S")

                    def gdr(gg, qi=qi):
                        p.wait(gg, p.now("V"), p.now("S"))
                        p.inc(gg.dma_start(
                            out=trow12[:],
                            in_=clg[:, qi * Q + Q - 1:qi * Q + Q],
                            allow_slow_non_contiguous=True), "SP")
                    p.G(gdr)
                    ev_tr = p.now("SP")
                    def sdr(ss, qi=qi):
                        p.wait(ss, ev_tr)
                        p.inc(ss.activation(
                            out=drow[:].rearrange("p (a b) -> p a b", b=64),
                            in_=rep_inner(trow12[:], 64),
                            func=AF.Exp), "S")
                    p.S(sdr)
                    ev_drow = p.now("S")

                    def gsp(gg, cg=cg, qi=qi):
                        p.wait(gg, ev_drow, p.now("V"))
                        gg.dma_start(out=self.sp_cl[d][:, cg * Q:(cg + 1) * Q],
                                     in_=clx[:, qsl]).then_inc(p.sem["SP"], 16)
                        gg.dma_start(out=self.sp_gam[d][:, cg * Q:(cg + 1) * Q],
                                     in_=gmr[:, qsl]).then_inc(p.sem["SP"], 16)
                        gg.dma_start(out=self.sp_ctl[d][cg * Q:(cg + 1) * Q, :],
                                     in_=tctl[:, 0:64]).then_inc(p.sem["SP"], 16)
                        gg.dma_start(out=self.sp_dr[d][cg:cg + 1, :],
                                     in_=drow[:]).then_inc(p.sem["SP"], 16)
                        p.cnt["SP"] += 64
                    p.G(gsp)

                    tev = [None]
                    def ttp(tt, qi=qi, tev=tev):
                        p.wait(tt, ev_xbc, p.now("V"))
                        for hp in range(16):
                            tt.transpose(
                                ps_tpb[:, hp * 128:(hp + 1) * 128],
                                XBC[:, hp * ST1 + qi * Q: hp * ST1 + qi * Q + Q],
                                self.ident_bf[:])
                        tev[0] = p.inc(tt.transpose(
                            ps_tpb2[:, 0:64],
                            XBC[0:64, 16 * ST1 + qi * Q:16 * ST1 + qi * Q + Q],
                            self.ident_bf[0:64, 0:64]), "T")
                    p.T(ttp)
                    def vev_(vv, qi=qi, tev=tev):
                        p.wait(vv, tev[0], xtsp_ev[qi % 2])
                        xts = xt_sb[:, (qi % 2) * D_INNER:(qi % 2 + 1) * D_INNER]
                        vv.tensor_copy(xts, ps_tpb[:])
                        vv.tensor_copy(bt_sb[:, 0:64], ps_tpb2[:, 0:64])
                        p.inc(vv.tensor_tensor(
                            out=xwt_sb[:].rearrange("p (a b) -> p a b", b=64),
                            in0=xts.rearrange("p (a b) -> p a b", b=64),
                            in1=rep_inner(tctl[:, 64:96], 64),
                            op=ALU.mult), "V")
                    p.V(vev_)
                    ev_xw = p.now("V")

                    def gxt(gg, cg=cg, qi=qi):
                        p.wait(gg, ev_xw)
                        xtsp_ev[qi % 2] = p.inc(gg.dma_start(
                            out=self.sp_xT[d][cg * Q:(cg + 1) * Q, :],
                            in_=xt_sb[:, (qi % 2) * D_INNER:(qi % 2 + 1) * D_INNER]), "SP")
                    p.G(gxt)

                    for half in range(2):
                        def tsa(tt, qi=qi, half=half):
                            p.wait(tt, ev_xw)
                            if half == 1:
                                p.wait(tt, p.now("S"))
                            for mi in range(2):
                                m = half * 2 + mi
                                tt.matmul(ps_sa[:, mi * 512:(mi + 1) * 512],
                                          bt_sb[:, 0:64],
                                          xwt_sb[:, m * 512:(m + 1) * 512],
                                          start=True, stop=True)
                            p.incl(tt, 'T')
                        p.T(tsa)
                        def ssa(ss, qi=qi, half=half):
                            p.wait(ss, p.now("T"), chain_ev[0])
                            p.inc(ss.copy(
                                sadd_sb[:, half * 1024:(half + 1) * 1024],
                                ps_sa[:]), "S")
                        p.S(ssa)
                    ev_sadd = p.now("S")
                    def gsa(gg, cg=cg):
                        p.wait(gg, ev_sadd)
                        p.inc(gg.dma_start(
                            out=self.sp_sadd[d][:, cg * D_INNER:(cg + 1) * D_INNER],
                            in_=sadd_sb[:]), "SP")
                        gg.wait_ge(p.sem["SP"], p.cnt["SP"])
                        p.inc(gg.dma_start(
                            out=dtile[:], in_=self.sp_dr[d][cg:cg + 1, :]
                            .broadcast_to([64, D_INNER])), "SP")
                    p.G(gsa)
                    ev_dtile = p.now("SP")

                    def vchain(vv, qi=qi):
                        p.wait(vv, ev_dtile)
                        vv.tensor_tensor(out=self.s_state[:, sdir],
                                         in0=self.s_state[:, sdir],
                                         in1=dtile[:], op=ALU.mult)
                        chain_ev[0] = p.inc(vv.tensor_tensor(
                            out=self.s_state[:, sdir], in0=self.s_state[:, sdir],
                            in1=sadd_sb[:], op=ALU.add), "V")
                    p.V(vchain)

            for kt in range(8):
                ldev = [None]
                def y(yq, kt=kt, ldev=ldev):
                    p.wait(yq, cast_ev[kt - 1] if kt >= 1 else ev_wx)
                    ldev[0] = p.inc(yq.dma_start(
                        out=wstg[kt % 2][:, 0:D_INNER],
                        in_=self.wz[d][kt * 128:(kt + 1) * 128, :]), "LD")
                p.Y(y)
                def v(vv, kt=kt, ldev=ldev):
                    p.wait(vv, ldev[0], p.now("T"))
                    cast_ev[kt] = p.inc(vv.tensor_copy(
                        wxb[:, kt * D_INNER:(kt + 1) * D_INNER],
                        wstg[kt % 2][:, 0:D_INNER]), "V")
                p.V(v)
            ev_wz = cast_ev[7]

            zrot = self.sbt(ph, "zrot", [128, 1024], F32)
            zsev = {}
            zgev = {}
            for zst in range(4 if P1LVL >= 4 else 0):
                zc0 = OWN0 + zst * 512
                for nt in range(16):
                    gi = zst * 16 + nt
                    zev = [None]
                    def t(tt, zst=zst, nt=nt, zev=zev, gi=gi):
                        if gi == 0:
                            p.wait(tt, ev_wz)
                        if gi >= 2:
                            p.wait(tt, zsev[gi - 2])
                        bk = (nt % 2) * 512
                        for kt in range(8):
                            tt.matmul(ps_ip[:, bk:bk + 512],
                                      wxb[:, kt * D_INNER + nt * 128:
                                          kt * D_INNER + nt * 128 + 128],
                                      self.h2[:, kt * XW + zc0:kt * XW + zc0 + 512],
                                      start=(kt == 0), stop=(kt == 7))
                        zev[0] = p.incl(tt, 'T')
                    p.T(t)
                    def s_(ss, zst=zst, nt=nt, zev=zev, gi=gi):
                        p.wait(ss, zev[0])
                        if gi >= 2:
                            p.wait(ss, zgev[gi - 2])
                        zsev[gi] = p.inc(ss.activation(
                            out=zrot[:, (nt % 2) * 512:(nt % 2 + 1) * 512],
                            in_=ps_ip[:, (nt % 2) * 512:(nt % 2) * 512 + 512],
                            func=AF.Silu), "S")
                    p.S(s_)
                    def g(gg, zst=zst, nt=nt, gi=gi):
                        p.wait(gg, zsev[gi])
                        for j in range(2):
                            ins = gg.dma_start(
                                out=self.sp_zs[d][
                                    (nt * 8 + 2 * zst + j) * 128:
                                    (nt * 8 + 2 * zst + j + 1) * 128, :],
                                in_=zrot[:, (nt % 2) * 512 + j * 256:
                                         (nt % 2) * 512 + j * 256 + 256])
                            zgev[gi] = p.inc(ins, "SP")
                    p.G(g)

            rowsl = slice(0, 64) if d == "f" else slice(64, 128)
            def gsend(gg):
                p.wait(gg, p.now("V"))
                p.inc(gg.dma_start(out=self.cc1_in[rowsl, 0:D_INNER],
                                   in_=self.s_state[:, sdir]), "SP")
            p.G(gsend)

            if f"xbc_{d}" in self.dbg:
                self.dump(f"xbc_{d}", XBC, 17, ST1, src_width=ST1)
            p.barrier()

    def dump_dram(self, name, src, shape):
        d = self.mkdbg(name, list(shape))
        if d is None:
            return
        p = self
        ph = contextlib.ExitStack()
        with ph:
            bb = self.sbt(ph, "dbg_b_" + name, [128, shape[1]], F32)
            nrt = (shape[0] + 127) // 128
            for rt in range(nrt):
                r0 = rt * 128
                r1 = min(shape[0], r0 + 128)
                ev = [None]
                def g1(gg, r0=r0, r1=r1, ev=ev):
                    p.wait(gg, p.now("SP"), p.now("V"), p.now("S"), p.now("T"))
                    ev[0] = p.inc(gg.dma_start(out=bb[0:r1 - r0, :],
                                               in_=src[r0:r1, :]), "SP")
                p.G(g1)
                def g2(gg, r0=r0, r1=r1, ev=ev):
                    p.wait(gg, ev[0])
                    p.inc(gg.dma_start(out=d[r0:r1, :], in_=bb[0:r1 - r0, :]), "SP")
                p.G(g2)
            self.barrier()

    def x1_issue(self, stream):
        p = self
        def g(gg):
            p.wait(gg, p.now("V"))
            p.inc(gg.dma_start(
                out=self.cc1_in[:, 2048:2056],
                in_=stream[:].rearrange("p (a b) -> p a b", b=SEG)[:, :, SEG - 1:SEG],
                allow_slow_non_contiguous=True), "SP")
            gg.wait_ge(p.sem["SP"], p.cnt["SP"])
            p.inc(gg.collective_compute(
                "AllGather", ALU.bypass, replica_groups=PAIRS,
                ins=[self.cc1_in[:, :]], outs=[self.cc1_out[:, :]]), "G")
        p.G(g)
        self.ev_x1 = p.now("G")

    def x1_consume(self, ph):
        p = self
        tf = [self.sbt(ph, f"x1tf{i}", [64, D_INNER], F32) for i in range(2)]
        tr = [self.sbt(ph, f"x1tr{i}", [64, D_INNER], F32) for i in range(2)]
        tm = [self.sbt(ph, f"x1tm{i}", [128, 8], F32) for i in range(2)]
        ev = [None]
        def g(gg):
            p.wait(gg, self.ev_x1)
            for sl, base in ((0, 0), (1, 128)):
                gg.dma_start(out=tf[sl][:],
                             in_=self.cc1_out[base:base + 64, 0:D_INNER]
                             ).then_inc(p.sem["SP"], 16)
                gg.dma_start(out=tr[sl][:],
                             in_=self.cc1_out[base + 64:base + 128, 0:D_INNER]
                             ).then_inc(p.sem["SP"], 16)
                gg.dma_start(out=tm[sl][:],
                             in_=self.cc1_out[base:base + 128, 2048:2056]
                             ).then_inc(p.sem["SP"], 16)
            p.cnt["SP"] += 96
            ev[0] = p.now("SP")
        p.G(g)
        x1c_sec = int(os.environ.get("KB_X1C_SEC", "99"))
        if x1c_sec <= 1:
            return
        def v(vv):
            p.wait(vv, ev[0])
            F = self.flags_sb
            vv.tensor_scalar(out=self.s_state[:, 0:D_INNER], in0=tf[0][:],
                             scalar1=F[0:64, 2:3], scalar2=None, op0=ALU.mult)
            vv.scalar_tensor_tensor(
                out=self.s_state[:, 0:D_INNER], in0=tf[1][:],
                scalar=F[0:64, 3:4], in1=self.s_state[:, 0:D_INNER],
                op0=ALU.mult, op1=ALU.add)
            vv.tensor_scalar(out=self.s_state[:, D_INNER:], in0=tr[0][:],
                             scalar1=F[0:64, 4:5], scalar2=None, op0=ALU.mult)
            vv.scalar_tensor_tensor(
                out=self.s_state[:, D_INNER:], in0=tr[1][:],
                scalar=F[0:64, 5:6], in1=self.s_state[:, D_INNER:],
                op0=ALU.mult, op1=ALU.add)
            vv.tensor_scalar(out=tm[0][:], in0=tm[0][:],
                             scalar1=F[:, 2:3], scalar2=None, op0=ALU.mult)
            p.inc(vv.scalar_tensor_tensor(
                out=tm[0][:], in0=tm[1][:], scalar=F[:, 3:4],
                in1=tm[0][:], op0=ALU.mult, op1=ALU.add), "V")
        p.V(v)
        if x1c_sec <= 2:
            return
        t0 = tm[0]
        def g2(gg):
            p.wait(gg, p.now("V"))
            for rt in range(8):
                p.inc(gg.dma_start(
                    out=self.sp_s1[rt * 128:(rt + 1) * 128, 0:1],
                    in_=t0[:, rt:rt + 1],
                    allow_slow_non_contiguous=True), "SP")
        p.G(g2)

    def p3(self, d, rev):
        p, nc = self, self.nc
        sdir = slice(0, D_INNER) if d == "f" else slice(D_INNER, 2 * D_INNER)
        STP = 256
        ph = contextlib.ExitStack()
        with ph:
            zs = self.sbt(ph, "zs", [128, 16 * STP], F32)
            xD = self.sbt(ph, "xD", [128, 16 * STP], BF16)
            xt0 = self.sbt(ph, "xt0", [128, D_INNER], BF16)
            xt1 = self.sbt(ph, "xt1", [128, D_INNER], BF16)
            xt = [xt0, xt1]
            Bc = self.sbt(ph, "Bc", [64, STP], BF16)
            Cc = self.sbt(ph, "Cc", [64, STP], BF16)
            ctl = [self.sbt(ph, f"ctl{i}", [128, 64], F32) for i in range(2)]
            CLB = self.sbt(ph, "CLB", [128, NHEADS * Q], F32)
            gmb = self.sbt(ph, "gmb", [64, NHEADS * Q], F32)
            tprime = CLB
            Ee = self.sbt(ph, "Ee", [128, NHEADS * Q], BF16)
            Sreal = self.sbt(ph, "Sreal", [128, NHEADS * Q], BF16)
            Svirt = self.sbt(ph, "Svirt", [64, NHEADS * Q], BF16)
            Gm = self.sbt(ph, "Gm", [128, Q], F32)
            y_sb = self.sbt(ph, "y_sb", [128, 16 * STP], F32)
            sadd20 = self.sbt(ph, "sadd20", [64, D_INNER], F32)
            sadd21 = self.sbt(ph, "sadd21", [64, D_INNER], F32)
            s_bf = self.sbt(ph, "s_bf", [64, D_INNER], BF16)
            sadd2 = [sadd20, sadd21]
            dtile = self.sbt(ph, "dtile", [64, D_INNER], F32)
            gbf = self.sbt(ph, "gbf", [128, 16 * STP], BF16)
            woutb = self.sbt(ph, "woutb", [128, 16 * 1024], BF16)
            guardhi = self.sbt(ph, "guardhi", [128, 1024], BF16)
            wstg = [self.sbt(ph, f"wstg3{i}", [128, 1024], F32) for i in range(2)]
            rnrow = self.sbt(ph, "rnrow", [1, STP], F32)
            rnb = self.sbt(ph, "rnb", [128, STP], F32)
            oacc = self.sbt(ph, "oacc", [128, STP], F32)
            ps_y = self.pst(ph, "ps_y", [128, 2048])
            ps_g = self.pst(ph, "ps_g", [128, 512])
            ps_r = self.pst(ph, "ps_r", [1, 512])
            ps_o = self.pst(ph, "ps_o", [128, 1024])

            rmsw_sb = self.sbt(ph, "rmsw_sb", [128, 16], F32)
            cast_ev = [None] * 16
            def y0(yq):
                p.inc(yq.dma_start(out=rmsw_sb[:], in_=self.rmsw[d][:, :]), "LD")
            p.Y(y0)
            ev_rm = p.now("LD")
            for kt in range(16):
                ldev = [None]
                def y(yq, kt=kt, ldev=ldev):
                    if kt >= 2:
                        p.wait(yq, cast_ev[kt - 2])
                    ldev[0] = p.inc(yq.dma_start(
                        out=wstg[kt % 2][:],
                        in_=self.wout[d][kt * 128:(kt + 1) * 128, :]), "LD")
                p.Y(y)
                def v(vv, kt=kt, ldev=ldev):
                    p.wait(vv, ldev[0], ev_rm)
                    cast_ev[kt] = p.inc(vv.tensor_scalar(
                        out=woutb[:, kt * 1024:(kt + 1) * 1024],
                        in0=wstg[kt % 2][:], scalar1=rmsw_sb[:, kt:kt + 1],
                        scalar2=None, op0=ALU.mult), "V")
                p.V(v)
            ev_wo = cast_ev[15]
            dwb0 = self.mkdbg(f"woutb0_{d}", [128, 16 * 1024])
            if dwb0 is not None:
                def gdwb0(gg, dwb0=dwb0):
                    p.wait(gg, ev_wo)
                    p.inc(gg.dma_start(out=dwb0[:, :], in_=woutb[:]), "SP")
                p.G(gdwb0)
                p.barrier()

            p3s = int(os.environ.get("KB_P3_SEC", "99"))
            prev_ep = {"v": None, "t": None}
            sts = list(range(8)) if not rev else list(range(7, -1, -1))
            chain_ev = [p.now("V")]
            ldch = {}
            prev = {"yeev": (0, 0), "yev": (0, 0), "eev": (0, 0),
                    "svev": (0, 0), "goev": (0, 0)}
            for sti, st in enumerate(sts):
                scol = slice(st * STP, (st + 1) * STP)
                ldst = [None]
                def yst(yq, st=st, ldst=ldst):
                    if sti > 0:
                        if prev_ep["v"] is not None:
                            p.wait(yq, prev_ep["v"], prev_ep["t"], p.now("S"))
                        else:
                            p.wait(yq, p.now("V"), p.now("T"), p.now("S"))
                    for ct in range(16):
                        yq.dma_start(
                            out=zs[:, ct * STP:(ct + 1) * STP],
                            in_=self.sp_zs[d][(ct * 8 + st) * 128:
                                              (ct * 8 + st + 1) * 128, :]
                        ).then_inc(p.sem["LD"], 16)
                        yq.dma_start(
                            out=xD[:, ct * STP:(ct + 1) * STP],
                            in_=self.sp_xD[d][(ct * 8 + st) * 128:
                                              (ct * 8 + st + 1) * 128, :]
                        ).then_inc(p.sem["LD"], 16)
                    yq.dma_start(out=Bc[:], in_=self.sp_B[d][:, scol]).then_inc(
                        p.sem["LD"], 16)
                    yq.dma_start(out=Cc[:], in_=self.sp_C[d][:, scol]).then_inc(
                        p.sem["LD"], 16)
                    p.cnt["LD"] += 34 * 16
                    ldst[0] = p.now("LD")
                p.Y(yst)
                if p3s <= 1:
                    continue

                qs_l = [0, 1] if not rev else [1, 0]
                for qi in qs_l:
                    cg = st * 2 + qi
                    ldq = [None]
                    def yq_(yq, cg=cg, qi=qi, ldq=ldq):
                        p.wait(yq, ldch.get((qi % 2), (0, 0)))
                        yq.dma_start(out=xt[qi % 2][:],
                                     in_=self.sp_xT[d][cg * Q:(cg + 1) * Q, :]
                                     ).then_inc(p.sem["LD"], 16)
                        yq.dma_start(out=ctl[qi % 2][:],
                                     in_=self.sp_ctl[d][cg * Q:(cg + 1) * Q, :]
                                     ).then_inc(p.sem["LD"], 16)
                        yq.dma_start(out=sadd2[qi % 2][:],
                                     in_=self.sp_sadd[d][:, cg * D_INNER:(cg + 1) * D_INNER]
                                     ).then_inc(p.sem["LD"], 16)
                        yq.dma_start(out=CLB[:], in_=rep_outer(
                            self.sp_cl[d][:, cg * Q:(cg + 1) * Q].transpose([1, 0])
                            if False else self.sp_cl[d][0:1, 0:1], 1)
                        ).then_inc(p.sem["LD"], 16) if False else None
                        p.cnt["LD"] += 48
                        ldq[0] = p.now("LD")
                    p.Y(yq_)
                    ldb = [None]
                    def gq(gg, cg=cg, ldb=ldb):
                        p.wait(gg, p.now("V"), prev["eev"])
                        src = self.sp_cl[d].rearrange(
                            "h (c i) -> h c i", i=Q)[:, cg, :]
                        src = src.rearrange(
                            "h (a i) -> a h i", a=1).broadcast_to([128, 32, Q])
                        gg.dma_start(out=CLB[:].rearrange("p (h i) -> p h i", i=Q),
                                     in_=src).then_inc(p.sem["SP"], 16)
                        src2 = self.sp_gam[d].rearrange(
                            "h (c i) -> h c i", i=Q)[:, cg, :]
                        src2 = src2.rearrange("h (a i) -> a h i", a=1).broadcast_to([64, 32, Q])
                        gg.dma_start(out=gmb[:].rearrange("p (h i) -> p h i", i=Q),
                                     in_=src2).then_inc(p.sem["SP"], 16)
                        gg.dma_start(
                            out=dtile[:], in_=self.sp_dr[d][cg:cg + 1, :]
                            .broadcast_to([64, D_INNER])
                        ).then_inc(p.sem["SP"], 16)
                        p.cnt["SP"] += 48
                        ldb[0] = p.now("SP")
                    p.G(gq)
                    if p3s <= 2:
                        continue

                    gev = [None]
                    def tg(tt, qi=qi, gev=gev, ldst=ldst):
                        p.wait(tt, ldst[0], prev.get("gsev", (0, 0)))
                        gev[0] = p.inc(tt.matmul(
                            ps_g[:, 0:Q], Bc[:, qi * Q:(qi + 1) * Q],
                            Cc[:, qi * Q:(qi + 1) * Q], start=True, stop=True), "T")
                    p.T(tg)
                    gsev = [None]
                    def sg2(ss, gev=gev, gsev=gsev):
                        p.wait(ss, gev[0], prev["svev"])
                        gsev[0] = p.inc(ss.copy(Gm[:], ps_g[:, 0:Q]), "S")
                    p.S(sg2)
                    gmev = [None]
                    def gg2(gg, gsev=gsev, gmev=gmev):
                        p.wait(gg, gsev[0])
                        if not rev:
                            gmev[0] = p.inc(gg.affine_select(
                                out=Gm[:], in_=Gm[:], pattern=[[1, Q]],
                                compare_op=ALU.is_ge, fill=0.0, base=0,
                                channel_multiplier=-1), "G")
                        else:
                            gmev[0] = p.inc(gg.affine_select(
                                out=Gm[:], in_=Gm[:], pattern=[[-1, Q]],
                                compare_op=ALU.is_ge, fill=0.0, base=0,
                                channel_multiplier=1), "G")
                    p.G(gg2)

                    vtev = [None]
                    def vt(vv, qi=qi, ldq=ldq, ldb=ldb, vtev=vtev):
                        p.wait(vv, ldq[0], ldb[0], prev["eev"])
                        cfT = rep_inner(ctl[qi % 2][:, 0:32], Q)
                        clb3 = CLB[:].rearrange("p (h i) -> p h i", i=Q)
                        tp3 = tprime[:].rearrange("p (h i) -> p h i", i=Q)
                        if not rev:
                            vv.tensor_tensor(out=tp3, in0=clb3, in1=cfT,
                                             op=ALU.subtract)
                        else:
                            vv.tensor_tensor(out=tp3, in0=cfT, in1=clb3,
                                             op=ALU.subtract)
                        vtev[0] = p.inc(vv.tensor_tensor(
                            out=tp3, in0=tp3,
                            in1=rep_inner(ctl[qi % 2][:, 32:64], Q),
                            op=ALU.min), "V")
                    p.V(vt)
                    eev = [None]
                    def se(ss, vtev=vtev, eev=eev):
                        p.wait(ss, vtev[0])
                        eev[0] = p.inc(ss.activation(out=Ee[:], in_=tprime[:],
                                                     func=AF.Exp), "S")
                    p.S(se)
                    svev = [None]
                    def vs(vv, gmev=gmev, eev=eev, ldb=ldb, svev=svev):
                        p.wait(vv, gmev[0], eev[0])
                        vv.tensor_tensor(
                            out=Sreal[:].rearrange("p (h i) -> p h i", i=Q),
                            in0=rep_outer(Gm[:], NHEADS),
                            in1=Ee[:].rearrange("p (h i) -> p h i", i=Q),
                            op=ALU.mult)
                        svev[0] = p.inc(vv.tensor_tensor(
                            out=Svirt[:].rearrange("p (h i) -> p h i", i=Q),
                            in0=rep_outer(Cc[0:64, (0 if True else 0):Q] if False
                                          else Cc[:, 0:Q], NHEADS) if False else
                            rep_outer(Cc[:, 0:Q], NHEADS),
                            in1=gmb[:].rearrange("p (h i) -> p h i", i=Q),
                            op=ALU.mult), "V")
                    def vs_fixed(vv, qi=qi, gmev=gmev, eev=eev, ldb=ldb, svev=svev):
                        p.wait(vv, gmev[0], eev[0], ldb[0], prev["yev"])
                        vv.tensor_tensor(
                            out=Sreal[:].rearrange("p (h i) -> p h i", i=Q),
                            in0=rep_outer(Gm[:], NHEADS),
                            in1=Ee[:].rearrange("p (h i) -> p h i", i=Q),
                            op=ALU.mult)
                        vv.tensor_tensor(
                            out=Svirt[:].rearrange("p (h i) -> p h i", i=Q),
                            in0=rep_outer(Cc[:, qi * Q:(qi + 1) * Q], NHEADS),
                            in1=gmb[:].rearrange("p (h i) -> p h i", i=Q),
                            op=ALU.mult)
                        p.wait(vv, chain_ev[0])
                        svev[0] = p.inc(vv.tensor_copy(
                            s_bf[:], self.s_state[:, sdir]), "V")
                    p.V(vs_fixed)
                    if p3s <= 3:
                        continue

                    if cg == 0 and d == "f":
                        dsr = self.mkdbg("sreal0", [128, NHEADS * Q])
                        dclb = self.mkdbg("clb0", [128, NHEADS * Q])
                        dgm = self.mkdbg("gm0", [128, Q])
                        dxt = self.mkdbg("xt0d", [128, D_INNER])
                        if dsr is not None:
                            def gdsr(gg):
                                p.wait(gg, p.now("V"), p.now("S"), p.now("G"))
                                p.inc(gg.dma_start(out=dsr[:, :], in_=Sreal[:]), "SP")
                                p.inc(gg.dma_start(out=dclb[:, :], in_=CLB[:]), "SP")
                                p.inc(gg.dma_start(out=dgm[:, :], in_=Gm[:]), "SP")
                                p.inc(gg.dma_start(out=dxt[:, :], in_=xt[qi % 2][:]), "SP")
                            p.G(gdsr)
                            p.barrier()
                    yev = [None, None]
                    yeev = [None, None]
                    for grp in range(2):
                        def tyg(tt, qi=qi, grp=grp):
                            if grp == 0:
                                p.wait(tt, svev[0], ldq[0], chain_ev[0],
                                       prev["yeev"])
                            else:
                                p.wait(tt, yeev[0])
                            novirt = bool(os.environ.get("KB_P3_NOVIRT"))
                            for hp in range(grp * 8, grp * 8 + 8):
                                st_ap = xt[qi % 2][:, hp * 128:(hp + 1) * 128]
                                hl = hp % 8
                                first = (hl % 4 == 0)
                                last = (hl % 4 == 3)
                                for hh in range(2):
                                    h = hp * 2 + hh
                                    pslice = ps_y[
                                        hh * 64:(hh + 1) * 64,
                                        hh * 1024 + hl * 128:
                                        hh * 1024 + hl * 128 + 128]
                                    tt.matmul(
                                        pslice,
                                        st_ap[:, hh * 64:(hh + 1) * 64],
                                        Sreal[:, h * Q:(h + 1) * Q],
                                        start=first,
                                        stop=(last and novirt),
                                        tile_position=(0, hh * 64))
                                    if not novirt:
                                        tt.matmul(
                                            pslice,
                                            s_bf[:, h * 64:(h + 1) * 64],
                                            Svirt[:, h * Q:(h + 1) * Q],
                                            start=False, stop=last,
                                            tile_position=(0, hh * 64))
                            yev[grp] = p.incl(tt, 'T')
                        p.T(tyg)

                        def vyg(vv, qi=qi, grp=grp):
                            p.wait(vv, yev[grp])
                            for hh in range(2):
                                hps = slice(grp * 8, grp * 8 + 8)
                                xAP = xD[hh * 64:(hh + 1) * 64, :].rearrange(
                                    "p (c w) -> p c w", w=STP)[
                                    :, hps, qi * Q:(qi + 1) * Q]
                                yAP = y_sb[hh * 64:(hh + 1) * 64, :].rearrange(
                                    "p (c w) -> p c w", w=STP)[
                                    :, hps, qi * Q:(qi + 1) * Q]
                                pAP = ps_y[hh * 64:(hh + 1) * 64,
                                           hh * 1024:hh * 1024 + 1024].rearrange(
                                    "p (c w) -> p c w", w=128)
                                if os.environ.get("KB_P3_NOD"):
                                    vv.scalar_tensor_tensor(
                                        out=yAP, in0=pAP, scalar=1.0, in1=pAP,
                                        op0=ALU.mult, op1=ALU.bypass)
                                else:
                                    vv.scalar_tensor_tensor(
                                        out=yAP, in0=xAP, scalar=1.0, in1=pAP,
                                        op0=ALU.mult, op1=ALU.add)
                            yeev[grp] = p.incl(vv, 'V')
                        p.V(vyg)

                    def vchain(vv, qi=qi, ldb=ldb, yev=yev):
                        p.wait(vv, ldb[0], yev[1])
                        vv.tensor_tensor(out=self.s_state[:, sdir],
                                         in0=self.s_state[:, sdir],
                                         in1=dtile[:], op=ALU.mult)
                        chain_ev[0] = p.inc(vv.tensor_tensor(
                            out=self.s_state[:, sdir], in0=self.s_state[:, sdir],
                            in1=sadd2[qi % 2][:], op=ALU.add), "V")
                    p.V(vchain)
                    ldch[qi % 2] = chain_ev[0]
                    prev["yeev"] = yeev[1]
                    prev["yev"] = yev[1]
                    prev["eev"] = eev[0]
                    prev["svev"] = svev[0]
                    prev["gsev"] = gsev[0]

                if p3s <= 4:
                    continue
                dy = self.mkdbg(f"y_{d}", [D_INNER, TOK])
                if dy is not None:
                    def gdy(gg, st=st, dy=dy):
                        p.wait(gg, p.now("V"))
                        for ct in range(16):
                            p.inc(gg.dma_start(
                                out=dy[ct * 128:(ct + 1) * 128,
                                       st * STP:(st + 1) * STP],
                                in_=y_sb[:, ct * STP:(ct + 1) * STP]), "SP")
                    p.G(gdy)
                    p.barrier()
                def vgate(vv, st=st):
                    p.wait(vv, p.now("V"), p.now("T"))
                    vv.tensor_tensor(out=y_sb[:], in0=y_sb[:], in1=zs[:],
                                     op=ALU.mult)
                    prev_ep["v"] = p.inc(vv.tensor_copy(gbf[:], y_sb[:]), "V")
                p.V(vgate)
                dg = self.mkdbg(f"gate_{d}", [D_INNER, TOK])
                if dg is not None:
                    def gdg(gg, st=st, dg=dg):
                        p.wait(gg, p.now("V"))
                        for ct in range(16):
                            p.inc(gg.dma_start(
                                out=dg[ct * 128:(ct + 1) * 128,
                                       st * STP:(st + 1) * STP],
                                in_=y_sb[:, ct * STP:(ct + 1) * STP]), "SP")
                    p.G(gdg)
                    p.barrier()
                def sgate(ss, st=st):
                    p.wait(ss, p.now("V"))
                    p.inc(ss.activation(out=xD[:], in_=y_sb[:], func=AF.Square), "S")
                p.S(sgate)
                def trms(tt, st=st):
                    p.wait(tt, p.now("S"))
                    for ct in range(16):
                        tt.matmul(ps_r[:, 0:STP], self.ones_bf[:],
                                  xD[:, ct * STP:(ct + 1) * STP],
                                  start=(ct == 0), stop=(ct == 15))
                    prev_ep["t"] = p.incl(tt, 'T')
                p.T(trms)
                def srms(ss, st=st):
                    p.wait(ss, p.now("T"))
                    ss.activation(out=rnrow[:], in_=ps_r[0:1, 0:STP], func=AF.Ln,
                                  scale=self.cc(self.C_IDIM)[0:1, :],
                                  bias=self.cc(self.C_EPS)[0:1, :])
                    p.inc(ss.activation(out=rnrow[:], in_=rnrow[:], func=AF.Exp,
                                        scale=self.cc(self.C_NH)[0:1, :]), "S")
                p.S(srms)
                def grn(gg, st=st):
                    p.wait(gg, p.now("S"))
                    gg.dma_start(out=self.sp_rows[8:9, 0:STP], in_=rnrow[:]
                                 ).then_inc(p.sem["SP"], 16)
                    p.cnt["SP"] += 16
                    gg.wait_ge(p.sem["SP"], p.cnt["SP"])
                    p.inc(gg.dma_start(out=rnb[:], in_=self.sp_rows[
                        8:9, 0:STP].broadcast_to([128, STP])), "SP")
                p.G(grn)
                ev_rnb = p.now("SP")
                dr_ = self.mkdbg(f"rnb_{d}", [128, 8 * STP])
                if dr_ is not None:
                    def gdr(gg, st=st, dr_=dr_):
                        p.wait(gg, ev_rnb)
                        p.inc(gg.dma_start(
                            out=dr_[:, st * STP:(st + 1) * STP],
                            in_=rnb[:]), "SP")
                    p.G(gdr)
                    p.barrier()

                if p3s <= 5:
                    continue
                for ont in range(8):
                    oev = [None]
                    def to(tt, ont=ont, st=st, oev=oev):
                        if ont == 0:
                            p.wait(tt, ev_wo, p.now("V"))
                        if ont >= 2:
                            p.wait(tt, p.now("SP"))
                        bk = (ont % 2) * 512
                        for kt in range(16):
                            tt.matmul(ps_o[:, bk:bk + STP],
                                      woutb[:, kt * 1024 + ont * 128:
                                            kt * 1024 + ont * 128 + 128],
                                      gbf[:, kt * STP:(kt + 1) * STP],
                                      start=(kt == 0), stop=(kt == 15))
                        oev[0] = p.incl(tt, 'T')
                    p.T(to)
                    sev = [None]
                    if os.environ.get("KB_P3_STOUCH"):
                        def sto(ss, ont=ont, oev=oev, sev=sev):
                            p.wait(ss, oev[0])
                            bk = (ont % 2) * 512
                            sev[0] = p.inc(ss.copy(rnrow[0:1, 0:STP],
                                                   ps_o[0:1, bk:bk + STP]), "S")
                        p.S(sto)
                    def vo(vv, ont=ont, oev=oev, sev=sev):
                        p.wait(vv, oev[0], ev_rnb, prev["goev"])
                        if sev[0] is not None:
                            p.wait(vv, sev[0])
                        bk = (ont % 2) * 512
                        p.inc(vv.tensor_tensor(out=oacc[:], in0=ps_o[:, bk:bk + STP],
                                               in1=rnb[:], op=ALU.mult), "V")
                    p.V(vo)
                    def go(gg, ont=ont, st=st):
                        p.wait(gg, p.now("V"))
                        if d == "f":
                            p.inc(gg.dma_start(
                                out=self.acc_m[ont * 128:(ont + 1) * 128, scol],
                                in_=oacc[:]), "SP")
                        else:
                            p.inc(gg.dma_start(
                                out=self.acc_m[ont * 128:(ont + 1) * 128, scol],
                                in_=oacc[:], accum_op=ALU.add), "SP")
                        prev["goev"] = p.now("SP")
                    p.G(go)
                if os.environ.get("KB_P3_OPBAR"):
                    p.barrier()
            dwb = self.mkdbg(f"woutb_{d}", [128, 16 * 1024])
            if dwb is not None:
                def gdwb(gg, dwb=dwb):
                    p.wait(gg, p.now("V"), p.now("T"))
                    p.inc(gg.dma_start(out=dwb[:, :], in_=woutb[:]), "SP")
                p.G(gdwb)
            p.barrier()

    def ffn_qkv(self, lidx, ph, QH, KH, VT, src_dram, src_dt, n_mc, kw0, tc0):
        p, nc = self, self.nc
        qsl = self.sbt(ph, "qsl", [128, 1], F32)
        ksl = self.sbt(ph, "ksl", [128, 1], F32)
        wqb = self.sbt(ph, "wqb", [128, 8 * 1536], BF16)
        wstq = [self.sbt(ph, f"wstq{i}", [128, 1536], F32) for i in range(2)]
        sbf = self.sbt(ph, "sbf", [128, 8 * 513], BF16)
        sstg = [self.sbt(ph, f"sstg{i}", [128, 513], F32) for i in range(2)]
        qkst = self.sbt(ph, "qkst", [128, 8 * 512], F32)
        vstg = self.sbt(ph, "vstg", [128, 4 * 512], BF16)
        sqb = self.sbt(ph, "sqb", [128, 512], BF16)
        nrw = self.sbt(ph, "nrw", [2, 2048], F32)
        invb = self.sbt(ph, "invb", [128, 512], F32)
        tq = self.sbt(ph, "tq", [128, 512], F32)
        tperm = self.sbt(ph, "tperm", [128, 512], F32)
        r1 = self.sbt(ph, "r1", [128, 512], F32)
        rtb = self.sbt(ph, "rtb", [128, 4 * 512], F32)
        ps_qk = self.pst(ph, "ps_qk", [128, 1024])
        ps_n = self.pst(ph, "ps_n", [2, 2048])
        ps_vt = self.pst(ph, "ps_vt", [128, 2048], BF16)

        def ysc(yq):
            yq.dma_start(out=qsl[:], in_=self.qsc[lidx, :, :]).then_inc(
                p.sem["LD"], 16)
            yq.dma_start(out=ksl[:], in_=self.ksc[lidx, :, :]).then_inc(
                p.sem["LD"], 16)
            p.cnt["LD"] += 32
        p.Y(ysc)
        ev_sc = p.now("LD")

        cast_ev = [None] * 8
        for kt in range(8):
            ldev = [None]
            def y(yq, kt=kt, ldev=ldev):
                if kt >= 2:
                    p.wait(yq, cast_ev[kt - 2])
                ldev[0] = p.inc(yq.dma_start(
                    out=wstq[kt % 2][:],
                    in_=self.wqkv[lidx, kt * 128:(kt + 1) * 128, :]), "LD")
            p.Y(y)
            def v(vv, kt=kt, ldev=ldev):
                p.wait(vv, ldev[0])
                cast_ev[kt] = p.inc(vv.tensor_copy(
                    wqb[:, kt * 1536:(kt + 1) * 1536], wstq[kt % 2][:]), "V")
            p.V(v)
        ev_wq = cast_ev[7]

        qkv_sec = int(os.environ.get("KB_QKV_SEC", "99"))
        prevm = {"stg": ("V", 0), "use": [("T", 0)] * 8}
        for m in range(n_mc):
            if qkv_sec == 0:
                break
            kw = kw0 + m
            stev = [None] * 8
            for rt in range(8):
                ldev = [None]
                def y(yq, m=m, rt=rt, ldev=ldev):
                    p.wait(yq, prevm["use"][rt])
                    if src_dram.dtype == F32 and rt >= 2 and stev[rt - 2]:
                        p.wait(yq, stev[rt - 2])
                    c0 = src_dt(m)
                    ldev[0] = p.inc(yq.dma_start(
                        out=sstg[rt % 2][:] if src_dram.dtype == F32 else
                        sbf[:, rt * 513:(rt + 1) * 513],
                        in_=src_dram[rt * 128:(rt + 1) * 128, c0:c0 + 513]), "LD")
                p.Y(y)
                if src_dram.dtype == F32:
                    def v(vv, rt=rt, ldev=ldev):
                        p.wait(vv, ldev[0])
                        stev[rt] = p.inc(vv.tensor_copy(
                            sbf[:, rt * 513:(rt + 1) * 513], sstg[rt % 2][:]), "V")
                    p.V(v)
                else:
                    stev[rt] = ldev[0]
            rtev = [None]
            def yr(yq, m=m, rtev=rtev):
                p.wait(yq, prevm["stg"])
                t0 = tc0(m)
                for i in range(4):
                    yq.dma_start(out=rtb[:, i * 512:(i + 1) * 512],
                                 in_=self.rot[i, :, t0:t0 + 512]).then_inc(
                        p.sem["LD"], 16)
                p.cnt["LD"] += 64
                rtev[0] = p.now("LD")
            p.Y(yr)

            if qkv_sec <= 1:
                prevm["stg"] = p.now("V")
                continue
            ntev = {}
            for nt in range(12):
                def t(tt, m=m, nt=nt):
                    if nt < 2:
                        p.wait(tt, ev_wq, *stev)
                        p.wait(tt, p.now("V"), p.now("S"))
                    else:
                        p.wait(tt, ntev[nt - 2][1])
                    bk = (nt % 2) * 512
                    for kt in range(8):
                        off = 1 if kt < 4 else 0
                        tt.matmul(ps_qk[:, bk:bk + 512],
                                  wqb[:, kt * 1536 + nt * 128:
                                      kt * 1536 + nt * 128 + 128],
                                  sbf[:, rt_off(kt, off)],
                                  start=(kt == 0), stop=(kt == 7))
                    ntev[nt] = [p.incl(tt, 'T'), None]
                def rt_off(kt, off):
                    return slice(kt * 513 + off, kt * 513 + off + 512)
                p.T(t)
                def s_(ss, m=m, nt=nt):
                    p.wait(ss, ntev[nt][0])
                    bk = (nt % 2) * 512
                    if nt < 8:
                        e = p.inc(ss.copy(qkst[:, nt * 512:(nt + 1) * 512],
                                          ps_qk[:, bk:bk + 512]), "S")
                    else:
                        e = p.inc(ss.activation(
                            out=vstg[:, (nt - 8) * 512:(nt - 7) * 512],
                            in_=ps_qk[:, bk:bk + 512], func=AF.Copy), "S")
                    ntev[nt][1] = e
                p.S(s_)
            ev_qkv = p.now("S")

            if qkv_sec <= 2:
                prevm["stg"] = p.now("V")
                prevm["use"] = [ntev[11][0]] * 8
                continue
            vt_evs = []
            vt_n = int(os.environ.get("KB_VT_N", "16"))
            for vt4 in range(4):
                for blk in range(4):
                    if vt4 * 4 + blk >= vt_n:
                        continue
                    tev = [None]
                    pb = (len(vt_evs) % 2) * 1024
                    def t(tt, m=m, vt4=vt4, blk=blk, tev=tev, pb=pb):
                        p.wait(tt, ntev[8 + vt4][1])
                        if len(vt_evs) >= 2:
                            p.wait(tt, vt_evs[-2])
                        tev[0] = p.inc(tt.transpose(
                            ps_vt[:, pb:pb + 128],
                            vstg[:, vt4 * 512 + blk * 128:
                                 vt4 * 512 + blk * 128 + 128],
                            self.ident_bf[:]), "T")
                    p.T(t)
                    def v(vv, m=m, vt4=vt4, blk=blk, tev=tev, pb=pb):
                        p.wait(vv, tev[0])
                        ktg = (kw0 + m) * 4 + blk
                        vt_evs.append(p.inc(vv.tensor_copy(
                            VT[:, ktg * 512 + vt4 * 128:ktg * 512 + vt4 * 128 + 128],
                            ps_vt[:, pb:pb + 128]), "V"))
                    p.V(v)

            if qkv_sec <= 3:
                prevm["stg"] = p.now("V")
                prevm["use"] = [ntev[11][0]] * 8
                continue
            use_evs = [None] * 8
            for iq, isq in ((0, False), (1, True)):
                if iq == 1 and kw == 0:
                    continue
                for t4 in range(4):
                    base = (4 if iq == 0 else 0) + t4
                    ti = (0 + t4) if iq == 1 else (4 + t4)
                    sq_ev = [None]
                    def s_(ss, m=m, ti=ti, sq_ev=sq_ev):
                        p.wait(ss, ev_qkv, p.now("V"))
                        sq_ev[0] = p.inc(ss.activation(
                            out=sqb[:], in_=qkst[:, ti * 512:(ti + 1) * 512],
                            func=AF.Square), "S")
                    p.S(s_)
                    nev = [None]
                    def t(tt, m=m, ti=ti, t4=t4, sq_ev=sq_ev, nev=nev):
                        p.wait(tt, sq_ev[0])
                        nev[0] = p.inc(tt.matmul(
                            ps_n[:, t4 * 512:(t4 + 1) * 512], self.blk2[:],
                            sqb[:], start=True, stop=True), "T")
                    p.T(t)
                    sinv = [None]
                    def s2(ss, m=m, t4=t4, nev=nev, sinv=sinv):
                        p.wait(ss, nev[0])
                        sl = slice(t4 * 512, (t4 + 1) * 512)
                        ss.activation(out=nrw[:, sl], in_=ps_n[:, sl], func=AF.Ln,
                                      bias=self.cc(self.C_TINY)[0:2, :])
                        sinv[0] = p.inc(ss.activation(
                            out=nrw[:, sl], in_=nrw[:, sl], func=AF.Exp,
                            scale=self.cc(self.C_NH)[0:2, :]), "S")
                    p.S(s2)
                    vcl = [None]
                    def v0(vv, t4=t4, sinv=sinv, vcl=vcl):
                        p.wait(vv, sinv[0])
                        sl = slice(t4 * 512, (t4 + 1) * 512)
                        vcl[0] = p.inc(vv.tensor_scalar_min(
                            nrw[:, sl], nrw[:, sl], 1e12), "V")
                    p.V(v0)
                    bev = [None]
                    def g(gg, m=m, t4=t4, vcl=vcl, bev=bev):
                        p.wait(gg, vcl[0])
                        gg.dma_start(out=self.sp_rows[10:12, 0:512],
                                     in_=nrw[:, t4 * 512:(t4 + 1) * 512]
                                     ).then_inc(p.sem["SP"], 16)
                        p.cnt["SP"] += 16
                        gg.wait_ge(p.sem["SP"], p.cnt["SP"])
                        src = self.sp_rows[10:12, 0:512].rearrange(
                            "a (b w) -> a b w", b=1).broadcast_to([2, 64, 512])
                        bev[0] = p.inc(gg.dma_start(out=invb[:], in_=src), "SP")
                    p.G(g)
                    def v1(vv, m=m, iq=iq, ti=ti, t4=t4, bev=bev):
                        p.wait(vv, bev[0], rtev[0])
                        qk = qkst[:, ti * 512:(ti + 1) * 512]
                        scv = qsl if iq == 1 else ksl
                        vv.scalar_tensor_tensor(out=tq[:], in0=qk, scalar=1.0,
                                                in1=invb[:], op0=ALU.mult,
                                                op1=ALU.mult)
                        vv.tensor_scalar(out=tq[:], in0=tq[:], scalar1=scv[:, 0:1],
                                         scalar2=None, op0=ALU.mult)
                        A = rtb[:, 0:512] if iq == 1 else rtb[:, 1024:1536]
                        Bs = rtb[:, 512:1024] if iq == 1 else rtb[:, 1536:2048]
                        vv.tensor_tensor(out=r1[:], in0=tq[:], in1=A, op=ALU.mult)
                        for half in range(2):
                            b0 = half * 64
                            vv.tensor_copy(tperm[b0:b0 + 32, :], tq[b0 + 32:b0 + 64, :])
                            vv.tensor_copy(tperm[b0 + 32:b0 + 64, :], tq[b0:b0 + 32, :])
                        vv.tensor_tensor(out=tperm[:], in0=tperm[:], in1=Bs,
                                         op=ALU.mult)
                        dst = QH if iq == 1 else KH
                        dw = 2048 if iq == 1 else 2560
                        dcol = (t4 * (dw) + (m - (1 - kw0)) * 512) if iq == 1 \
                            else (t4 * dw + (kw0 + m) * 512)
                        use_evs[ti] = p.inc(vv.tensor_tensor(
                            out=dst[:, dcol:dcol + 512], in0=r1[:], in1=tperm[:],
                            op=ALU.add), "V")
                    p.V(v1)
            prevm["stg"] = p.now("V")
            prevm["use"] = [ntev[11][0]] * 8
        self.ev_qkv_done = [p.now("V"), p.now("S"), p.now("T")]

    def ffn_attn(self, lidx, ph, wins, QH, KH, VT, stream, wo_bf, AO,
                 ebuf, recipb, first_kv_mask):
        p, nc = self, self.nc
        ph = self.es_attn = contextlib.ExitStack()
        self._attn_stack = ph
        ps_sc = self.pst(ph, "ps_sc", [128, 2 * 512])
        ps_s0 = self.pst(ph, "ps_s0", [1, 512])
        ps_s1 = self.pst(ph, "ps_s1", [1, 512])
        ps_av = self.pst(ph, "ps_av", [128, 1024])
        srow0 = self.sbt(ph, "srow0", [1, 512], F32)
        srow1 = self.sbt(ph, "srow1", [1, 512], F32)

        for w in wins:
            for hp in range(4):
                mev = {}
                for blk in range(2):
                    kw = w + blk
                    for kt in range(4):
                        bi = blk * 4 + kt
                        scev = [None]
                        def t(tt, w=w, hp=hp, blk=blk, kt=kt, bi=bi, scev=scev):
                            if bi < 1:
                                p.wait(tt, *self.ev_qkv_done)
                                p.wait(tt, p.now("G"), p.now("V"))
                            if bi >= 1:
                                p.wait(tt, mev.get(bi - 1, (0, 0)))
                            kcol = hp * 2560 + (w + blk) * 512 + kt * 128
                            for hh in range(2):
                                tt.matmul(
                                    ps_sc[:, hh * 512:(hh + 1) * 512],
                                    KH[hh * 64:(hh + 1) * 64, kcol:kcol + 128],
                                    QH[hh * 64:(hh + 1) * 64,
                                       hp * 2048 + w * 512:hp * 2048 + w * 512 + 512],
                                    start=True, stop=True,
                                    tile_position=(hh * 64, 0))
                            scev[0] = p.incl(tt, 'T')
                        p.T(t)
                        eev = [None]
                        def s_(ss, hp=hp, blk=blk, kt=kt, bi=bi, scev=scev, eev=eev):
                            p.wait(ss, scev[0])
                            for hh in range(2):
                                ss.activation(
                                    out=ebuf[:, (hh * 8 + bi) * 512:
                                             (hh * 8 + bi) * 512 + 512],
                                    in_=ps_sc[:, hh * 512:(hh + 1) * 512],
                                    func=AF.Exp, bias=self.cc(self.C_M8))
                            eev[0] = p.incl(ss, 'S')
                        p.S(s_)
                        def g(gg, w=w, blk=blk, kt=kt, bi=bi, eev=eev):
                            p.wait(gg, eev[0])
                            for hh in range(2):
                                esl = ebuf[:, (hh * 8 + bi) * 512:
                                           (hh * 8 + bi) * 512 + 512]
                                if blk == 1:
                                    gg.affine_select(
                                        out=esl, in_=esl, pattern=[[1, 512]],
                                        compare_op=ALU.is_ge, fill=0.0,
                                        base=-128 * kt, channel_multiplier=-1)
                                else:
                                    gg.affine_select(
                                        out=esl, in_=esl, pattern=[[-1, 512]],
                                        compare_op=ALU.is_ge, fill=0.0,
                                        base=128 * kt, channel_multiplier=1)
                            mev[bi] = p.incl(gg, 'G')
                        p.G(g)
                if w == 0:
                    def vkm(vv, hp=hp):
                        p.wait(vv, mev[3])
                        for hh in range(2):
                            for kt in range(4):
                                bi = kt
                                esl = ebuf[:, (hh * 8 + bi) * 512:
                                           (hh * 8 + bi) * 512 + 512]
                                vv.tensor_scalar_mul(esl, esl, first_kv_mask)
                        p.incl(vv, 'V')
                    p.V(vkm)
                    ev_km = p.now("V")
                else:
                    ev_km = (0, 0)

                smev = [None]
                def t2(tt, hp=hp, smev=smev):
                    p.wait(tt, mev[7], ev_km, p.now("S"))
                    for hh, psn in ((0, ps_s0), (1, ps_s1)):
                        for bi in range(8):
                            tt.matmul(psn[:], self.ones_bf[:],
                                      ebuf[:, (hh * 8 + bi) * 512:
                                           (hh * 8 + bi) * 512 + 512],
                                      start=(bi == 0), stop=(bi == 7))
                    smev[0] = p.incl(tt, 'T')
                p.T(t2)
                srev = [None]
                def s2(ss, smev=smev, srev=srev):
                    p.wait(ss, smev[0])
                    ss.copy(srow0[0:1, :], ps_s0[:])
                    srev[0] = p.inc(ss.copy(srow1[0:1, :], ps_s1[:]), "S")
                p.S(s2)
                vrev = [None]
                def v2(vv, srev=srev, vrev=vrev):
                    p.wait(vv, srev[0])
                    vv.reciprocal(srow0[:], srow0[:])
                    vrev[0] = p.inc(vv.reciprocal(srow1[:], srow1[:]), "V")
                p.V(v2)
                bev = [None]
                def g2(gg, vrev=vrev, bev=bev):
                    p.wait(gg, vrev[0], p.now("V"))
                    gg.dma_start(out=self.sp_rows[12:13, 0:512], in_=srow0[:]
                                 ).then_inc(p.sem["SP"], 16)
                    gg.dma_start(out=self.sp_rows[13:14, 0:512], in_=srow1[:]
                                 ).then_inc(p.sem["SP"], 16)
                    p.cnt["SP"] += 32
                    gg.wait_ge(p.sem["SP"], p.cnt["SP"])
                    src = self.sp_rows[12:14, 0:512].rearrange(
                        "a (b w) -> a b w", b=1).broadcast_to([2, 64, 512])
                    bev[0] = p.inc(gg.dma_start(out=recipb[:], in_=src), "SP")
                p.G(g2)

                avev = [None]
                def t3(tt, w=w, hp=hp, avev=avev):
                    p.wait(tt, mev[7], ev_km, p.now("V"))
                    for bi in range(8):
                        blk, kt = bi // 4, bi % 4
                        ktg = (w + blk) * 4 + kt
                        for hh in range(2):
                            tt.matmul(
                                ps_av[hh * 64:(hh + 1) * 64,
                                      hh * 512:hh * 512 + 512],
                                VT[:, ktg * 512 + hp * 128 + hh * 64:
                                   ktg * 512 + hp * 128 + hh * 64 + 64],
                                ebuf[:, (hh * 8 + bi) * 512:
                                     (hh * 8 + bi) * 512 + 512],
                                start=(bi == 0), stop=(bi == 7),
                                tile_position=(0, hh * 64))
                    avev[0] = p.incl(tt, 'T')
                p.T(t3)
                def v3(vv, w=w, hp=hp, avev=avev, bev=bev):
                    p.wait(vv, avev[0], bev[0])
                    vv.scalar_tensor_tensor(
                        out=AO[0:64, hp * 2048 + w * 512:hp * 2048 + w * 512 + 512],
                        in0=ps_av[0:64, 0:512], scalar=1.0, in1=recipb[0:64, :],
                        op0=ALU.mult, op1=ALU.mult)
                    p.inc(vv.scalar_tensor_tensor(
                        out=AO[64:128, hp * 2048 + w * 512:hp * 2048 + w * 512 + 512],
                        in0=ps_av[64:128, 512:1024], scalar=1.0,
                        in1=recipb[64:128, :],
                        op0=ALU.mult, op1=ALU.mult), "V")
                p.V(v3)
            for ont in range(8):
                oev = [None]
                def t4(tt, w=w, ont=ont, oev=oev):
                    p.wait(tt, p.now("V"))
                    if ont >= 2:
                        p.wait(tt, p.now("V"))
                    bk = (ont % 2) * 512
                    for k4 in range(4):
                        tt.matmul(ps_sc[:, bk:bk + 512],
                                  wo_bf[:, k4 * 1024 + ont * 128:
                                        k4 * 1024 + ont * 128 + 128],
                                  AO[:, k4 * 2048 + w * 512:k4 * 2048 + w * 512 + 512],
                                  start=(k4 == 0), stop=(k4 == 3))
                    oev[0] = p.incl(tt, 'T')
                p.T(t4)
                def v4(vv, w=w, ont=ont, oev=oev):
                    p.wait(vv, oev[0])
                    sl = slice(ont * SEG + 1 + w * 512, ont * SEG + 1 + w * 512 + 512)
                    p.inc(vv.scalar_tensor_tensor(
                        out=stream[:, sl], in0=ps_sc[:, (ont % 2) * 512:
                                                     (ont % 2) * 512 + 512],
                        scalar=1.0, in1=stream[:, sl],
                        op0=ALU.mult, op1=ALU.add), "V")
                p.V(v4)
        self.barrier()
        ph.close()

    def mlp_ln(self, lidx, ph, stream_dram, LNIN, mcs):
        p = self
        stf = self.sbt(ph, "stf", [128, 8 * 513], F32)
        sbfm = self.sbt(ph, "sbfm", [128, 512], BF16)
        sqm = self.sbt(ph, "sqm", [128, 512], BF16)
        rows0 = self.sbt(ph, "rows_m0", [1, 512], F32)
        rows1 = self.sbt(ph, "rows_m1", [1, 512], F32)
        rows2 = self.sbt(ph, "rows_m2", [1, 512], F32)
        mub = self.sbt(ph, "mub_m", [128, 512], F32)
        rb = self.sbt(ph, "rb_m", [128, 512], F32)
        tq = self.sbt(ph, "tq_m", [128, 512], F32)
        ps_st = self.pst(ph, "ps_stm", [1, 512])
        ps_st2 = self.pst(ph, "ps_stm2", [1, 512])

        for mc in mcs:
            ldev = [None]
            def y(yq, mc=mc, ldev=ldev):
                p.wait(yq, p.now("T"), p.now("V"))
                for rt in range(8):
                    yq.dma_start(out=stf[:, rt * 513:(rt + 1) * 513],
                                 in_=stream_dram[rt * 128:(rt + 1) * 128,
                                                 mc * 512:mc * 512 + 513]
                                 ).then_inc(p.sem["LD"], 16)
                p.cnt["LD"] += 128
                ldev[0] = p.now("LD")
            p.Y(y)
            stev = [None]
            for pi, (dstrow, which) in enumerate(((0, "v"), (1, "s"))):
                for rt in range(8):
                    off = 1 if rt < 4 else 0
                    src = stf[:, rt * 513 + off: rt * 513 + off + 512]
                    cev = [None]
                    if which == "v":
                        def v(vv, src=src, ldev=ldev, cev=cev):
                            p.wait(vv, ldev[0], p.now("T"))
                            cev[0] = p.inc(vv.tensor_copy(sbfm[:], src), "V")
                        p.V(v)
                        buf = sbfm
                    else:
                        def s_(ss, src=src, ldev=ldev, cev=cev):
                            p.wait(ss, ldev[0], p.now("T"))
                            cev[0] = p.inc(ss.activation(out=sqm[:], in_=src,
                                                         func=AF.Square), "S")
                        p.S(s_)
                        buf = sqm
                    def t(tt, rt=rt, dstrow=dstrow, buf=buf, cev=cev):
                        p.wait(tt, cev[0])
                        pst = ps_st if dstrow == 0 else ps_st2
                        p.inc(tt.matmul(pst[0:1, :],
                                        self.ones_bf[:], buf[:],
                                        start=(rt == 0), stop=(rt == 7)), "T")
                    p.T(t)
            def sev(ss):
                p.wait(ss, p.now("T"))
                ss.copy(rows0[0:1, :], ps_st[0:1, :])
                p.inc(ss.copy(rows1[0:1, :], ps_st2[0:1, :]), "S")
            p.S(sev)
            def vr(vv):
                p.wait(vv, p.now("S"))
                vv.tensor_scalar_mul(rows0[0:1, :], rows0[0:1, :], 1.0 / 1024.0)
                vv.tensor_scalar_mul(rows1[0:1, :], rows1[0:1, :], 1.0 / 1024.0)
                vv.tensor_tensor(out=rows2[0:1, :], in0=rows0[0:1, :],
                                 in1=rows0[0:1, :], op=ALU.mult)
                p.inc(vv.tensor_tensor(out=rows1[0:1, :], in0=rows1[0:1, :],
                                       in1=rows2[0:1, :], op=ALU.subtract), "V")
            p.V(vr)
            def sr(ss):
                p.wait(ss, p.now("V"))
                ss.activation(out=rows1[0:1, :], in_=rows1[0:1, :], func=AF.Ln,
                              bias=self.cc(self.C_EPS)[0:1, :])
                p.inc(ss.activation(out=rows1[0:1, :], in_=rows1[0:1, :],
                                    func=AF.Exp, scale=self.cc(self.C_NH)[0:1, :]), "S")
            p.S(sr)
            bev = [None]
            def g(gg, bev=bev):
                p.wait(gg, p.now("S"), p.now("V"))
                gg.dma_start(out=self.sp_rows[14:15, 0:512], in_=rows0[0:1, :]
                             ).then_inc(p.sem["SP"], 16)
                gg.dma_start(out=self.sp_rows[15:16, 0:512], in_=rows1[0:1, :]
                             ).then_inc(p.sem["SP"], 16)
                p.cnt["SP"] += 32
                gg.wait_ge(p.sem["SP"], p.cnt["SP"])
                gg.dma_start(out=mub[:], in_=self.sp_rows[
                    14:15, 0:512].broadcast_to([128, 512])
                ).then_inc(p.sem["SP"], 16)
                gg.dma_start(out=rb[:], in_=self.sp_rows[
                    15:16, 0:512].broadcast_to([128, 512])
                ).then_inc(p.sem["SP"], 16)
                p.cnt["SP"] += 32
                bev[0] = p.now("SP")
            p.G(g)
            def va(vv, mc=mc, ldev=ldev, bev=bev):
                p.wait(vv, bev[0])
                for rt in range(8):
                    off = 1 if rt < 4 else 0
                    src = stf[:, rt * 513 + off: rt * 513 + off + 512]
                    vv.tensor_tensor(out=tq[:], in0=src, in1=mub[:],
                                     op=ALU.subtract)
                    vv.tensor_tensor(out=tq[:], in0=tq[:], in1=rb[:], op=ALU.mult)
                    vv.tensor_scalar(
                        out=LNIN[:, rt * 2048 + mc * 512:rt * 2048 + mc * 512 + 512],
                        in0=tq[:], scalar1=self.lng_sb[:, rt:rt + 1],
                        scalar2=self.lnb2_sb[:, rt:rt + 1], op0=ALU.mult,
                        op1=ALU.add)
                p.incl(vv, 'V')
            p.V(va)

    def mlp_a(self, lidx, ph, LNIN, gated, mcs, bufs):
        p = self
        w1s, w1b, ggf, tmpa, ps_a, ps_g, b1sb = bufs
        prevk = {"ev": ("V", 0)}
        for k in range(22):
            ldev = [None]
            def y(yq, k=k, ldev=ldev):
                p.wait(yq, prevk["ev"])
                for rt in range(8):
                    yq.dma_start(
                        out=w1s[:, rt * 256:rt * 256 + 128],
                        in_=self.w1p[lidx, rt * 128:(rt + 1) * 128,
                                     k * 128:(k + 1) * 128]).then_inc(p.sem["LD"], 16)
                    yq.dma_start(
                        out=w1s[:, rt * 256 + 128:rt * 256 + 256],
                        in_=self.w1p[lidx, rt * 128:(rt + 1) * 128,
                                     HIDP + k * 128:HIDP + (k + 1) * 128]
                    ).then_inc(p.sem["LD"], 16)
                p.cnt["LD"] += 256
                ldev[0] = p.now("LD")
            p.Y(y)
            cev = [None]
            def v(vv, ldev=ldev, cev=cev):
                p.wait(vv, ldev[0])
                cev[0] = p.inc(vv.tensor_copy(w1b[:], w1s[:]), "V")
            p.V(v)
            mmev = [None]
            def t(tt, k=k, mcs=mcs, cev=cev, mmev=mmev):
                p.wait(tt, cev[0], p.now("V"))
                for mi, mc in enumerate(mcs):
                    for ag in range(2):
                        ps = ps_a if ag == 0 else ps_g
                        for kt in range(8):
                            tt.matmul(
                                ps[:, mi * 512:(mi + 1) * 512],
                                w1b[:, kt * 256 + ag * 128:kt * 256 + ag * 128 + 128],
                                LNIN[:, kt * 2048 + mc * 512:kt * 2048 + mc * 512 + 512],
                                start=(kt == 0), stop=(kt == 7))
                mmev[0] = p.incl(tt, 'T')
            p.T(t)
            gev = [None]
            def s_(ss, k=k, mcs=mcs, mmev=mmev, gev=gev):
                p.wait(ss, mmev[0])
                for mi, mc in enumerate(mcs):
                    ss.activation(out=ggf[:, mi * 512:(mi + 1) * 512],
                                  in_=ps_g[:, mi * 512:(mi + 1) * 512],
                                  func=AF.Gelu, bias=b1sb[:, 22 + k:23 + k])
                gev[0] = p.incl(ss, 'S')
            p.S(s_)
            def v2(vv, k=k, mcs=mcs, mmev=mmev, gev=gev):
                p.wait(vv, gev[0])
                for mi, mc in enumerate(mcs):
                    vv.tensor_scalar(out=tmpa[:], in0=ps_a[:, mi * 512:(mi + 1) * 512],
                                     scalar1=b1sb[:, k:k + 1], scalar2=None,
                                     op0=ALU.add)
                    vv.tensor_tensor(
                        out=gated[:, k * 2048 + mc * 512:k * 2048 + mc * 512 + 512],
                        in0=tmpa[:], in1=ggf[:, mi * 512:(mi + 1) * 512],
                        op=ALU.mult)
                prevk["ev"] = p.incl(vv, 'V')
            p.V(v2)

    def mlp_b(self, lidx, ph, gated, mode):
        p = self
        w2s = [self.sbt(ph, f"w2s{i}", [128, 1024], F32) for i in range(2)]
        w2b = self.sbt(ph, "w2b", [128, 22 * 1024], BF16)
        t1 = self.sbt(ph, "t1_m", [128, 512], F32)
        t2 = self.sbt(ph, "t2_m", [128, 512], F32)
        rl1 = [self.sbt(ph, f"rl1_{i}", [128, 512], F32) for i in range(2)]
        rl2 = [self.sbt(ph, f"rl2_{i}", [128, 512], F32) for i in range(2)]
        rl3 = [self.sbt(ph, f"rl3_{i}", [128, 512], F32) for i in range(2)]
        rl4 = [self.sbt(ph, f"rl4_{i}", [128, 512], F32) for i in range(2)]
        ps_w2 = self.pst(ph, "ps_w2", [128, 1024])

        cast_ev = [None] * 22
        for kt in range(22):
            ldev = [None]
            def y(yq, kt=kt, ldev=ldev):
                if kt >= 2:
                    p.wait(yq, cast_ev[kt - 2])
                ldev[0] = p.inc(yq.dma_start(
                    out=w2s[kt % 2][:],
                    in_=self.w2p[lidx, kt * 128:(kt + 1) * 128, :]), "LD")
            p.Y(y)
            def v(vv, kt=kt, ldev=ldev):
                p.wait(vv, ldev[0])
                cast_ev[kt] = p.inc(vv.tensor_copy(
                    w2b[:, kt * 1024:(kt + 1) * 1024], w2s[kt % 2][:]), "V")
            p.V(v)
        ev_w2 = cast_ev[21]

        prevev = [("SP", 0), ("SP", 0)]
        for ont in range(8):
            for mc in range(4):
                i = (ont * 4 + mc) % 2
                ldev = [None]
                def y(yq, ont=ont, mc=mc, i=i, ldev=ldev):
                    p.wait(yq, prevev[i])
                    rsl = slice(ont * 128, (ont + 1) * 128)
                    csl = slice(1 + mc * 512, 1 + mc * 512 + 512)
                    yq.dma_start(out=rl1[i][:], in_=self.sp_s1[rsl, csl]
                                 ).then_inc(p.sem["LD"], 16)
                    p.cnt["LD"] += 16
                    if mode == "l2":
                        yq.dma_start(out=rl2[i][:],
                                     in_=self.x_fm[rsl, OWN0 + mc * 512:
                                                   OWN0 + mc * 512 + 512]
                                     ).then_inc(p.sem["LD"], 16)
                        yq.dma_start(out=rl3[i][:],
                                     in_=self.sp_xn32[rsl, mc * 512:mc * 512 + 512]
                                     ).then_inc(p.sem["LD"], 16)
                        yq.dma_start(out=rl4[i][:],
                                     in_=self.acc_m[rsl, mc * 512:mc * 512 + 512]
                                     ).then_inc(p.sem["LD"], 16)
                        p.cnt["LD"] += 48
                    ldev[0] = p.now("LD")
                p.Y(y)
                mmev = [None]
                def t(tt, ont=ont, mc=mc, i=i, mmev=mmev):
                    if ont == 0 and mc < 2:
                        p.wait(tt, ev_w2, p.now("V"))
                    p.wait(tt, prevev[i])
                    bk = i * 512
                    for kt in range(22):
                        tt.matmul(ps_w2[:, bk:bk + 512],
                                  w2b[:, kt * 1024 + ont * 128:
                                      kt * 1024 + ont * 128 + 128],
                                  gated[:, kt * 2048 + mc * 512:
                                        kt * 2048 + mc * 512 + 512],
                                  start=(kt == 0), stop=(kt == 21))
                    mmev[0] = p.incl(tt, 'T')
                p.T(t)
                vev = [None]
                def v(vv, ont=ont, mc=mc, i=i, ldev=ldev, mmev=mmev, vev=vev):
                    p.wait(vv, ldev[0], mmev[0])
                    bk = i * 512
                    vv.tensor_scalar(out=t1[:], in0=ps_w2[:, bk:bk + 512],
                                     scalar1=self.b2_sb[:, ont:ont + 1],
                                     scalar2=None, op0=ALU.add)
                    vv.tensor_tensor(out=t1[:], in0=t1[:], in1=rl1[i][:],
                                     op=ALU.add)
                    if mode == "l2":
                        vv.tensor_tensor(out=t2[:], in0=rl2[i][:], in1=rl3[i][:],
                                         op=ALU.add)
                        vv.tensor_tensor(out=t2[:], in0=t2[:], in1=rl4[i][:],
                                         op=ALU.add)
                        vv.tensor_tensor(out=t1[:], in0=t1[:], in1=t2[:],
                                         op=ALU.add)
                    vev[0] = p.inc(vv.tensor_copy(
                        (rl1[i][:] if True else t1[:]), t1[:]), "V")
                p.V(v)
                def g(gg, ont=ont, mc=mc, i=i, vev=vev):
                    p.wait(gg, vev[0])
                    if mode == "l1":
                        e = p.inc(gg.dma_start(
                            out=self.sp_s2[ont * 128:(ont + 1) * 128,
                                           1 + mc * 512:1 + mc * 512 + 512],
                            in_=rl1[i][:]), "SP")
                    else:
                        e = p.inc(gg.dma_start(
                            out=self.y_fm[ont * 128:(ont + 1) * 128,
                                          mc * 512:mc * 512 + 512],
                            in_=rl1[i][:]), "SP")
                    prevev[i] = e
                p.G(g)
        self.ev_mlpb_out = p.now("SP")

    def p2ab(self):
        p = self
        self.barrier()
        self.ph_h2.close()
        ph = contextlib.ExitStack()
        with ph:
            QH = self.sbt(ph, "QH", [128, 4 * 2048], BF16)
            KH = self.sbt(ph, "KH", [128, 4 * 2560], BF16)
            VT = self.sbt(ph, "VT", [128, 20 * 512], BF16)
            ph_q = contextlib.ExitStack()
            cut = os.environ.get("KB_P2AB_CUT", "")
            with ph_q:
                self.ffn_qkv(0, ph_q, QH, KH, VT, self.sp_xnbf,
                             lambda m: KV0 - 1 + m * 512, 5, 0,
                             lambda m: m * 512)
                p.barrier()
            if cut == "qkv":
                return

            stream = self.sbt(ph, "stream1", [128, 8 * SEG], F32)
            AO = self.sbt(ph, "AO", [128, 4 * 2048], BF16)
            ebuf = self.sbt(ph, "ebuf", [128, 16 * 512], BF16)
            recipb = self.sbt(ph, "recipb", [128, 512], F32)
            wo_bf = self.sbt(ph, "wo_bf", [128, 4 * 1024], BF16)
            wos = self.sbt(ph, "wos", [128, 1024], F32)

            for kt in range(4):
                ldev = [None]
                def y(yq, kt=kt, ldev=ldev):
                    p.wait(yq, p.now("V"))
                    ldev[0] = p.inc(yq.dma_start(
                        out=wos[:], in_=self.wo2[0, kt * 128:(kt + 1) * 128, :]), "LD")
                p.Y(y)
                def v(vv, kt=kt, ldev=ldev):
                    p.wait(vv, ldev[0])
                    p.inc(vv.tensor_copy(wo_bf[:, kt * 1024:(kt + 1) * 1024],
                                         wos[:]), "V")
                p.V(v)
            def yst(yq):
                for rt in range(8):
                    yq.dma_start(out=stream[:, rt * SEG + 1:rt * SEG + 1 + TOK],
                                 in_=self.sp_xn32[rt * 128:(rt + 1) * 128, :]
                                 ).then_inc(p.sem["LD"], 16)
                p.cnt["LD"] += 128
            p.Y(yst)
            def vst(vv):
                p.wait(vv, p.now("LD"))
                p.incl(vv, 'V')
            p.V(vst)

            self.ffn_attn(0, ph, [0, 1, 2, 3], QH, KH, VT, stream, wo_bf, AO,
                          ebuf, recipb, self.mf)
            if cut == "attn":
                p.barrier()
                return
            self.x1_issue(stream)
            def gsp(gg):
                p.wait(gg, p.now("V"))
                for rt in range(8):
                    p.inc(gg.dma_start(
                        out=self.sp_s1[rt * 128:(rt + 1) * 128, 1:1 + TOK],
                        in_=stream[:, rt * SEG + 1:rt * SEG + 1 + TOK]), "SP")
            p.G(gsp)
            self.dump("stream1", stream, 8, TOK, cols0=1, src_width=SEG)
            p.barrier()

    def p2c(self):
        p = self
        ph = contextlib.ExitStack()
        with ph:
            self.b1_sb = self.sbt(ph, "b1_sb", [128, 44], F32)
            self.b2_sb = self.sbt(ph, "b2_sb", [128, 8], F32)
            self.lng_sb = self.sbt(ph, "lng_sb", [128, 8], F32)
            self.lnb2_sb = self.sbt(ph, "lnb2_sb", [128, 8], F32)
            def y(yq):
                yq.dma_start(out=self.b1_sb[:], in_=self.b1p[0, :, :]).then_inc(
                    p.sem["LD"], 16)
                yq.dma_start(out=self.b2_sb[:], in_=self.b2v[0, :, :]).then_inc(
                    p.sem["LD"], 16)
                yq.dma_start(out=self.lng_sb[:], in_=self.lng[0, :, :]).then_inc(
                    p.sem["LD"], 16)
                yq.dma_start(out=self.lnb2_sb[:], in_=self.lnb2[0, :, :]).then_inc(
                    p.sem["LD"], 16)
                p.cnt["LD"] += 64
            p.Y(y)
            gated = self.sbt(ph, "gated", [128, 22 * 2048], BF16)
            ph_a = contextlib.ExitStack()
            with ph_a:
                LNIN = self.sbt(ph_a, "LNIN", [128, 8 * 2048], BF16)
                ph_ln = contextlib.ExitStack()
                with ph_ln:
                    self.mlp_ln(0, ph_ln, self.sp_s1, LNIN, [0, 1, 2, 3])
                    self.barrier()
                w1s = self.sbt(ph_a, "w1s", [128, 8 * 256], F32)
                w1b = self.sbt(ph_a, "w1b", [128, 8 * 256], BF16)
                ggf = self.sbt(ph_a, "ggf", [128, 4 * 512], F32)
                tmpa = self.sbt(ph_a, "tmpa", [128, 512], F32)
                ps_a = self.pst(ph_a, "ps_a", [128, 2048])
                ps_g = self.pst(ph_a, "ps_g", [128, 2048])
                self.mlp_a(0, ph_a, LNIN, gated, [0, 1, 2, 3],
                           (w1s, w1b, ggf, tmpa, ps_a, ps_g, self.b1_sb))
                p.barrier()
            self.mlp_b(0, ph, gated, "l1")
            p.barrier()

    def x2a(self):
        p = self
        ph = contextlib.ExitStack()
        with ph:
            hb = self.sbt(ph, "hb2a", [128, 8], F32)
            h0 = self.sbt(ph, "h02a", [128, 8], F32)
            h1 = self.sbt(ph, "h12a", [128, 8], F32)
            def g(gg):
                p.wait(gg, self.ev_mlpb_out)
                gg.dma_start(
                    out=hb[:],
                    in_=self.sp_s2[:, 2048:2049].rearrange(
                        "(a q) w -> q (a w)", q=128),
                    allow_slow_non_contiguous=True).then_inc(p.sem["SP"], 16)
                p.cnt["SP"] += 16
                gg.wait_ge(p.sem["SP"], p.cnt["SP"])
                gg.dma_start(out=self.cc2a_in[:, :], in_=hb[:]).then_inc(
                    p.sem["SP"], 16)
                p.cnt["SP"] += 16
                gg.wait_ge(p.sem["SP"], p.cnt["SP"])
                p.inc(gg.collective_compute(
                    "AllGather", ALU.bypass, replica_groups=PAIRS,
                    ins=[self.cc2a_in[:, :]], outs=[self.cc2a_out[:, :]]), "G")
                gg.wait_ge(p.sem["G"], p.cnt["G"])
                gg.dma_start(out=h0[:], in_=self.cc2a_out[0:128, :]).then_inc(
                    p.sem["SP"], 16)
                gg.dma_start(out=h1[:], in_=self.cc2a_out[128:256, :]).then_inc(
                    p.sem["SP"], 16)
                p.cnt["SP"] += 32
            p.G(g)
            ev = p.now("SP")
            def v(vv):
                p.wait(vv, ev)
                F = self.flags_sb
                vv.tensor_scalar(out=h0[:], in0=h0[:], scalar1=F[:, 2:3],
                                 scalar2=None, op0=ALU.mult)
                p.inc(vv.scalar_tensor_tensor(
                    out=h0[:], in0=h1[:], scalar=F[:, 3:4], in1=h0[:],
                    op0=ALU.mult, op1=ALU.add), "V")
            p.V(v)
            def g2(gg):
                p.wait(gg, p.now("V"))
                for rt in range(8):
                    p.inc(gg.dma_start(
                        out=self.sp_s2[rt * 128:(rt + 1) * 128, 0:1],
                        in_=h0[:, rt:rt + 1],
                        allow_slow_non_contiguous=True), "SP")
            p.G(g2)
            p.barrier()

    def p4(self):
        p = self
        ph = contextlib.ExitStack()
        with ph:
            QH = self.sbt(ph, "QH2", [128, 4 * 2048], BF16)
            KH = self.sbt(ph, "KH2", [128, 4 * 2560], BF16)
            VT = self.sbt(ph, "VT2", [128, 20 * 512], BF16)
            ph_q = contextlib.ExitStack()
            with ph_q:
                self.ffn_qkv(1, ph_q, QH, KH, VT, self.sp_s2,
                             lambda m: m * 512, 4, 1,
                             lambda m: 512 + m * 512)
                p.barrier()

            stream = self.sbt(ph, "stream2", [128, 8 * SEG], F32)
            AO = self.sbt(ph, "AO2", [128, 4 * 2048], BF16)
            ebuf = self.sbt(ph, "ebuf2", [128, 16 * 512], BF16)
            recipb = self.sbt(ph, "recipb2", [128, 512], F32)
            wo_bf = self.sbt(ph, "wo_bf2", [128, 4 * 1024], BF16)
            wos = self.sbt(ph, "wos2", [128, 1024], F32)
            for kt in range(4):
                ldev = [None]
                def y(yq, kt=kt, ldev=ldev):
                    p.wait(yq, p.now("V"))
                    ldev[0] = p.inc(yq.dma_start(
                        out=wos[:], in_=self.wo2[1, kt * 128:(kt + 1) * 128, :]), "LD")
                p.Y(y)
                def v(vv, kt=kt, ldev=ldev):
                    p.wait(vv, ldev[0])
                    p.inc(vv.tensor_copy(wo_bf[:, kt * 1024:(kt + 1) * 1024],
                                         wos[:]), "V")
                p.V(v)
            def yst(yq):
                for rt in range(8):
                    yq.dma_start(out=stream[:, rt * SEG:rt * SEG + SEG],
                                 in_=self.sp_s2[rt * 128:(rt + 1) * 128, :]
                                 ).then_inc(p.sem["LD"], 16)
                p.cnt["LD"] += 128
            p.Y(yst)
            def vst(vv):
                p.wait(vv, p.now("LD"))
                p.incl(vv, 'V')
            p.V(vst)

            def vz(vv):
                for t4 in range(4):
                    vv.memset(KH[:, t4 * 2560:t4 * 2560 + 512], 0.0)
                for ktg in range(4):
                    vv.memset(VT[:, ktg * 512:(ktg + 1) * 512], 0.0)
                p.incl(vv, 'V')
            p.V(vz)

            self.ffn_attn(1, ph, [1, 2, 3], QH, KH, VT, stream, wo_bf, AO,
                          ebuf, recipb, self.mf)

            def gx(gg):
                p.wait(gg, p.now("V"), *self.ev_qkv_done)
                for t4 in range(4):
                    gg.dma_start(
                        out=self.cc2b_in[:, t4 * 512:(t4 + 1) * 512],
                        in_=KH[:, t4 * 2560 + 2048:t4 * 2560 + 2560]).then_inc(
                        p.sem["SP"], 16)
                for kt in range(4):
                    gg.dma_start(
                        out=self.cc2b_in[:, 2048 + kt * 512:2048 + (kt + 1) * 512],
                        in_=VT[:, (16 + kt) * 512:(17 + kt) * 512]).then_inc(
                        p.sem["SP"], 16)
                gg.dma_start(
                    out=self.cc2b_in[:, 4096:4104],
                    in_=stream[:].rearrange("p (a b) -> p a b", b=SEG)
                    [:, :, SEG - 1:SEG],
                    allow_slow_non_contiguous=True).then_inc(p.sem["SP"], 16)
                p.cnt["SP"] += 144
                gg.wait_ge(p.sem["SP"], p.cnt["SP"])
                p.inc(gg.collective_compute(
                    "AllGather", ALU.bypass, replica_groups=PAIRS,
                    ins=[self.cc2b_in[:, :]], outs=[self.cc2b_out[:, :]]), "G")
            p.G(gx)
            ev_x2b = p.now("G")

            t0 = self.sbt(ph, "x2t0", [128, 4096], BF16)
            t1 = self.sbt(ph, "x2t1", [128, 4096], BF16)
            tm0 = self.sbt(ph, "x2m0", [128, 8], F32)
            tm1 = self.sbt(ph, "x2m1", [128, 8], F32)
            ev = [None]
            def gc(gg):
                p.wait(gg, ev_x2b)
                gg.dma_start(out=t0[:], in_=self.cc2b_out[0:128, 0:4096]
                             ).then_inc(p.sem["SP"], 16)
                gg.dma_start(out=t1[:], in_=self.cc2b_out[128:256, 0:4096]
                             ).then_inc(p.sem["SP"], 16)
                gg.dma_start(out=tm0[:], in_=self.cc2b_out[0:128, 4096:4104]
                             ).then_inc(p.sem["SP"], 16)
                gg.dma_start(out=tm1[:], in_=self.cc2b_out[128:256, 4096:4104]
                             ).then_inc(p.sem["SP"], 16)
                p.cnt["SP"] += 64
                ev[0] = p.now("SP")
            p.G(gc)
            def vc(vv):
                p.wait(vv, ev[0])
                F = self.flags_sb
                vv.tensor_scalar(out=t0[:], in0=t0[:], scalar1=F[:, 2:3],
                                 scalar2=None, op0=ALU.mult)
                vv.scalar_tensor_tensor(out=t0[:], in0=t1[:], scalar=F[:, 3:4],
                                        in1=t0[:], op0=ALU.mult, op1=ALU.add)
                vv.tensor_scalar(out=tm0[:], in0=tm0[:], scalar1=F[:, 2:3],
                                 scalar2=None, op0=ALU.mult)
                vv.scalar_tensor_tensor(out=tm0[:], in0=tm1[:], scalar=F[:, 3:4],
                                        in1=tm0[:], op0=ALU.mult, op1=ALU.add)
                for t4 in range(4):
                    vv.tensor_copy(KH[:, t4 * 2560:t4 * 2560 + 512],
                                   t0[:, t4 * 512:(t4 + 1) * 512])
                for kt in range(4):
                    vv.tensor_copy(VT[:, kt * 512:(kt + 1) * 512],
                                   t0[:, 2048 + kt * 512:2048 + (kt + 1) * 512])
                p.incl(vv, 'V')
            p.V(vc)
            def gmid(gg):
                p.wait(gg, p.now("V"))
                for rt in range(8):
                    p.inc(gg.dma_start(
                        out=self.sp_s1[rt * 128:(rt + 1) * 128, 0:1],
                        in_=tm0[:, rt:rt + 1],
                        allow_slow_non_contiguous=True), "SP")
            p.G(gmid)

            self.ffn_attn(1, ph, [0], QH, KH, VT, stream, wo_bf, AO,
                          ebuf, recipb, self.mf)

            def gsp(gg):
                p.wait(gg, p.now("V"))
                for rt in range(8):
                    p.inc(gg.dma_start(
                        out=self.sp_s1[rt * 128:(rt + 1) * 128, 1:1 + TOK],
                        in_=stream[:, rt * SEG + 1:rt * SEG + 1 + TOK]), "SP")
            p.G(gsp)
            self.dump("stream2", stream, 8, TOK, cols0=1, src_width=SEG)
            p.barrier()

        ph2 = contextlib.ExitStack()
        with ph2:
            self.b1_sb = self.sbt(ph2, "b1_sb2", [128, 44], F32)
            self.b2_sb = self.sbt(ph2, "b2_sb2", [128, 8], F32)
            self.lng_sb = self.sbt(ph2, "lng_sb2", [128, 8], F32)
            self.lnb2_sb = self.sbt(ph2, "lnb2_sb2", [128, 8], F32)
            def y(yq):
                yq.dma_start(out=self.b1_sb[:], in_=self.b1p[1, :, :]).then_inc(
                    p.sem["LD"], 16)
                yq.dma_start(out=self.b2_sb[:], in_=self.b2v[1, :, :]).then_inc(
                    p.sem["LD"], 16)
                yq.dma_start(out=self.lng_sb[:], in_=self.lng[1, :, :]).then_inc(
                    p.sem["LD"], 16)
                yq.dma_start(out=self.lnb2_sb[:], in_=self.lnb2[1, :, :]).then_inc(
                    p.sem["LD"], 16)
                p.cnt["LD"] += 64
            p.Y(y)
            gated = self.sbt(ph2, "gated2", [128, 22 * 2048], BF16)
            ph_a = contextlib.ExitStack()
            with ph_a:
                LNIN = self.sbt(ph_a, "LNIN2", [128, 8 * 2048], BF16)
                ph_ln = contextlib.ExitStack()
                with ph_ln:
                    self.mlp_ln(1, ph_ln, self.sp_s1, LNIN, [0, 1, 2, 3])
                    self.barrier()
                w1s = self.sbt(ph_a, "w1s2", [128, 8 * 256], F32)
                w1b = self.sbt(ph_a, "w1b2", [128, 8 * 256], BF16)
                ggf = self.sbt(ph_a, "ggf2", [128, 4 * 512], F32)
                tmpa = self.sbt(ph_a, "tmpa2", [128, 512], F32)
                ps_a = self.pst(ph_a, "ps_a2", [128, 2048])
                ps_g = self.pst(ph_a, "ps_g2", [128, 2048])
                self.mlp_a(1, ph_a, LNIN, gated, [0, 1, 2, 3],
                           (w1s, w1b, ggf, tmpa, ps_a, ps_g, self.b1_sb))
                p.barrier()
            self.mlp_b(1, ph2, gated, "l2")
            def gf(gg):
                p.wait(gg, self.ev_mlpb_out)
                gg.nop()
            p.G(gf)
            p.barrier()


def build(dbg=(), stop_after=None):
    nc = bass.Bass("TRN2", target_bir_lowering=False,
                   detect_race_conditions=False)
    es = contextlib.ExitStack()
    with es:
        k = Kern(nc, es, dbg)
        k.declare_io()
        k.persistent()
        k.p0()
        if stop_after == "P0":
            k.barrier()
            k.ph_h2.close()
            _finish_dummy(k)
            return nc, k.dbg_out
        if "f" in P1DIRS:
            k.p1("f", rev=False)
        if "r" in P1DIRS:
            k.p1("r", rev=True)
        if stop_after == "P1":
            k.barrier()
            k.ph_h2.close()
            for d in "fr":
                k.dump_dram(f"xT_{d}", k.sp_xT[d], (TOK, D_INNER))
                k.dump_dram(f"xD_{d}", k.sp_xD[d], (16 * 8 * 128, 256))
                k.dump_dram(f"B_{d}", k.sp_B[d], (64, TOK))
                k.dump_dram(f"C_{d}", k.sp_C[d], (64, TOK))
                k.dump_dram(f"ctl_{d}", k.sp_ctl[d], (TOK, 64))
                k.dump_dram(f"zs_{d}", k.sp_zs[d], (16 * 8 * 128, 256))
                k.dump_dram(f"sadd_{d}", k.sp_sadd[d], (64, NCH * D_INNER))
                k.dump_dram(f"cl_{d}", k.sp_cl[d], (32, TOK))
                k.dump_dram(f"gam_{d}", k.sp_gam[d], (32, TOK))
                k.dump_dram(f"dr_{d}", k.sp_dr[d], (NCH, D_INNER))
            _finish_dummy(k)
            return nc, k.dbg_out
        k.p2ab()
        if stop_after == "P2AB":
            k.barrier()
            _finish_dummy(k)
            return nc, k.dbg_out
        ph = contextlib.ExitStack()
        with ph:
            k.x1_consume(ph)
            k.barrier()
        if stop_after == "X1":
            _finish_dummy(k)
            return nc, k.dbg_out
        if "f" in os.environ.get("KB_P3_DIRS", "fr"):
            k.p3("f", rev=False)
        if "r" in os.environ.get("KB_P3_DIRS", "fr"):
            k.p3("r", rev=True)
        if stop_after == "P3":
            k.dump_dram("accm", k.acc_m, (D, TOK))
            _finish_dummy(k)
            return nc, k.dbg_out
        k.p2c()
        if stop_after == "P2C":
            k.dump_dram("s2", k.sp_s2, (D, SEG))
            _finish_dummy(k)
            return nc, k.dbg_out
        k.x2a()
        k.p4()
    return nc, k.dbg_out


def _finish_dummy(k):
    p = k
    nc = k.nc
    with nc.sbuf_tensor("zout", [128, TOK], F32) as z:
        with nc.semaphore("stf") as stf:
            def v(vv):
                p.inc(vv.memset(z[:], 0.0), "V")
            p.V(v)
            def g(gg):
                p.wait(gg, p.now("V"), p.now("SP"))
                n = 0
                for rt in range(8):
                    gg.dma_start(out=k.y_fm[rt * 128:(rt + 1) * 128, :],
                                 in_=z[:]).then_inc(stf, 16)
                    n += 16
                gg.wait_ge(stf, n)
            p.G(g)


def _rotary_tables():
    n = L
    half = np.arange(0, HEADDIM, 2, dtype=np.float32)
    inv_freq = 1.0 / 10000.0 ** (half / HEADDIM)
    t = np.arange(-512, n, dtype=np.float32)
    freqs = np.concatenate([t[:, None] * inv_freq[None, :]] * 2, 1)
    base = (half + 0.4 * HEADDIM) / (1.4 * HEADDIM)
    power = (t - n // 2) / float(WIN // 2)
    scale = np.concatenate([base[None, :] ** power[:, None]] * 2, 1)
    cos, sin = np.cos(freqs), np.sin(freqs)
    sgn = np.where(np.arange(HEADDIM) < 32, -1.0, 1.0).astype(np.float32)[None, :]
    Aq = (cos * scale * 8.0).T
    Bq = (sin * scale * 8.0 * sgn).T
    Ak = (cos / scale).T
    Bk = (sin / scale * sgn).T
    return [np.ascontiguousarray(np.tile(x, (2, 1)), dtype=np.float32)
            for x in (Aq, Bq, Ak, Bk)]


def _tile_pp(v, nt):
    return np.ascontiguousarray(v.reshape(nt, 128).T, dtype=np.float32)


def host_prep(inputs):
    x = inputs["L0_attn_x"]
    rotf = _rotary_tables()
    common = {}
    for d, pre in (("f", "L0_attn_f_"), ("r", "L0_attn_r_")):
        ip = inputs[pre + "in_proj"]
        common[f"wz_{d}"] = np.ascontiguousarray(ip[:D_INNER].T)
        common[f"wx_{d}"] = np.ascontiguousarray(ip[D_INNER:].T)
        common[f"wout_{d}"] = np.ascontiguousarray(inputs[pre + "out_proj"].T)
        cw = inputs[pre + "conv_w"]
        common[f"convw_{d}"] = np.ascontiguousarray(
            cw.reshape(17, 128, 4).transpose(1, 0, 2).reshape(128, 68))
        common[f"convb_{d}"] = _tile_pp(inputs[pre + "conv_b"], 17)
        common[f"mv_{d}"] = np.ascontiguousarray(np.stack(
            [inputs[pre + "dt_bias"], inputs[pre + "A_log"],
             inputs[pre + "D"]], 1), dtype=np.float32)
        common[f"dexp_{d}"] = _tile_pp(np.repeat(inputs[pre + "D"], 64), 16)
        common[f"rmsw_{d}"] = _tile_pp(inputs[pre + "rms_w"], 16)
    common["lnwb"] = np.ascontiguousarray(np.concatenate(
        [_tile_pp(inputs["L0_attn_ln_w"], 8), _tile_pp(inputs["L0_attn_ln_b"], 8)], 1))
    common["wqkv"] = np.ascontiguousarray(inputs["L0_ffn_wqkv"])
    common["wo2"] = np.ascontiguousarray(inputs["L0_ffn_wo"])
    w1 = inputs["L0_ffn_w1"]
    w1p = np.zeros((2, D, 2 * HIDP), np.float32)
    w1p[:, :, :HID] = w1[:, :, :HID]
    w1p[:, :, HIDP:HIDP + HID] = w1[:, :, HID:]
    common["w1p"] = w1p
    b1 = inputs["L0_ffn_b1"]
    b1p = np.zeros((2, 2 * HIDP), np.float32)
    b1p[:, :HID] = b1[:, :HID]
    b1p[:, HIDP:HIDP + HID] = b1[:, HID:]
    common["b1p"] = np.stack([_tile_pp(b1p[i], 44) for i in range(2)])
    w2 = inputs["L0_ffn_w2"]
    w2p = np.zeros((2, HIDP, D), np.float32)
    w2p[:, :HID] = w2
    common["w2p"] = w2p
    common["b2v"] = np.stack([_tile_pp(inputs["L0_ffn_b2"][i], 8) for i in range(2)])
    common["lng"] = np.stack([_tile_pp(inputs["L0_ffn_ln_g"][i], 8) for i in range(2)])
    common["lnb2"] = np.stack([_tile_pp(inputs["L0_ffn_ln_b"][i], 8) for i in range(2)])
    common["qsc"] = np.ascontiguousarray(
        np.tile(inputs["L0_ffn_q_scale"], (1, 2))[:, :, None])
    common["ksc"] = np.ascontiguousarray(
        np.tile(inputs["L0_ffn_k_scale"], (1, 2))[:, :, None])

    in_maps = []
    for c in range(8):
        b, h = c // 2, c % 2
        start = h * TOK
        lo, hi = start - OWN0, start + TOK + 128
        xs = np.zeros((XW, D), np.float32)
        lo_c, hi_c = max(lo, 0), min(hi, L)
        xs[lo_c - lo:hi_c - lo] = x[b, lo_c:hi_c]
        m = dict(common)
        m["x_fm"] = np.ascontiguousarray(xs.T)
        c0 = start - 512 + 512
        m["rot"] = np.ascontiguousarray(
            np.stack([rt[:, c0:c0 + RW] for rt in rotf]))
        fl = np.zeros((128, 8), np.float32)
        m_front = 1.0 if h == 1 else 0.0
        m_back = 1.0 if h == 0 else 0.0
        wsel0, wsel1 = float(h), 1.0 - float(h)
        fl[:, 0] = m_front
        fl[:, 1] = m_back
        fl[:, 2] = wsel0 * m_front
        fl[:, 3] = wsel1 * m_front
        fl[:, 4] = wsel0 * m_back
        fl[:, 5] = wsel1 * m_back
        m["flags"] = fl
        in_maps.append(m)
    return in_maps


_NC_CACHE = {}


def kernel(**inputs):
    in_maps = host_prep(inputs)
    nc, _ = build()
    res = run_bass_kernel_spmd(nc, in_maps, core_ids=list(range(8)))
    out = np.empty((B, L, D), np.float32)
    for c in range(8):
        b, h = c // 2, c % 2
        out[b, h * TOK:(h + 1) * TOK] = res.results[c]["y_fm"].T
    return out
